# version 08
# speedup vs baseline: 1.0645x; 1.0001x over previous
import math
import contextlib
import numpy as np
import concourse.bass as bass
import concourse.mybir as mybir
from concourse.bass_utils import run_bass_kernel_spmd

F32, BF16 = mybir.dt.float32, mybir.dt.bfloat16
AF = mybir.ActivationFunctionType
ALU = mybir.AluOpType
AX = mybir.AxisListType

NCORES = 8
S = 2048
NSEQ = 2
T = S * NSEQ
E = 1024
H = 16
INNER = 4096
NT = T // 128
RC = 12582912.0
NQT = 8
VPAD = 1056
DEBUG = False


class Ctr:
    def __init__(self, handle):
        self.h = handle
        self.n = 0

    def bump(self, k=1):
        self.n += k
        return self.n


def build_nc():
    nc = bass.Bass("TRN2", target_bir_lowering=False)

    xin = nc.dram_tensor("xin", [T, E], F32, kind="ExternalInput").ap()
    win = {}
    for nm in ["wqc", "wkc", "wvc", "woc", "wql", "wkl", "wvl", "wol"]:
        win[nm] = nc.dram_tensor(nm, [128, 8 * E], F32, kind="ExternalInput").ap()
    wg = nc.dram_tensor("wg", [128, 8 * INNER], F32, kind="ExternalInput").ap()
    wu = nc.dram_tensor("wu", [128, 8 * INNER], F32, kind="ExternalInput").ap()
    wdn = nc.dram_tensor("wdn", [128, 32 * E], F32, kind="ExternalInput").ap()
    bqkv_c = nc.dram_tensor("bqkv_c", [3 * E], F32, kind="ExternalInput").ap()
    bqkv_l = nc.dram_tensor("bqkv_l", [3 * E], F32, kind="ExternalInput").ap()
    boc = nc.dram_tensor("boc", [E], F32, kind="ExternalInput").ap()
    bol = nc.dram_tensor("bol", [E], F32, kind="ExternalInput").ap()
    y = nc.dram_tensor("y", [T, E], F32, kind="ExternalOutput").ap()

    xn_sc = nc.dram_tensor("xn_sc", [T, E], BF16, kind="Internal").ap()
    ai_sc = nc.dram_tensor("ai_sc", [T, E], BF16, kind="Internal").ap()
    hq_sc = nc.dram_tensor("hq_sc", [T, INNER], BF16, kind="Internal").ap()
    part_sc = nc.dram_tensor("part_sc", [T, E], F32, kind="Internal").ap()
    wgq_sc = nc.dram_tensor("wgq_sc", [128, 8 * INNER], BF16, kind="Internal").ap()
    wuq_sc = nc.dram_tensor("wuq_sc", [128, 8 * INNER], BF16, kind="Internal").ap()
    wdq_sc = nc.dram_tensor("wdq_sc", [128, 32 * E], BF16, kind="Internal").ap()
    wbf_sc = nc.dram_tensor("wbf_sc", [128, 64 * E], BF16, kind="Internal").ap()
    bbf_sc = nc.dram_tensor("bbf_sc", [6144], BF16, kind="Internal").ap()
    csum_sc = nc.dram_tensor("csum_sc", [256], F32, kind="Internal").ap()
    one_sc = nc.dram_tensor("one_sc", [8], F32, kind="Internal").ap()
    vloc_sc = nc.dram_tensor("vloc_sc", [2080, 2176], BF16, kind="Internal").ap()
    vcls_sc = nc.dram_tensor("vcls_sc", [2080, 2048], BF16, kind="Internal").ap()
    WOFF = {"wkc": 0, "wvc": 8192, "wqc": 16384, "woc": 24576,
            "wql": 32768, "wkl": 40960, "wvl": 49152, "wol": 57344}
    dbg = {}
    if DEBUG:
        dbg["d_xn"] = nc.dram_tensor("d_xn", [T, E], BF16, kind="ExternalOutput").ap()
        dbg["d_ai"] = nc.dram_tensor("d_ai", [T, E], BF16, kind="ExternalOutput").ap()
        dbg["d_part"] = nc.dram_tensor("d_part", [T, E], F32, kind="ExternalOutput").ap()
        dbg["d_vloc"] = nc.dram_tensor("d_vloc", [1040, 2176], BF16, kind="ExternalOutput").ap()
        dbg["d_vcls"] = nc.dram_tensor("d_vcls", [1040, 2048], BF16, kind="ExternalOutput").ap()
        dbg["d_hq"] = nc.dram_tensor("d_hq", [T, INNER], BF16, kind="ExternalOutput").ap()
        dbg["d_sb"] = nc.dram_tensor("d_sb", [128, 4096], BF16, kind="ExternalOutput").ap()
        dbg["d_pcls"] = nc.dram_tensor("d_pcls", [16, S], BF16, kind="ExternalOutput").ap()
        dbg["d_qf"] = nc.dram_tensor("d_qf", [128, 8192], BF16, kind="ExternalOutput").ap()
        dbg["d_msk"] = nc.dram_tensor("d_msk", [128, 768], BF16, kind="ExternalOutput").ap()
        dbg["d_ctm"] = nc.dram_tensor("d_ctm", [128, 1024], BF16, kind="ExternalOutput").ap()
        dbg["d_cfm"] = nc.dram_tensor("d_cfm", [128, 1024], BF16, kind="ExternalOutput").ap()
        dbg["d_vs"] = nc.dram_tensor("d_vs", [128, 2112], BF16, kind="ExternalOutput").ap()
        dbg["d_pts"] = nc.dram_tensor("d_pts", [128, 512], BF16, kind="ExternalOutput").ap()
        dbg["d_clsrows"] = nc.dram_tensor("d_clsrows", [16, 1040], F32, kind="ExternalOutput").ap()
        dbg["d_lb"] = nc.dram_tensor("d_lb", [16, 16], F32, kind="ExternalOutput").ap()
        dbg["d_outcls"] = nc.dram_tensor("d_outcls", [1, E], F32, kind="ExternalOutput").ap()
        dbg["d_cfm0"] = nc.dram_tensor("d_cfm0", [128, 8], BF16, kind="ExternalOutput").ap()
        dbg["d_scr0"] = nc.dram_tensor("d_scr0", [16, 1040], BF16, kind="ExternalOutput").ap()
        dbg["d_qhat"] = nc.dram_tensor("d_qhat", [128, 128], BF16, kind="ExternalOutput").ap()
        dbg["d_w3"] = nc.dram_tensor("d_w3", [128, 8192], BF16, kind="ExternalOutput").ap()
        dbg["d_st0"] = nc.dram_tensor("d_st0", [128, 11], F32, kind="ExternalOutput").ap()
        dbg["d_st1"] = nc.dram_tensor("d_st1", [128, 11], F32, kind="ExternalOutput").ap()
        dbg["d_wbfol"] = nc.dram_tensor("d_wbfol", [128, 8192], BF16, kind="ExternalOutput").ap()
        dbg["d_t0pts"] = nc.dram_tensor("d_t0pts", [128, 1024], BF16, kind="ExternalOutput").ap()
        dbg["d_t0ctm"] = nc.dram_tensor("d_t0ctm", [128, 1024], BF16, kind="ExternalOutput").ap()
        dbg["d_t0vs"] = nc.dram_tensor("d_t0vs", [128, 2112], BF16, kind="ExternalOutput").ap()

    es = contextlib.ExitStack()
    with es:
        ec = es.enter_context
        xbuf = [ec(nc.sbuf_tensor(f"xbuf{i}", [128, E], F32)) for i in range(2)]
        fstream = ec(nc.sbuf_tensor("fstream", [128, 4096], F32))
        evac1 = ec(nc.sbuf_tensor("evac1", [128, E], F32))
        gbuf = ec(nc.sbuf_tensor("gbuf", [128, 1024], F32))
        stats = ec(nc.sbuf_tensor("stats", [128, 44], F32))
        pstats = ec(nc.sbuf_tensor("pstats", [128, 24], F32))
        rs1g_all = ec(nc.sbuf_tensor("rs1g_all", [128, NT], F32))
        cc_all = ec(nc.sbuf_tensor("cc_all", [128, NT], F32))
        rs1_all = ec(nc.sbuf_tensor("rs1_all", [128, NT], F32))
        sdown_all = ec(nc.sbuf_tensor("sdown_all", [128, NT], F32))
        wsc = ec(nc.sbuf_tensor("wsc", [128, 12], F32))
        consts = ec(nc.sbuf_tensor("consts", [128, 8], F32))
        bo_bc = ec(nc.sbuf_tensor("bo_bc", [128, E], F32))
        ident = ec(nc.sbuf_tensor("ident", [128, 128], BF16))
        bfm = ec(nc.sbuf_tensor("bfm", [128, 48], BF16))
        brow = ec(nc.sbuf_tensor("brow", [2, E], F32))
        masks = ec(nc.sbuf_tensor("masks", [128, 768], BF16))
        zrow = ec(nc.sbuf_tensor("zrow", [128, 112], BF16))
        padt = ec(nc.sbuf_tensor("padt", [128, 256], F32))
        p0stage = ec(nc.sbuf_tensor("p0stage", [128, 4096], BF16))
        cshift = ec(nc.sbuf_tensor("cshift", [128, 18], F32))
        rowred = ec(nc.sbuf_tensor("rowred", [2, 256], F32))
        lbuf = ec(nc.sbuf_tensor("lbuf", [128, 16], F32))
        clsrows = ec(nc.sbuf_tensor("clsrows", [16, 1040], F32))
        qrow = ec(nc.sbuf_tensor("qrow", [1, E], BF16))
        outcls = ec(nc.sbuf_tensor("outcls", [1, E], F32))
        WARENA = 73728
        warena = ec(nc.sbuf_tensor("warena", [128, WARENA], BF16))

        SL = Ctr(ec(nc.semaphore("s_ld")))
        ST = Ctr(ec(nc.semaphore("s_st")))
        SP = Ctr(ec(nc.semaphore("s_pe")))
        SV = Ctr(ec(nc.semaphore("s_dve")))
        SA = Ctr(ec(nc.semaphore("s_act")))
        SO = Ctr(ec(nc.semaphore("s_pool")))
        SX = Ctr(ec(nc.semaphore("s_xld")))
        SH = Ctr(ec(nc.semaphore("s_hst")))

        pw0 = ec(nc.psum_tensor("pw0", [128, 1024], F32))
        pw1 = ec(nc.psum_tensor("pw1", [128, 1024], F32))
        px = ec(nc.psum_tensor("px", [128, 1024], F32))
        py = ec(nc.psum_tensor("py", [128, 512], F32))
        ptp = ec(nc.psum_tensor("ptp", [128, 256], BF16))
        psums = [pw0, pw1, px, py]

        Q = {"sync": [], "vector": [], "scalar": [], "tensor": [], "gpsimd": []}

        def on(eng, fn):
            Q[eng].append(fn)

        def wAP(col, pbase=0, dims=None):
            return bass.AP(warena, pbase * WARENA + col, dims)

        D = {}

        def init_phase():
            base0 = SO.n

            def gp(g, base0=base0):
                k = [base0]

                def step(instr):
                    instr.then_inc(SO.h, 1)
                    k[0] += 1
                    g.wait_ge(SO.h, k[0])
                step(g.memset(consts[:, 0:1], 1e-5))
                step(g.memset(consts[:, 1:2], math.log(127.0)))
                step(g.memset(consts[:, 4:8], 1.0))
                step(g.memset(ident[:, :], 0.0))
                step(g.affine_select(ident[:, :], ident[:, :], compare_op=ALU.not_equal,
                                     fill=1.0, base=0, pattern=[[-1, 128]], channel_multiplier=1))
                m = masks
                step(g.memset(m[:, 0:768], 1.0))
                for bc in (0, 256, 512):
                    step(g.affine_select(m[:, bc:bc + 128], m[:, bc:bc + 128], compare_op=ALU.is_ge,
                                         fill=0.0, base=0, pattern=[[-1, 128]], channel_multiplier=1))
                    step(g.affine_select(m[:, bc:bc + 128], m[:, bc:bc + 128], compare_op=ALU.is_ge,
                                         fill=0.0, base=32, pattern=[[1, 128]], channel_multiplier=-1))
                    bb = bc + 128
                    step(g.affine_select(m[:, bb:bb + 128], m[:, bb:bb + 128], compare_op=ALU.is_ge,
                                         fill=0.0, base=-96, pattern=[[1, 128]], channel_multiplier=-1))
                    step(g.affine_select(m[:, bb:bb + 128], m[:, bb:bb + 128], compare_op=ALU.is_ge,
                                         fill=0.0, base=128, pattern=[[-1, 128]], channel_multiplier=1))
                step(g.affine_select(m[:, 256:384], m[:, 256:384], compare_op=ALU.is_ge,
                                     fill=0.0, base=-17, pattern=[[0, 128]], channel_multiplier=1))
                step(g.affine_select(m[:, 640:768], m[:, 640:768], compare_op=ALU.is_ge,
                                     fill=0.0, base=15, pattern=[[0, 128]], channel_multiplier=-1))
                step(g.memset(zrow[:, :], 0.0))
                step(g.memset(padt[:, :], 0.0))
            SO.bump(22)
            D["init_pool"] = SO.n
            on("gpsimd", gp)
            ld = SL.n + 48

            def sy(s):
                s.dma_start(bo_bc[:, :], bass.AP(bol.tensor, 0, [[0, 128], [1, E]])).then_inc(SL.h, 16)
                s.dma_start(brow[0:1, :], bass.AP(boc.tensor, 0, [[E, 1], [1, E]])).then_inc(SL.h, 16)
                s.dma_start(brow[1:2, :], bass.AP(bol.tensor, 0, [[E, 1], [1, E]])).then_inc(SL.h, 16)
            SL.bump(48)
            D["init_ld"] = ld
            on("sync", sy)

        init_phase()

        def prep_phase():
            ld0 = SL.n + 32

            def sy1(s):
                s.dma_start(fstream[0:1, 0:3072], bass.AP(bqkv_c.tensor, 0, [[3072, 1], [1, 3072]])).then_inc(SL.h, 16)
                s.dma_start(fstream[1:2, 0:3072], bass.AP(bqkv_l.tensor, 0, [[3072, 1], [1, 3072]])).then_inc(SL.h, 16)
            SL.bump(32)
            on("sync", sy1)
            v_b = SV.bump()
            on("vector", lambda v, ld0=ld0: (
                v.wait_ge(SL.h, ld0),
                v.tensor_copy(warena[0:2, 0:3072], fstream[0:2, 0:3072]).then_inc(SV.h, 1)))
            st_b = ST.n + 32

            def sc1(sc, v_b=v_b):
                sc.wait_ge(SV.h, v_b)
                sc.dma_start(bass.AP(bbf_sc.tensor, 0, [[3072, 1], [1, 3072]]), warena[0:1, 0:3072]).then_inc(ST.h, 16)
                sc.dma_start(bass.AP(bbf_sc.tensor, 3072, [[3072, 1], [1, 3072]]), warena[1:2, 0:3072]).then_inc(ST.h, 16)
            ST.bump(32)
            on("scalar", sc1)

            def sy2(s, st_b=st_b):
                s.wait_ge(ST.h, st_b)
                s.dma_start(bfm[:, 0:48], bass.AP(bbf_sc.tensor, 0, [[128, 48], [1, 128]]),
                            transpose=True).then_inc(SL.h, 16)
            SL.bump(16)
            on("sync", sy2)
            D["bfm_ld"] = SL.n



            pa_state = {0: v_b, 1: v_b}
            pcnt = 0
            for wi, wsrc in enumerate([wg, wu, wdn]):
                on("vector", lambda v, wi=wi: v.memset(stats[:, 33 + wi:34 + wi], 0.0))
                for tix in range(16):
                    bufp = pcnt % 2
                    pcnt += 1
                    f0 = 2048 * bufp
                    prev_v = pa_state[bufp]
                    ldt = SL.n + 16

                    def syl(s, wsrc=wsrc, tix=tix, prev_v=prev_v, f0=f0):
                        s.wait_ge(SV.h, prev_v)
                        s.dma_start(fstream[:, f0:f0 + 2048],
                                    wsrc[:, 2048 * tix:2048 * (tix + 1)]).then_inc(SL.h, 16)
                    SL.bump(16)
                    on("sync", syl)
                    red_t = SV.bump()
                    SV.bump()

                    def vel(v, ldt=ldt, wi=wi, red_t=red_t, f0=f0):
                        v.wait_ge(SL.h, ldt)
                        v.reduce_sum(stats[:, 36:37], fstream[:, f0:f0 + 2048], axis=AX.X,
                                     apply_absolute_value=True).then_inc(SV.h, 1)
                        v.wait_ge(SV.h, red_t)
                        v.tensor_tensor(stats[:, 33 + wi:34 + wi], stats[:, 33 + wi:34 + wi],
                                        stats[:, 36:37], op=ALU.add).then_inc(SV.h, 1)
                    on("vector", vel)
                    pa_state[bufp] = SV.n
            v_sums = SV.n

            for wi in range(3):
                stA = ST.n + 16

                def sc_a(sc, wi=wi, v_sums=v_sums):
                    sc.wait_ge(SV.h, v_sums)
                    sc.dma_start(csum_sc[0:128], stats[:, 33 + wi:34 + wi]).then_inc(ST.h, 16)
                ST.bump(16)
                on("scalar", sc_a)
                stB = ST.n + 16

                def sc_b(sc, stA=stA):
                    sc.wait_ge(ST.h, stA)
                    sc.dma_start(rowred[0:1, 0:128],
                                 bass.AP(csum_sc.tensor, 0, [[128, 1], [1, 128]])).then_inc(ST.h, 16)
                ST.bump(16)
                on("scalar", sc_b)
                r_t = SV.bump()
                fin_t = SV.bump()

                def ve_r(v, stB=stB, wi=wi, r_t=r_t):
                    v.wait_ge(ST.h, stB)
                    v.reduce_sum(rowred[0:1, 128:129], rowred[0:1, 0:128], axis=AX.X).then_inc(SV.h, 1)
                    v.wait_ge(SV.h, r_t)
                    v.tensor_copy(padt[:, 128:192], padt[:, 192:256])
                    v.tensor_scalar(rowred[0:1, 140 + wi:141 + wi], rowred[0:1, 128:129],
                                    scalar1=1.0 / (4096.0 * 1024.0), scalar2=1e-5,
                                    op0=ALU.mult, op1=ALU.max)
                    v.tensor_copy(padt[:, 128:192], padt[:, 192:256]).then_inc(SV.h, 1)
                on("vector", ve_r)
                nxt = SV.n
                on("scalar", lambda sc, nxt=nxt: sc.wait_ge(SV.h, nxt))
            v_scales = SV.n
            stC = ST.n + 16

            def sc_c(sc, v_scales=v_scales):
                sc.wait_ge(SV.h, v_scales)
                sc.dma_start(one_sc[0:3], rowred[0:1, 140:143]).then_inc(ST.h, 16)
            ST.bump(16)
            on("scalar", sc_c)
            stD = ST.n + 48

            def sc_d(sc, stC=stC):
                sc.wait_ge(ST.h, stC)
                for wi in range(3):
                    sc.dma_start(wsc[:, 2 * wi + 1:2 * wi + 2],
                                 bass.AP(one_sc.tensor, wi, [[0, 128], [1, 1]])).then_inc(ST.h, 16)
            ST.bump(48)
            on("scalar", sc_d)
            sw_t = SA.bump()

            def ac_sw(a, stD=stD):
                a.wait_ge(ST.h, stD)
                for wi in range(3):
                    a.activation(wsc[:, 8 + wi:9 + wi], wsc[:, 2 * wi + 1:2 * wi + 2], AF.Ln)
                for wi in range(3):
                    a.activation(wsc[:, 2 * wi:2 * wi + 1], wsc[:, 8 + wi:9 + wi], AF.Exp, scale=-1.0)
                a.activation(padt[:, 0:64], padt[:, 64:128], AF.Identity).then_inc(SA.h, 1)
            on("scalar", ac_sw)
            D["sw_t"] = sw_t

            v_pa = SV.n
            pb_state = {0: (0, v_pa), 1: (0, v_pa)}
            pcnt = 0
            for wi, (wsrc, dst) in enumerate([(wg, None), (wu, None), (wdn, wdq_sc)]):
                for tix in range(16):
                    bufp = pcnt % 2
                    pcnt += 1
                    f0 = 2048 * bufp
                    w0 = (32768 * wi + 2048 * tix) if dst is None else (65536 + 2048 * bufp)
                    prev_st, prev_q = pb_state[bufp]
                    st_full = ST.n
                    ldt = SL.n + 16

                    def syl(s, wsrc=wsrc, tix=tix, prev_q=prev_q, f0=f0):
                        s.wait_ge(SV.h, prev_q)
                        s.dma_start(fstream[:, f0:f0 + 2048],
                                    wsrc[:, 2048 * tix:2048 * (tix + 1)]).then_inc(SL.h, 16)
                    SL.bump(16)
                    on("sync", syl)
                    q_t = SV.bump()

                    def vel(v, ldt=ldt, wi=wi, sw_t=sw_t, f0=f0, w0=w0, stw=st_full):
                        v.wait_ge(SL.h, ldt)
                        v.wait_ge(SA.h, sw_t)
                        v.wait_ge(ST.h, stw)
                        v.tensor_scalar(fstream[:, f0:f0 + 2048], fstream[:, f0:f0 + 2048],
                                        scalar1=wsc[:, 2 * wi:2 * wi + 1], scalar2=RC,
                                        op0=ALU.mult, op1=ALU.add)
                        v.tensor_scalar(fstream[:, f0:f0 + 2048], fstream[:, f0:f0 + 2048],
                                        scalar1=RC, scalar2=None, op0=ALU.subtract)
                        v.tensor_scalar(warena[:, w0:w0 + 2048], fstream[:, f0:f0 + 2048],
                                        scalar1=1.0, scalar2=-1.0, op0=ALU.min, op1=ALU.max).then_inc(SV.h, 1)
                    on("vector", vel)

                    if dst is not None:
                        def scl(sc, dst=dst, tix=tix, q_t=q_t, w0=w0):
                            sc.wait_ge(SV.h, q_t)
                            sc.dma_start(dst[:, 2048 * tix:2048 * (tix + 1)],
                                         warena[:, w0:w0 + 2048]).then_inc(ST.h, 16)
                        ST.bump(16)
                        on("scalar", scl)
                        pb_state[bufp] = (ST.n, q_t)
                    else:
                        pb_state[bufp] = (prev_st, q_t)
                    if wi == 1 and tix == 15:
                        D["wq_v"] = q_t
            D["wq_st"] = ST.n
            pb_v = SV.n

            D["pb_v"] = SV.n

        prep_phase()

        def p0_f1_phase():
            WG0, WU0 = 0, 32768
            AI = [65536, 69632]
            SUPER = 4
            p0_mark = {}
            p0_stnow = {}
            p0v = {}
            p0_aq = {}
            ai_free = {0: 0, 1: 0}
            ev_par = {0: (0, 0), 1: (0, 0)}
            st = {"hq_free": 0}
            lded = set()

            def emit_p0_load(j):
                if j in lded or j >= NT:
                    return
                lded.add(j)
                buf = j % 2

                def gyl(g, j=j, buf=buf):
                    pf = p0_mark.get(j - 2, (0, 0))
                    g.wait_ge(SV.h, pf[0])
                    g.wait_ge(SA.h, pf[1])
                    g.dma_start(xbuf[buf][:, :], xin[128 * j:128 * (j + 1), :]).then_inc(SX.h, 16)
                SX.bump(16)
                on("gpsimd", gyl)

            def emit_p0_tile(j):
                buf = j % 2
                sb0 = 11 * (j % 2)
                emit_p0_load(j)
                sum_t = SV.bump()
                ld_full = 16 * (j + 1)
                prev_aq = p0_aq.get(j - 1, 0)
                on("vector", lambda v, ld_full=ld_full, buf=buf, sb0=sb0, prev_aq=prev_aq: (
                    v.wait_ge(SX.h, ld_full),
                    v.wait_ge(SA.h, prev_aq),
                    v.reduce_sum(pstats[:, sb0 + 0:sb0 + 1], xbuf[buf][:, :], axis=AX.X).then_inc(SV.h, 1)))
                m_t = SV.bump()

                def ve_b(v, sum_t=sum_t, sb0=sb0):
                    v.wait_ge(SV.h, sum_t)
                    v.tensor_scalar(pstats[:, sb0 + 2:sb0 + 3], pstats[:, sb0 + 0:sb0 + 1],
                                    scalar1=1.0 / 1024, scalar2=None, op0=ALU.mult)
                    v.tensor_copy(padt[:, 128:192], padt[:, 192:256])
                    v.tensor_scalar(pstats[:, sb0 + 3:sb0 + 4], pstats[:, sb0 + 2:sb0 + 3],
                                    scalar1=pstats[:, sb0 + 2:sb0 + 3], scalar2=-1.0,
                                    op0=ALU.mult, op1=ALU.mult)
                    v.tensor_copy(padt[:, 128:192], padt[:, 192:256]).then_inc(SV.h, 1)
                on("vector", ve_b)
                a_st = SA.bump()

                def ac_a(a, j=j, buf=buf, m_t=m_t, sb0=sb0, prev_done=D.get("p0_done_prev", 0)):
                    a.wait_ge(SV.h, max(m_t, prev_done))
                    a.activation(evac1[:, :], xbuf[buf][:, :], AF.Square, accum_out=pstats[:, sb0 + 1:sb0 + 2])
                    a.activation(padt[:, 0:32], padt[:, 64:96], AF.Identity)
                    a.activation(pstats[:, sb0 + 4:sb0 + 5], pstats[:, sb0 + 1:sb0 + 2], AF.Identity,
                                 bias=pstats[:, sb0 + 3:sb0 + 4], scale=1.0 / 1024)
                    a.activation(padt[:, 0:32], padt[:, 64:96], AF.Identity)
                    a.activation(pstats[:, sb0 + 5:sb0 + 6], pstats[:, sb0 + 4:sb0 + 5], AF.Ln, bias=consts[:, 0:1])
                    a.activation(padt[:, 0:32], padt[:, 64:96], AF.Identity)
                    a.activation(pstats[:, sb0 + 6:sb0 + 7], pstats[:, sb0 + 5:sb0 + 6], AF.Exp, scale=-0.5)
                    a.activation(padt[:, 0:64], padt[:, 64:128], AF.Identity).then_inc(SA.h, 1)
                on("scalar", ac_a)
                xn_t = SV.n + 2
                SV.bump(2)

                def ve_c(v, buf=buf, a_st=a_st, sb0=sb0):
                    v.wait_ge(SA.h, a_st)
                    v.tensor_scalar(evac1[:, :], xbuf[buf][:, :], scalar1=pstats[:, sb0 + 2:sb0 + 3],
                                    scalar2=pstats[:, sb0 + 6:sb0 + 7], op0=ALU.subtract, op1=ALU.mult)
                    v.reduce_max(pstats[:, sb0 + 7:sb0 + 8], evac1[:, :], axis=AX.X,
                                 apply_absolute_value=True).then_inc(SV.h, 2)
                on("vector", ve_c)
                p0_mark[j] = (xn_t, a_st)
                cl_t = SV.bump()

                def ve_d(v, xn_t=xn_t, sb0=sb0):
                    v.wait_ge(SV.h, xn_t)
                    v.tensor_scalar(pstats[:, sb0 + 8:sb0 + 9], pstats[:, sb0 + 7:sb0 + 8], scalar1=1e-5,
                                    scalar2=None, op0=ALU.max)
                    v.tensor_copy(padt[:, 128:192], padt[:, 192:256]).then_inc(SV.h, 1)
                on("vector", ve_d)
                a_q = SA.bump()

                def ac_c(a, cl_t=cl_t, j=j, sb0=sb0):
                    a.wait_ge(SV.h, cl_t)
                    a.activation(pstats[:, sb0 + 9:sb0 + 10], pstats[:, sb0 + 8:sb0 + 9], AF.Ln)
                    a.activation(padt[:, 0:32], padt[:, 64:96], AF.Identity)
                    a.activation(pstats[:, sb0 + 10:sb0 + 11], pstats[:, sb0 + 9:sb0 + 10], AF.Exp,
                                 scale=-1.0, bias=consts[:, 1:2])
                    a.activation(rs1_all[:, j:j + 1], pstats[:, sb0 + 8:sb0 + 9], AF.Identity, scale=1.0 / 127)
                    a.activation(padt[:, 0:64], padt[:, 64:128], AF.Identity).then_inc(SA.h, 1)
                on("scalar", ac_c)
                p0_aq[j] = a_q
                if DEBUG and j in (0, 1):
                    def sc_dst(sc, j=j, sb0=sb0, a_q_=SA.n):
                        sc.wait_ge(SA.h, a_q_)
                        sc.dma_start(dbg["d_st%d" % j], pstats[:, sb0:sb0 + 11]).then_inc(ST.h, 16)
                    ST.bump(16)
                    on("scalar", sc_dst)
                done_t = SV.n + 2
                SV.bump(2)

                ps0 = 2048 * (j % 2)

                def ve_e(v, a_q=a_q, j=j, sb0=sb0, ps0=ps0, prev_st=p0_stnow.get(j - 2, 0)):
                    v.wait_ge(SA.h, a_q)
                    v.wait_ge(ST.h, prev_st)
                    v.tensor_copy(p0stage[:, ps0:ps0 + 1024], evac1[:, :])
                    v.tensor_scalar(evac1[:, :], evac1[:, :], scalar1=pstats[:, sb0 + 10:sb0 + 11],
                                    scalar2=RC, op0=ALU.mult, op1=ALU.add)
                    v.tensor_scalar(p0stage[:, ps0 + 1024:ps0 + 2048], evac1[:, :], scalar1=RC,
                                    scalar2=None, op0=ALU.subtract).then_inc(SV.h, 2)
                on("vector", ve_e)
                D["p0_done_prev"] = done_t

                def scl(sc, j=j, done_t=done_t, ps0=ps0):
                    sc.wait_ge(SV.h, done_t)
                    sc.dma_start(xn_sc[128 * j:128 * (j + 1), :], p0stage[:, ps0:ps0 + 1024]).then_inc(ST.h, 16)
                    sc.dma_start(ai_sc[128 * j:128 * (j + 1), :], p0stage[:, ps0 + 1024:ps0 + 2048]).then_inc(ST.h, 16)
                ST.bump(32)
                st_now = ST.n
                p0_stnow[j] = st_now
                on("scalar", scl)
                if j % 4 == 3:
                    mark = SV.bump()

                    def ve_mark(v, st_now=st_now):
                        v.wait_ge(ST.h, st_now)
                        v.tensor_copy(padt[:, 128:192], padt[:, 192:256]).then_inc(SV.h, 1)
                    on("vector", ve_mark)
                    p0v[j] = mark

            def emit_f1_super(sb):
                abuf = AI[sb % 2]
                rsg_sb = SV.bump()

                def ve_rs4(v, sb=sb, aq=p0_aq[4 * sb + 3]):
                    v.wait_ge(SA.h, aq)
                    v.tensor_scalar(rs1g_all[:, 4 * sb:4 * sb + 4], rs1_all[:, 4 * sb:4 * sb + 4],
                                    scalar1=wsc[:, 1:2], scalar2=None, op0=ALU.mult)
                    v.tensor_scalar(cc_all[:, 4 * sb:4 * sb + 4], rs1_all[:, 4 * sb:4 * sb + 4],
                                    scalar1=wsc[:, 3:4], scalar2=None, op0=ALU.mult)
                    v.tensor_copy(padt[:, 128:192], padt[:, 192:256]).then_inc(SV.h, 1)
                on("vector", ve_rs4)

                def syl(s, sb=sb, abuf=abuf, pf=ai_free[sb % 2], pv=p0v[4 * sb + 3]):
                    s.wait_ge(SV.h, pv)
                    s.wait_ge(SP.h, pf)
                    for c in range(8):
                        s.dma_start(warena[:, abuf + 512 * c:abuf + 512 * (c + 1)],
                                    bass.AP(ai_sc.tensor, (512 * sb) * E + 128 * c,
                                            [[E, 512], [1, 128]]), transpose=True).then_inc(SL.h, 16)
                SL.bump(128)
                ai_t = SL.n
                on("sync", syl)
                for jj in range(SUPER):
                    j = sb * SUPER + jj
                    for grp in range(8):
                        par = grp % 2
                        evd = ev_par[par]
                        g_t = SP.bump()
                        u_t = SP.bump()

                        def te(t, jj=jj, grp=grp, par=par, abuf=abuf, evd=evd, ai_t=ai_t, wq_v=D["wq_v"]):
                            t.wait_ge(SL.h, ai_t)
                            t.wait_ge(SV.h, max(wq_v, evd[1]))
                            t.wait_ge(SA.h, evd[0])
                            n0 = 512 * grp
                            for dstp, w0 in ((pw0, WG0), (pw1, WU0)):
                                last = None
                                for c in range(8):
                                    last = t.matmul(dstp[:, 512 * par:512 * (par + 1)],
                                                    wAP(abuf + 512 * c + 128 * jj, dims=[[WARENA, 128], [1, 128]]),
                                                    warena[:, w0 + 4096 * c + n0:w0 + 4096 * c + n0 + 512],
                                                    start=(c == 0), stop=(c == 7))
                                last.then_inc(SP.h, 1)
                        on("tensor", te)
                        a_silu = SA.bump()

                        def ac_s(a, g_t=g_t, j=j, par=par, rsg_sb=rsg_sb, evd=evd):
                            a.wait_ge(SP.h, g_t)
                            a.wait_ge(SV.h, max(rsg_sb, evd[1]))
                            a.activation(gbuf[:, 512 * par:512 * (par + 1)],
                                         pw0[:, 512 * par:512 * (par + 1)], AF.Silu,
                                         scale=rs1g_all[:, j:j + 1]).then_inc(SA.h, 1)
                        on("scalar", ac_s)
                        h_t = SV.bump()

                        def ve_h(v, u_t=u_t, a_silu=a_silu, grp=grp, par=par, hq_free=st["hq_free"]):
                            v.wait_ge(SP.h, u_t)
                            v.wait_ge(SA.h, a_silu)
                            if grp == 0:
                                v.wait_ge(SH.h, hq_free)
                            v.tensor_tensor(fstream[:, 512 * grp:512 * (grp + 1)],
                                            pw1[:, 512 * par:512 * (par + 1)],
                                            gbuf[:, 512 * par:512 * (par + 1)], op=ALU.mult).then_inc(SV.h, 1)
                        on("vector", ve_h)
                        ev_par[par] = (a_silu, h_t)
                    am_t = SV.bump()
                    on("vector", lambda v, am_t=am_t: (
                        v.reduce_max(stats[:, 38:39], fstream[:, :], axis=AX.X,
                                     apply_absolute_value=True).then_inc(SV.h, 1)))
                    mc_t = SV.bump()

                    def ve_mc(v, am_t=am_t, j=j):
                        v.wait_ge(SV.h, am_t)
                        v.tensor_scalar(stats[:, 39:40], stats[:, 38:39], scalar1=cc_all[:, j:j + 1],
                                        scalar2=1e-5, op0=ALU.mult, op1=ALU.max)
                        v.tensor_copy(padt[:, 128:192], padt[:, 192:256]).then_inc(SV.h, 1)
                    on("vector", ve_mc)
                    a_q2 = SA.bump()

                    def ac_q2(a, mc_t=mc_t, j=j):
                        a.wait_ge(SV.h, mc_t)
                        a.activation(stats[:, 40:41], stats[:, 39:40], AF.Ln)
                        a.activation(padt[:, 0:32], padt[:, 64:96], AF.Identity)
                        a.activation(stats[:, 41:42], stats[:, 40:41], AF.Exp, scale=-1.0)
                        a.activation(padt[:, 0:32], padt[:, 64:96], AF.Identity)
                        a.activation(stats[:, 42:43], stats[:, 41:42], AF.Identity,
                                     scale=cc_all[:, j:j + 1])
                        a.activation(padt[:, 0:64], padt[:, 64:128], AF.Identity).then_inc(SA.h, 1)
                    on("scalar", ac_q2)
                    hq_t = SV.n + 2
                    SV.bump(2)

                    def ve_hq(v, a_q2=a_q2, j=j):
                        v.wait_ge(SA.h, a_q2)
                        v.tensor_scalar(stats[:, 43:44], stats[:, 42:43], scalar1=127.0, scalar2=None, op0=ALU.mult)
                        v.tensor_scalar(sdown_all[:, j:j + 1], stats[:, 39:40], scalar1=wsc[:, 5:6],
                                        scalar2=1.0 / 127, op0=ALU.mult, op1=ALU.mult)
                        v.tensor_copy(padt[:, 128:192], padt[:, 192:256])
                        v.tensor_scalar(fstream[:, :], fstream[:, :], scalar1=stats[:, 43:44],
                                        scalar2=RC, op0=ALU.mult, op1=ALU.add)
                        v.tensor_scalar(fstream[:, :], fstream[:, :], scalar1=RC,
                                        scalar2=None, op0=ALU.subtract).then_inc(SV.h, 2)
                    on("vector", ve_hq)

                    def gpl(g, j=j, hq_t=hq_t):
                        g.wait_ge(SV.h, hq_t)
                        g.dma_start(hq_sc[128 * j:128 * (j + 1), :], fstream[:, :]).then_inc(SH.h, 16)
                    SH.bump(16)
                    st["hq_free"] = SH.n
                    on("gpsimd", gpl)
                ai_free[sb % 2] = SP.n

            for j in range(NT):
                emit_p0_tile(j)
                if j % 4 == 3:
                    emit_f1_super(j // 4)
            D["p0_st"] = ST.n
            D["f1_pe"] = SP.n
            D["hq_sh"] = SH.n

        p0_f1_phase()

        CVB = SX.n
        cv_mark = {}
        CV_SRC = [(nm, hv) for nm in ["wkc", "wvc", "wqc", "woc", "wql", "wkl", "wvl", "wol"]
                  for hv in range(4)]

        def emit_conv_tile(t):
            nm, hv = CV_SRC[t]
            if True:
                if True:
                    f0 = 2048 * (t % 2)

                    def gyl(g, nm=nm, hv=hv, t=t, f0=f0, hq_sh=D["hq_sh"]):
                        if t < 2:
                            g.wait_ge(SH.h, hq_sh)
                        else:
                            g.wait_ge(SV.h, cv_mark[t - 2])
                        g.dma_start(fstream[:, f0:f0 + 2048],
                                    win[nm][:, 2048 * hv:2048 * (hv + 1)]).then_inc(SX.h, 16)
                    SX.bump(16)
                    on("gpsimd", gyl)
                    st_full = ST.n
                    vvt = SV.bump()

                    def vv(v, t=t, f0=f0, st_full=st_full, CVB=CVB):
                        v.wait_ge(SX.h, CVB + 16 * (t + 1))
                        v.wait_ge(ST.h, st_full)
                        v.tensor_copy(p0stage[:, 0:2048], fstream[:, f0:f0 + 2048]).then_inc(SV.h, 1)
                    on("vector", vv)
                    cv_mark[t] = vvt

                    def scl(sc, nm=nm, hv=hv, vvt=vvt):
                        sc.wait_ge(SV.h, vvt)
                        sc.dma_start(wbf_sc[:, WOFF[nm] + 2048 * hv:WOFF[nm] + 2048 * (hv + 1)],
                                     p0stage[:, 0:2048]).then_inc(ST.h, 16)
                    ST.bump(16)
                    on("scalar", scl)
        def emit_conv_done():
            st_all = ST.n
            mk = SV.bump()

            def ve_mark(v, st_all=st_all):
                v.wait_ge(ST.h, st_all)
                v.tensor_copy(padt[:, 128:192], padt[:, 192:256]).then_inc(SV.h, 1)
            on("vector", ve_mark)
            D["wbf_v"] = mk
            D["wbf_st"] = st_all
            D["conv_v"] = SV.n

        def shifts_phase():
            for si, (bsrc, rcol) in enumerate([(bqkv_c, 144), (bqkv_l, 160)]):
                ldsh = SL.n + 32

                def sy_sh(s, bsrc=bsrc, conv_v=D["conv_v"]):
                    s.wait_ge(SV.h, conv_v)
                    s.dma_start(fstream[0:1, 2048:3072], bass.AP(bsrc.tensor, 0, [[1024, 1], [1, 1024]])).then_inc(SL.h, 16)
                    s.dma_start(fstream[0:1, 3072:4096], bass.AP(bsrc.tensor, 1024, [[1024, 1], [1, 1024]])).then_inc(SL.h, 16)
                SL.bump(32)
                on("sync", sy_sh)
                b0 = SV.n
                SV.bump(3)

                def ve_sh(v, ldsh=ldsh, rcol=rcol, b0=b0):
                    v.wait_ge(SL.h, ldsh)
                    v.tensor_tensor(fstream[0:1, 2048:3072], fstream[0:1, 2048:3072],
                                    fstream[0:1, 3072:4096], op=ALU.mult).then_inc(SV.h, 1)
                    v.wait_ge(SV.h, b0 + 1)
                    last = None
                    for h in range(16):
                        last = v.reduce_sum(rowred[0:1, rcol + h:rcol + h + 1],
                                            fstream[0:1, 2048 + 64 * h:2048 + 64 * (h + 1)], axis=AX.X)
                    last.then_inc(SV.h, 1)
                    v.wait_ge(SV.h, b0 + 2)
                    v.tensor_copy(padt[:, 128:192], padt[:, 192:256])
                    v.tensor_scalar(rowred[0:1, rcol:rcol + 16], rowred[0:1, rcol:rcol + 16],
                                    scalar1=-0.125, scalar2=-20.0, op0=ALU.mult, op1=ALU.add)
                    v.tensor_copy(padt[:, 128:192], padt[:, 192:256]).then_inc(SV.h, 1)
                on("vector", ve_sh)
            sh_v = SV.n
            stsh = ST.n + 16

            def sc_sh(sc, sh_v=sh_v):
                sc.wait_ge(SV.h, sh_v)
                sc.dma_start(csum_sc[160:192], rowred[0:1, 144:176]).then_inc(ST.h, 16)
            ST.bump(16)
            on("scalar", sc_sh)
            ldsh2 = SL.n + 32

            def sy_sh2(s, stsh=stsh):
                s.wait_ge(ST.h, stsh)
                s.dma_start(cshift[0:16, 16:17], bass.AP(csum_sc.tensor, 160, [[1, 16], [1, 1]])).then_inc(SL.h, 16)
                s.dma_start(cshift[:, 0:16], bass.AP(csum_sc.tensor, 176, [[0, 128], [1, 16]])).then_inc(SL.h, 16)
            SL.bump(32)
            on("sync", sy_sh2)
            D["sh_ld"] = SL.n


        def f2_phase():
            WD0 = 0
            HF = [32768, 49152]
            SUPER = 4
            ld_w = SL.n + 16

            def syw(s, f1_pe=D["f1_pe"]):
                s.wait_ge(SP.h, f1_pe)
                s.dma_start(warena[:, WD0:WD0 + 32768], wdq_sc[:, :]).then_inc(SL.h, 16)
            SL.bump(16)
            on("sync", syw)
            hf_free = {0: D["f1_pe"], 1: D["f1_pe"]}
            x_free = {0: 0, 1: 0}
            ev_par = {0: 0, 1: 0}
            for sb in range(NT // SUPER):
                hbuf = HF[sb % 2]

                def syl(s, sb=sb, hbuf=hbuf, pf=hf_free[sb % 2], hq_sh=D["hq_sh"]):
                    s.wait_ge(SH.h, hq_sh)
                    s.wait_ge(SP.h, pf)
                    for c in range(32):
                        s.dma_start(warena[:, hbuf + 512 * c:hbuf + 512 * (c + 1)],
                                    bass.AP(hq_sc.tensor, (512 * sb) * INNER + 128 * c,
                                            [[INNER, 512], [1, 128]]), transpose=True).then_inc(SL.h, 16)
                SL.bump(16 * 32)
                hld = SL.n
                on("sync", syl)
                for jj in range(SUPER):
                    j = sb * SUPER + jj
                    par = j % 2
                    pdn = pw0 if par == 0 else px
                    buf = j % 2
                    xld = SL.n + 16

                    def syx(s, j=j, buf=buf, xf=x_free[buf]):
                        s.wait_ge(SV.h, xf)
                        s.dma_start(xbuf[buf][:, :], xin[128 * j:128 * (j + 1), :]).then_inc(SL.h, 16)
                    SL.bump(16)
                    on("sync", syx)
                    dn_t = SP.bump()

                    def te(t, jj=jj, hbuf=hbuf, hld=hld, ld_w=ld_w, evd=ev_par[par], pdn=pdn):
                        t.wait_ge(SL.h, max(hld, ld_w))
                        t.wait_ge(SV.h, evd)
                        last = None
                        for half in range(2):
                            n0 = 512 * half
                            for c in range(32):
                                last = t.matmul(pdn[:, n0:n0 + 512],
                                                wAP(hbuf + 512 * c + 128 * jj, dims=[[WARENA, 128], [1, 128]]),
                                                warena[:, WD0 + 1024 * c + n0:WD0 + 1024 * c + n0 + 512],
                                                start=(c == 0), stop=(c == 31))
                        last.then_inc(SP.h, 1)
                    on("tensor", te)
                    out_t = SV.bump()

                    def ve_o(v, dn_t=dn_t, xld=xld, j=j, buf=buf, pdn=pdn):
                        v.wait_ge(SP.h, dn_t)
                        v.wait_ge(SL.h, xld)
                        v.tensor_scalar(evac1[:, :], pdn[:, :], scalar1=sdown_all[:, j:j + 1],
                                        scalar2=None, op0=ALU.mult)
                        v.tensor_tensor(evac1[:, :], evac1[:, :], xbuf[buf][:, :], op=ALU.add)
                        v.tensor_tensor(evac1[:, :], evac1[:, :], bo_bc[:, :], op=ALU.add).then_inc(SV.h, 1)
                    on("vector", ve_o)
                    ev_par[par] = out_t
                    x_free[buf] = out_t

                    def scl(sc, j=j, out_t=out_t):
                        sc.wait_ge(SV.h, out_t)
                        sc.dma_start(part_sc[128 * j:128 * (j + 1), :], evac1[:, :]).then_inc(ST.h, 16)
                    ST.bump(16)
                    st_now = ST.n
                    on("scalar", scl)
                    on("vector", lambda v, st_now=st_now: v.wait_ge(ST.h, st_now))
                    emit_conv_tile(j)
                hf_free[sb % 2] = SP.n
            D["part_st"] = ST.n
            D["f2_pe"] = SP.n
            emit_conv_done()

        f2_phase()
        shifts_phase()

        def attn_phase():
            def gones(g, wbf_st=D["wbf_st"]):
                g.wait_ge(ST.h, wbf_st)
                g.memset(p0stage[0:1, 0:2176], 1.0).then_inc(SO.h, 1)
            D["ones_rdy"] = SO.bump()
            on("gpsimd", gones)
            A_W1, A_W2, A_W3, A_W4 = 0, 8192, 16384, 24576
            A_XNF = 32768
            A_QF = 41984
            A_KF = 50176
            A_VS = 59392
            A_KCB = 61504
            A_PT = 63552
            A_CTM = 63808
            A_CFM = 64832
            A_QHAT = 65856
            A_SCR = 66048
            A_ONES = 68096
            A_PCLS = 70272

            def pclsAP(col, n):
                return bass.AP(warena, A_PCLS + col, [[WARENA, 16], [1, n]])

            def wslot_load(slot, nm, after_pe=None):
                def syl(s, slot=slot, nm=nm, after_pe=after_pe, wbf_v=D["wbf_v"]):
                    s.wait_ge(SV.h, wbf_v)
                    if after_pe is not None:
                        s.wait_ge(SP.h, after_pe)
                    s.dma_start(warena[:, slot:slot + 8192],
                                wbf_sc[:, WOFF[nm]:WOFF[nm] + 8192]).then_inc(SL.h, 16)
                SL.bump(16)
                on("sync", syl)
                return SL.n

            for seq in range(NSEQ):
                tb = seq * S
                seq_barrier = SP.n
                seq_sv = SV.n
                seq_sa = SA.n
                seq_st = ST.n
                on("tensor", lambda t, sv=seq_sv, sa=seq_sa: (t.wait_ge(SV.h, sv), t.wait_ge(SA.h, sa)))
                ld_vl = wslot_load(A_W1, "wvl", after_pe=seq_barrier)
                ld_vc = wslot_load(A_W2, "wvc", after_pe=seq_barrier)
                vbase_l = seq * 1040 * 2176
                vbase_c = seq * 1040 * 2048

                def sc_pad(sc, vbase_l=vbase_l, vbase_c=vbase_c, ip=D["init_pool"]):
                    sc.wait_ge(SO.h, max(ip, D["ones_rdy"]))
                    for r in range(9):
                        rr = min(128 * r, 1040 - 128)
                        sc.dma_start(bass.AP(vloc_sc.tensor, vbase_l + rr * 2176, [[2176, 128], [1, 16]]),
                                     zrow[:, 0:16]).then_inc(ST.h, 16)
                        sc.dma_start(bass.AP(vloc_sc.tensor, vbase_l + rr * 2176 + 2064, [[2176, 128], [1, 112]]),
                                     zrow[:, 0:112]).then_inc(ST.h, 16)
                    for hh in range(16):
                        sc.dma_start(bass.AP(vloc_sc.tensor, vbase_l + (65 * hh + 64) * 2176, [[2176, 1], [1, 2176]]),
                                     p0stage[0:1, 0:2176]).then_inc(ST.h, 16)
                        sc.dma_start(bass.AP(vcls_sc.tensor, vbase_c + (65 * hh + 64) * 2048, [[2048, 1], [1, 2048]]),
                                     p0stage[0:1, 0:2048]).then_inc(ST.h, 16)
                ST.bump(16 * 50)
                on("scalar", sc_pad)

                for vh in range(2):
                    hs = vh * 1024
                    xnf_prev = SP.n

                    def syx(s, hs=hs, xnf_prev=xnf_prev, p0_st=D["p0_st"], tb=tb):
                        s.wait_ge(ST.h, p0_st)
                        s.wait_ge(SP.h, xnf_prev)
                        for c in range(8):
                            s.dma_start(warena[:, A_XNF + 1152 * c:A_XNF + 1152 * c + 1024],
                                        bass.AP(xn_sc.tensor, (tb + hs) * E + 128 * c,
                                                [[E, 1024], [1, 128]]), transpose=True).then_inc(SL.h, 16)
                    SL.bump(128)
                    xnf_t = SL.n
                    on("sync", syx)
                    ci_par = 0
                    for wslot, bcol, dst_sc, dcols, wld in [(A_W1, 40, vloc_sc, 2176, ld_vl),
                                                           (A_W2, 16, vcls_sc, 2048, ld_vc)]:
                        for c in range(8):
                            par = ci_par % 2
                            ci_par += 1
                            pp = px if par == 0 else pw0
                            prev_ev = SA.n
                            pj = SP.bump()

                            def te(t, wslot=wslot, c=c, xnf_t=xnf_t, wld=wld, prev_ev=prev_ev, pp=pp):
                                t.wait_ge(SL.h, max(xnf_t, wld))
                                t.wait_ge(SA.h, prev_ev)
                                last = None
                                for half in range(2):
                                    n0 = 512 * half
                                    for ci in range(8):
                                        last = t.matmul(pp[:, n0:n0 + 512],
                                                        warena[:, wslot + 1024 * ci + 128 * c:wslot + 1024 * ci + 128 * (c + 1)],
                                                        warena[:, A_XNF + 1152 * ci + n0:A_XNF + 1152 * ci + n0 + 512],
                                                        start=(ci == 0), stop=(ci == 7))
                                last.then_inc(SP.h, 1)
                            on("tensor", te)
                            prev_st2 = ST.n
                            ev_t = SA.bump()

                            def ac_ev(a, pj=pj, bcol=bcol, c=c, bfm_ld=D["bfm_ld"], pp=pp, prev_st2=prev_st2):
                                a.wait_ge(SP.h, pj)
                                a.wait_ge(SL.h, bfm_ld)
                                a.wait_ge(ST.h, prev_st2)
                                a.activation(warena[:, A_SCR:A_SCR + 1024], pp[:, 0:1024], AF.Identity,
                                             bias=bfm[:, bcol + c:bcol + c + 1]).then_inc(SA.h, 1)
                            on("scalar", ac_ev)

                            def sc_st(sc, ev_t=ev_t, c=c, dst_sc=dst_sc, dcols=dcols, hs=hs, seq=seq):
                                sc.wait_ge(SA.h, ev_t)
                                coff = seq * 1040 * dcols + (16 if dcols == 2176 else 0) + hs
                                for g in range(2):
                                    sc.dma_start(bass.AP(dst_sc.tensor, (130 * c + 65 * g) * dcols + coff,
                                                         [[dcols, 64], [1, 1024]]),
                                                 bass.AP(warena, (64 * g) * WARENA + A_SCR,
                                                         [[WARENA, 64], [1, 1024]])).then_inc(ST.h, 16)
                            ST.bump(32)
                            on("scalar", sc_st)
                a0_st = ST.n
                a0_pe = SP.n

                ld_kc = wslot_load(A_W1, "wkc", after_pe=a0_pe)
                ld_ql = wslot_load(A_W2, "wql", after_pe=a0_pe)
                ld_kl = wslot_load(A_W4, "wkl", after_pe=seq_barrier)
                ld_qc = wslot_load(A_W3, "wqc", after_pe=seq_barrier)
                wol_ld = [None]
                qh_done = [0]
                pfree = {0: 0, 1: 0}
                vs_tiles = {}
                ctxev = {0: 0, 1: 0}
                for hf in range(2):
                    hs = hf * 1024
                    xnf_prev = SP.n

                    def syx(s, hf=hf, hs=hs, xnf_prev=xnf_prev, tb=tb):
                        s.wait_ge(SP.h, xnf_prev)
                        lo = max(0, hs - 16)
                        hi = min(S, hs + 1136)
                        rows = hi - lo
                        cofs = lo - hs + 16
                        for c in range(8):
                            s.dma_start(warena[:, A_XNF + 1152 * c + cofs:A_XNF + 1152 * c + cofs + rows],
                                        bass.AP(xn_sc.tensor, (tb + lo) * E + 128 * c,
                                                [[E, rows], [1, 128]]), transpose=True).then_inc(SL.h, 16)
                    SL.bump(128)
                    xnf_t = SL.n
                    on("sync", syx)
                    xpad_t = SO.bump()

                    def gp_xpad(g, hf=hf, xnf_prev=xnf_prev):
                        g.wait_ge(SP.h, xnf_prev)
                        last = None
                        for c in range(8):
                            if hf == 0:
                                last = g.memset(warena[:, A_XNF + 1152 * c:A_XNF + 1152 * c + 16], 0.0)
                            else:
                                last = g.memset(warena[:, A_XNF + 1152 * c + 1040:A_XNF + 1152 * (c + 1)], 0.0)
                        last.then_inc(SO.h, 1)
                    on("gpsimd", gp_xpad)

                    ci_par = 0
                    for dst0, wld, wslot, bbase, width, cofs in [(A_QF, ld_ql, A_W2, 24, 1024, 16),
                                                                (A_KF, ld_kl, A_W4, 32, 1152, 0)]:
                        for c in range(8):
                            par = ci_par % 2
                            ci_par += 1
                            pp = px if par == 0 else pw0
                            p3 = (pw1, 0) if par == 0 else (pw1, 512)
                            prev_ev = SA.n
                            pj = SP.bump()

                            def te(t, c=c, wslot=wslot, width=width, cofs=cofs, xnf_t=xnf_t,
                                   wld=wld, xpad_t=xpad_t, prev_ev=prev_ev, pp=pp, p3=p3):
                                t.wait_ge(SL.h, max(xnf_t, wld))
                                t.wait_ge(SO.h, xpad_t)
                                t.wait_ge(SA.h, prev_ev)
                                spans = [(0, 512, pp, 0), (512, 512, pp, 512)]
                                if width == 1152:
                                    spans.append((1024, 128, p3[0], p3[1]))
                                last = None
                                for n0, nl, dstp, d0 in spans:
                                    for ci in range(8):
                                        last = t.matmul(dstp[:, d0:d0 + nl],
                                                        warena[:, wslot + 1024 * ci + 128 * c:wslot + 1024 * ci + 128 * (c + 1)],
                                                        warena[:, A_XNF + 1152 * ci + cofs + n0:A_XNF + 1152 * ci + cofs + n0 + nl],
                                                        start=(ci == 0), stop=(ci == 7))
                                last.then_inc(SP.h, 1)
                            on("tensor", te)
                            ev_t = SA.bump()

                            def ac_ev(a, pj=pj, c=c, dst0=dst0, width=width, bbase=bbase,
                                      pp=pp, p3=p3, bfm_ld=D["bfm_ld"]):
                                a.wait_ge(SP.h, pj)
                                a.wait_ge(SL.h, bfm_ld)
                                last = a.activation(warena[:, dst0 + width * c:dst0 + width * c + min(width, 1024)],
                                                    pp[:, 0:min(width, 1024)], AF.Identity,
                                                    bias=bfm[:, bbase + c:bbase + c + 1])
                                if width == 1152:
                                    last = a.activation(warena[:, dst0 + width * c + 1024:dst0 + width * (c + 1)],
                                                        p3[0][:, p3[1]:p3[1] + 128], AF.Identity,
                                                        bias=bfm[:, bbase + c:bbase + c + 1])
                                last.then_inc(SA.h, 1)
                            on("scalar", ac_ev)
                    kq_ev = SA.n
                    kq_pe = SP.n

                    v_q = 0
                    if hf == 0:
                        ld_bq = SL.n + 16

                        def sy_bq(s, wbf_v=D["wbf_v"]):
                            s.wait_ge(SV.h, wbf_v)
                            s.dma_start(qrow[0:1, 0:1024],
                                        bass.AP(bbf_sc.tensor, 0, [[1024, 1], [1, 1024]])).then_inc(SL.h, 16)
                        SL.bump(16)
                        on("sync", sy_bq)
                        pj_q = SP.bump()

                        def te_q(t, xnf_t=xnf_t, ld_qc=ld_qc, kq_ev=kq_ev):
                            t.wait_ge(SL.h, max(xnf_t, ld_qc))
                            t.wait_ge(SA.h, kq_ev)
                            last = None
                            for half in range(2):
                                n0 = 512 * half
                                for ci in range(8):
                                    last = t.matmul(px[0:1, n0:n0 + 512],
                                                    wAP(A_XNF + 1152 * ci + 16, dims=[[WARENA, 128], [1, 1]]),
                                                    warena[:, A_W3 + 1024 * ci + n0:A_W3 + 1024 * ci + n0 + 512],
                                                    start=(ci == 0), stop=(ci == 7))
                            last.then_inc(SP.h, 1)
                        on("tensor", te_q)
                        v_q = SV.bump()

                        def ve_q(v, pj_q=pj_q, ld_bq=ld_bq):
                            v.wait_ge(SP.h, pj_q)
                            v.wait_ge(SL.h, ld_bq)
                            v.tensor_tensor(qrow[0:1, 0:1024], px[0:1, 0:1024], qrow[0:1, 0:1024],
                                            op=ALU.add).then_inc(SV.h, 1)
                        on("vector", ve_q)
                        qh0 = SO.bump()
                        on("gpsimd", lambda g, v_q=v_q: (
                            g.memset(warena[:, A_QHAT:A_QHAT + 128], 0.0).then_inc(SO.h, 1)))
                        tq_t = SP.bump()

                        def te_tq(t, v_q=v_q):
                            t.wait_ge(SV.h, v_q)
                            last = None
                            for c in range(8):
                                last = t.transpose(ptp[:, 2 * c:2 * c + 1], qrow[0:1, 128 * c:128 * (c + 1)], ident[0:1, 0:1])
                            last.then_inc(SP.h, 1)
                        on("tensor", te_tq)
                        qh_t = SV.bump()

                        def ve_qh(v, tq_t=tq_t, qh0=qh0):
                            v.wait_ge(SP.h, tq_t)
                            v.wait_ge(SO.h, qh0)
                            last = None
                            for c in range(8):
                                last = v.tensor_copy(wAP(A_QHAT + 16 * c + 2 * c, dims=[[WARENA, 64], [1, 1]]),
                                                     ptp[0:64, 2 * c:2 * c + 1])
                                last = v.tensor_copy(wAP(A_QHAT + 16 * c + 2 * c + 1, pbase=64,
                                                         dims=[[WARENA, 64], [1, 1]]),
                                                     ptp[64:128, 2 * c:2 * c + 1])
                            last.then_inc(SV.h, 1)
                        on("vector", ve_qh)
                        qh_done[0] = qh_t

                    for c in range(8):
                        kcb = A_KCB + 1024 * (c % 2)
                        prev_ev = SA.n
                        kc_pj = SP.bump()

                        def te_kc(t, c=c, xnf_t=xnf_t, ld_kc=ld_kc, xpad_t=xpad_t,
                                  prev_ev=prev_ev, v_q=v_q):
                            t.wait_ge(SL.h, max(xnf_t, ld_kc))
                            t.wait_ge(SO.h, xpad_t)
                            t.wait_ge(SA.h, prev_ev)
                            if c == 0 and v_q:
                                t.wait_ge(SV.h, v_q)
                            last = None
                            for half in range(2):
                                n0 = 512 * half
                                for ci in range(8):
                                    last = t.matmul(px[:, n0:n0 + 512],
                                                    warena[:, A_W1 + 1024 * ci + 128 * c:A_W1 + 1024 * ci + 128 * (c + 1)],
                                                    warena[:, A_XNF + 1152 * ci + 16 + n0:A_XNF + 1152 * ci + 16 + n0 + 512],
                                                    start=(ci == 0), stop=(ci == 7))
                            last.then_inc(SP.h, 1)
                        on("tensor", te_kc)
                        kc_ev = SA.bump()

                        def ac_kc(a, kc_pj=kc_pj, kcb=kcb, c=c, bfm_ld=D["bfm_ld"]):
                            a.wait_ge(SP.h, kc_pj)
                            a.wait_ge(SL.h, bfm_ld)
                            a.activation(warena[:, kcb:kcb + 1024], px[:, 0:1024], AF.Identity,
                                         bias=bfm[:, 8 + c:8 + c + 1]).then_inc(SA.h, 1)
                        on("scalar", ac_kc)
                        sc_t = SP.bump()

                        def te_sc(t, kc_ev=kc_ev, c=c, kcb=kcb, qh=qh_done[0]):
                            t.wait_ge(SA.h, kc_ev)
                            t.wait_ge(SV.h, qh)
                            last = None
                            for half in range(2):
                                last = t.matmul(pw1[0:16, 512 * half:512 * half + 512],
                                                wAP(A_QHAT + 16 * c, dims=[[WARENA, 128], [1, 16]]),
                                                warena[:, kcb + 512 * half:kcb + 512 * (half + 1)],
                                                start=(c == 0), stop=(c == 7))
                            last.then_inc(SP.h, 1)
                        on("tensor", te_sc)
                    cls_sc_pe = SP.n
                    e_t = SA.bump()

                    def ac_ecls(a, cls_sc_pe=cls_sc_pe, hf=hf, shl=D["sh_ld"]):
                        a.wait_ge(SP.h, cls_sc_pe)
                        a.wait_ge(SL.h, shl)
                        a.activation(pclsAP(1024 * hf, 512), pw1[0:16, 0:512],
                                     AF.Exp, scale=0.125, bias=cshift[0:16, 16:17])
                        a.activation(pclsAP(1024 * hf + 512, 512), pw1[0:16, 512:1024],
                                     AF.Exp, scale=0.125, bias=cshift[0:16, 16:17]).then_inc(SA.h, 1)
                    on("scalar", ac_ecls)
                    for kk in range(8):
                        prev_cp = SV.n
                        ttp = SP.bump()

                        def te_tp(t, e_t=e_t, hf=hf, kk=kk, prev_cp=prev_cp):
                            t.wait_ge(SA.h, e_t)
                            t.wait_ge(SV.h, prev_cp)
                            t.transpose(ptp[:, 16:32],
                                        pclsAP(1024 * hf + 128 * kk, 128),
                                        ident[0:16, 0:16]).then_inc(SP.h, 1)
                        on("tensor", te_tp)
                        cp_t = SV.bump()

                        def ve_cp(v, ttp=ttp, hf=hf, kk=kk):
                            v.wait_ge(SP.h, ttp)
                            v.tensor_copy(warena[:, A_PT + 16 * (8 * hf + kk):A_PT + 16 * (8 * hf + kk + 1)],
                                          ptp[:, 16:32]).then_inc(SV.h, 1)
                        on("vector", ve_cp)

                    if hf == 0:
                        wol_ld[0] = wslot_load(A_W3, "wol", after_pe=SP.n)

                    pmask = {}
                    seq_state_scx = {}
                    for jj in range(NQT):
                        qs = hs + 128 * jj
                        gj = qs // 128
                        for which in range(2):
                            gv = gj + which
                            key = f"t{gv}"
                            if key in vs_tiles:
                                continue
                            prev_pe = SP.n

                            def syv(s, gv=gv, prev_pe=prev_pe, a0_st=a0_st, seq=seq):
                                s.wait_ge(ST.h, a0_st)
                                s.wait_ge(SP.h, prev_pe)
                                s.dma_start(warena[:, A_VS + VPAD * (gv % 2):A_VS + VPAD * (gv % 2) + 1040],
                                            bass.AP(vloc_sc.tensor, seq * 1040 * 2176 + 128 * gv,
                                                    [[2176, 1040], [1, 128]]),
                                            transpose=True).then_inc(SL.h, 16)
                            SL.bump(16)
                            vs_tiles.pop(f"t{gv - 2}", None)
                            vs_tiles[key] = SL.n
                            on("sync", syv)
                        vs_t = max(vs_tiles[f"t{gj}"], vs_tiles[f"t{gj + 1}"])
                        mbase = 256 if gj == 0 else (512 if gj == 15 else 0)
                        av_last = {}
                        psc_map = [(pw1, 0), (pw1, 512), (py, 0)]
                        for ph in range(H + 3):
                            if ph < H:
                                h = ph
                                hh, hb = h // 2, 64 * (h % 2)
                                pscT, psc0 = psc_map[h % 3]
                                pts = A_SCR + 256 * (h % 4)
                                prev_exp = seq_state_scx.get(h % 3, e_t)
                                s_t = SP.bump()

                                def te_s(t, h=h, hh=hh, hb=hb, jj=jj, pscT=pscT, psc0=psc0,
                                         prev_exp=prev_exp, kq_ev=kq_ev, vs_t=vs_t):
                                    t.wait_ge(SA.h, kq_ev)
                                    t.wait_ge(SA.h, prev_exp)
                                    t.wait_ge(SL.h, vs_t)
                                    last = None
                                    for ch in range(2):
                                        last = t.matmul(pscT[:, psc0 + 128 * ch:psc0 + 128 * (ch + 1)],
                                                        wAP(A_KF + 1152 * hh + 128 * (jj + ch), pbase=hb,
                                                            dims=[[WARENA, 64], [1, 128]]),
                                                        wAP(A_QF + 1024 * hh + 128 * jj, pbase=hb,
                                                            dims=[[WARENA, 64], [1, 128]]),
                                                        start=True, stop=True)
                                    last.then_inc(SP.h, 1)
                                on("tensor", te_s)
                                ex_t = SA.bump()

                                def ac_e(a, s_t=s_t, pscT=pscT, psc0=psc0, pts=pts, h=h, shl=D["sh_ld"]):
                                    a.wait_ge(SP.h, s_t)
                                    a.wait_ge(SL.h, shl)
                                    a.activation(warena[:, pts:pts + 256], pscT[:, psc0:psc0 + 256],
                                                 AF.Exp, scale=0.125, bias=cshift[:, h:h + 1]).then_inc(SA.h, 1)
                                on("scalar", ac_e)
                                seq_state_scx[h % 3] = ex_t
                                m_t = SV.bump()

                                def ve_m(v, ex_t=ex_t, pts=pts, mbase=mbase, ip=D["init_pool"]):
                                    v.wait_ge(SA.h, ex_t)
                                    v.wait_ge(SO.h, ip)
                                    v.tensor_tensor(warena[:, pts:pts + 256], warena[:, pts:pts + 256],
                                                    masks[:, mbase:mbase + 256], op=ALU.mult).then_inc(SV.h, 1)
                                on("vector", ve_m)
                                pmask[h] = m_t
                            h2 = ph - 3
                            if h2 < 0 or h2 >= H:
                                continue
                            h = h2
                            pts = A_SCR + 256 * (h % 4)
                            pctx0 = 65 * (h % 4) + 512 * ((h // 4) % 2)
                            av_t = SP.bump()

                            def te_av(t, m_t=pmask[h], pts=pts, h=h, gj=gj, pctx0=pctx0,
                                      prev_ev=ctxev[(h // 4) % 2]):
                                t.wait_ge(SV.h, m_t)
                                t.wait_ge(SV.h, prev_ev)
                                last = None
                                for ch in range(2):
                                    vslot = A_VS + VPAD * ((gj + ch) % 2)
                                    last = t.matmul(px[:, pctx0:pctx0 + 65],
                                                    warena[:, pts + 128 * ch:pts + 128 * (ch + 1)],
                                                    warena[:, vslot + 65 * h:vslot + 65 * (h + 1)],
                                                    start=(ch == 0), stop=(ch == 1))
                                last.then_inc(SP.h, 1)
                            on("tensor", te_av)
                            av_last[h] = av_t
                            if h % 4 != 3:
                                continue
                            wv = h // 4
                            avs = av_last[4 * wv + 3]
                            lc_t = SV.bump()

                            def ve_l(v, avs=avs, wv=wv):
                                v.wait_ge(SP.h, avs)
                                last = None
                                for k in range(4):
                                    h = 4 * wv + k
                                    pctx0 = 65 * (h % 4) + 512 * ((h // 4) % 2)
                                    last = v.tensor_copy(lbuf[:, k:k + 1], px[:, pctx0 + 64:pctx0 + 65])
                                v.tensor_copy(padt[:, 128:192], padt[:, 192:256]).then_inc(SV.h, 1)
                            on("vector", ve_l)
                            il_t = SA.bump()

                            def ac_il(a, lc_t=lc_t):
                                a.wait_ge(SV.h, lc_t)
                                a.activation(lbuf[:, 8:12], lbuf[:, 0:4], AF.Ln, scale=2.0 ** -32)
                                a.activation(padt[:, 0:32], padt[:, 64:96], AF.Identity)
                                a.activation(lbuf[:, 8:12], lbuf[:, 8:12], AF.Exp, scale=-1.0)
                                a.activation(padt[:, 0:64], padt[:, 64:128], AF.Identity).then_inc(SA.h, 1)
                            on("scalar", ac_il)
                            ce_t = SV.bump()

                            def ve_ce(v, il_t=il_t, wv=wv):
                                v.wait_ge(SA.h, il_t)
                                last = None
                                for k in range(4):
                                    h = 4 * wv + k
                                    pctx0 = 65 * (h % 4) + 512 * ((h // 4) % 2)
                                    last = v.tensor_scalar(warena[:, A_CTM + 64 * h:A_CTM + 64 * (h + 1)],
                                                           px[:, pctx0:pctx0 + 64],
                                                           scalar1=lbuf[:, 8 + k:9 + k], scalar2=2.0 ** -32,
                                                           op0=ALU.mult, op1=ALU.mult)
                                last.then_inc(SV.h, 1)
                            on("vector", ve_ce)
                            ctxev[wv % 2] = ce_t
                        ctx_done = SV.n
                        for c in range(8):
                            prev_cp = SV.n
                            tt = SP.bump()

                            def te_t(t, c=c, ctx_done=ctx_done, prev_cp=prev_cp):
                                t.wait_ge(SV.h, max(ctx_done, prev_cp))
                                t.transpose(ptp[:, 128 * (c % 2):128 * (c % 2) + 128],
                                            warena[:, A_CTM + 128 * c:A_CTM + 128 * (c + 1)],
                                            ident[:, :]).then_inc(SP.h, 1)
                            on("tensor", te_t)
                            cp_t = SV.bump()

                            def ve_t(v, tt=tt, c=c):
                                v.wait_ge(SP.h, tt)
                                v.tensor_copy(warena[:, A_CFM + 128 * c:A_CFM + 128 * (c + 1)],
                                              ptp[:, 128 * (c % 2):128 * (c % 2) + 128]).then_inc(SV.h, 1)
                            on("vector", ve_t)
                        cfm_done = SV.n
                        buf = jj % 2

                        def sy_p(s, qs=qs, buf=buf, pf=pfree[buf], pst=D["part_st"], tb=tb):
                            s.wait_ge(ST.h, pst)
                            s.wait_ge(SV.h, pf)
                            s.dma_start(xbuf[buf][:, :], part_sc[tb + qs:tb + qs + 128, :]).then_inc(SL.h, 16)
                        SL.bump(16)
                        pld = SL.n
                        on("sync", sy_p)
                        prev_fo = SV.n
                        op_t = SP.bump()

                        def te_op(t, cfm_done=cfm_done, wol=wol_ld[0], prev_fo=prev_fo):
                            t.wait_ge(SV.h, cfm_done)
                            t.wait_ge(SL.h, wol)
                            t.wait_ge(SV.h, prev_fo)
                            last = None
                            for half in range(2):
                                n0 = 512 * half
                                for ci in range(8):
                                    last = t.matmul(pw0[:, n0:n0 + 512],
                                                    warena[:, A_CFM + 128 * ci:A_CFM + 128 * (ci + 1)],
                                                    warena[:, A_W3 + 1024 * ci + n0:A_W3 + 1024 * ci + n0 + 512],
                                                    start=(ci == 0), stop=(ci == 7))
                            last.then_inc(SP.h, 1)
                        on("tensor", te_op)
                        fo_t = SV.bump()

                        def ve_fo(v, op_t=op_t, pld=pld, buf=buf):
                            v.wait_ge(SP.h, op_t)
                            v.wait_ge(SL.h, pld)
                            v.scalar_tensor_tensor(evac1[:, :], pw0[:, :], 1.0, xbuf[buf][:, :],
                                                   op0=ALU.mult, op1=ALU.add).then_inc(SV.h, 1)
                        on("vector", ve_fo)
                        pfree[buf] = fo_t

                        def sc_y(sc, qs=qs, fo_t=fo_t, tb=tb):
                            sc.wait_ge(SV.h, fo_t)
                            sc.dma_start(y[tb + qs:tb + qs + 128, :], evac1[:, :]).then_inc(ST.h, 16)
                        ST.bump(16)
                        st_now = ST.n
                        on("scalar", sc_y)
                        on("vector", lambda v, st_now=st_now: v.wait_ge(ST.h, st_now))
                        if DEBUG and seq == 0 and hf == 0 and jj == 0:
                            def sc_t0(sc, fo_t=fo_t):
                                sc.wait_ge(SV.h, fo_t)
                                sc.dma_start(dbg["d_t0pts"], warena[:, A_SCR:A_SCR + 1024]).then_inc(ST.h, 16)
                                sc.dma_start(dbg["d_t0ctm"], warena[:, A_CTM:A_CTM + 1024]).then_inc(ST.h, 16)
                                sc.dma_start(dbg["d_t0vs"], warena[:, A_VS:A_VS + 2112]).then_inc(ST.h, 16)
                            ST.bump(48)
                            on("scalar", sc_t0)

                ld_woc = wslot_load(A_W1, "woc", after_pe=SP.n)
                prev_sv = SV.n
                prev_sa2 = SA.n
                for g in range(16):
                    prev_pe = SP.n

                    def syv(s, g=g, prev_pe=prev_pe, a0_st=a0_st, seq=seq):
                        s.wait_ge(ST.h, a0_st)
                        s.wait_ge(SP.h, prev_pe)
                        s.dma_start(warena[:, A_VS + VPAD * (g % 2):A_VS + VPAD * (g % 2) + 1040],
                                    bass.AP(vcls_sc.tensor, seq * 1040 * 2048 + 128 * g,
                                            [[2048, 1040], [1, 128]]),
                                    transpose=True).then_inc(SL.h, 16)
                    SL.bump(16)
                    vld = SL.n
                    on("sync", syv)
                    cav = SP.bump()

                    def te_cav(t, g=g, vld=vld, prev_sv=prev_sv, prev_sa2=prev_sa2):
                        t.wait_ge(SL.h, vld)
                        if g == 0:
                            t.wait_ge(SV.h, prev_sv)
                            t.wait_ge(SA.h, prev_sa2)
                        vslot = A_VS + VPAD * (g % 2)
                        last = None
                        for dstp, n0, nl in [(pw0[0:16, 0:455], 0, 455), (pw0[0:16, 512:967], 455, 455),
                                             (py[0:16, 0:130], 910, 130)]:
                            last = t.matmul(dstp, warena[:, A_PT + 16 * g:A_PT + 16 * (g + 1)],
                                            warena[:, vslot + n0:vslot + n0 + nl],
                                            start=(g == 0), stop=(g == 15))
                        last.then_inc(SP.h, 1)
                    on("tensor", te_cav)
                cav_done = SP.n
                lr_t = SV.bump()
                cv_t = SV.bump()

                def ve_cv(v, cav_done=cav_done, lr_t=lr_t):
                    v.wait_ge(SP.h, cav_done)
                    v.reduce_sum(lbuf[0:16, 15:16], pclsAP(0, S), axis=AX.X).then_inc(SV.h, 1)
                    v.tensor_copy(clsrows[0:16, 0:455], pw0[0:16, 0:455])
                    v.tensor_copy(clsrows[0:16, 455:910], pw0[0:16, 512:967])
                    v.tensor_copy(clsrows[0:16, 910:1040], py[0:16, 0:130])
                    v.tensor_copy(padt[:, 128:192], padt[:, 192:256]).then_inc(SV.h, 1)
                on("vector", ve_cv)
                il_t = SA.bump()

                def ac_il2(a, cv_t=cv_t):
                    a.wait_ge(SV.h, cv_t)
                    a.activation(lbuf[0:16, 14:15], lbuf[0:16, 15:16], AF.Ln, scale=2.0 ** -32)
                    a.activation(padt[:, 0:32], padt[:, 64:96], AF.Identity)
                    a.activation(lbuf[0:16, 14:15], lbuf[0:16, 14:15], AF.Exp, scale=-1.0)
                    a.activation(padt[:, 0:64], padt[:, 64:128], AF.Identity).then_inc(SA.h, 1)
                on("scalar", ac_il2)
                sc_v = SV.bump()
                prev_stg = D.get("stg", 0)

                def ve_sc(v, il_t=il_t, prev_stg=prev_stg):
                    v.wait_ge(SA.h, il_t)
                    v.wait_ge(ST.h, prev_stg)
                    v.tensor_scalar(warena[0:16, A_SCR:A_SCR + 1040], clsrows[0:16, 0:1040],
                                    scalar1=lbuf[0:16, 14:15], scalar2=2.0 ** -32,
                                    op0=ALU.mult, op1=ALU.mult).then_inc(SV.h, 1)
                on("vector", ve_sc)
                st_g = ST.n + 16 * 16

                def sc_g(sc, sc_v=sc_v):
                    sc.wait_ge(SV.h, sc_v)
                    for c in range(8):
                        for g in range(2):
                            h = 2 * c + g
                            sc.dma_start(bass.AP(warena, (64 * g) * WARENA + A_CFM + c,
                                                 [[WARENA, 64], [1, 1]]),
                                         bass.AP(warena, h * WARENA + A_SCR + 65 * h,
                                                 [[WARENA, 1], [1, 64]])).then_inc(ST.h, 16)
                ST.bump(16 * 16)
                D["stg"] = ST.n
                on("scalar", sc_g)
                ocp = SP.bump()

                def te_oc(t, st_g=st_g, ld_woc=ld_woc):
                    t.wait_ge(ST.h, st_g)
                    t.wait_ge(SL.h, ld_woc)
                    last = None
                    for half in range(2):
                        n0 = 512 * half
                        for ci in range(8):
                            dstp = py[0:1, 0:512] if half == 0 else pw1[0:1, 0:512]
                            last = t.matmul(dstp,
                                            wAP(A_CFM + ci, dims=[[WARENA, 128], [1, 1]]),
                                            warena[:, A_W1 + 1024 * ci + n0:A_W1 + 1024 * ci + n0 + 512],
                                            start=(ci == 0), stop=(ci == 7))
                    last.then_inc(SP.h, 1)
                on("tensor", te_oc)

                def sy_pr(s, pst=D["part_st"], tb=tb, pf=pfree[0]):
                    s.wait_ge(ST.h, pst)
                    s.wait_ge(SV.h, pf)
                    s.dma_start(xbuf[0][0:1, :], part_sc[tb:tb + 1, :]).then_inc(SL.h, 16)
                SL.bump(16)
                pld2 = SL.n
                on("sync", sy_pr)
                oc_t = SV.bump()
                prev_r0 = D.get("r0_st", 0)

                def ve_oc(v, ocp=ocp, pld2=pld2, prev_r0=prev_r0):
                    v.wait_ge(SP.h, ocp)
                    v.wait_ge(SL.h, pld2)
                    v.wait_ge(ST.h, prev_r0)
                    v.tensor_copy(outcls[0:1, 0:512], py[0:1, 0:512])
                    v.tensor_copy(outcls[0:1, 512:1024], pw1[0:1, 0:512])
                    v.tensor_tensor(outcls[0:1, :], outcls[0:1, :], brow[0:1, :], op=ALU.add)
                    v.tensor_tensor(outcls[0:1, :], outcls[0:1, :], bo_bc[0:1, :], op=ALU.subtract)
                    v.tensor_tensor(outcls[0:1, :], outcls[0:1, :], xbuf[0][0:1, :], op=ALU.add).then_inc(SV.h, 1)
                on("vector", ve_oc)
                ystore = ST.n

                def sc_r0(sc, oc_t=oc_t, ystore=ystore, tb=tb):
                    sc.wait_ge(SV.h, oc_t)
                    sc.wait_ge(ST.h, ystore)
                    sc.dma_start(y[tb:tb + 1, :], outcls[0:1, :]).then_inc(ST.h, 16)
                ST.bump(16)
                D["r0_st"] = ST.n
                on("scalar", sc_r0)
                if DEBUG and seq == 0:
                    def sc_dbgc(sc, oc_t=oc_t, sc_v=sc_v, stg=D["stg"], cv_t=cv_t):
                        sc.wait_ge(SV.h, oc_t)
                        sc.wait_ge(ST.h, stg)
                        sc.dma_start(dbg["d_clsrows"], clsrows[0:16, 0:1040]).then_inc(ST.h, 16)
                        sc.dma_start(dbg["d_lb"], lbuf[0:16, 0:16]).then_inc(ST.h, 16)
                        sc.dma_start(dbg["d_outcls"], outcls[0:1, :]).then_inc(ST.h, 16)
                        sc.dma_start(dbg["d_cfm0"], warena[:, 64832:64832 + 8]).then_inc(ST.h, 16)
                        sc.dma_start(dbg["d_scr0"], warena[0:16, 66048:66048 + 1040]).then_inc(ST.h, 16)
                        sc.dma_start(dbg["d_qhat"], warena[:, 65856:65856 + 128]).then_inc(ST.h, 16)
                    ST.bump(96)
                    D["r0_st"] = ST.n
                    on("scalar", sc_dbgc)

        attn_phase()

        if DEBUG:
            dtgt = SL.n + 256

            def sy_dbg(s, stf=ST.n, dtgt=dtgt):
                s.wait_ge(ST.h, stf)
                s.dma_start(dbg["d_xn"], xn_sc).then_inc(SL.h, 16)
                s.dma_start(dbg["d_ai"], ai_sc).then_inc(SL.h, 16)
                s.dma_start(dbg["d_part"], part_sc).then_inc(SL.h, 16)
                s.dma_start(dbg["d_vloc"], bass.AP(vloc_sc.tensor, 0, [[2176, 1040], [1, 2176]])).then_inc(SL.h, 16)
                s.dma_start(dbg["d_vcls"], bass.AP(vcls_sc.tensor, 0, [[2048, 1040], [1, 2048]])).then_inc(SL.h, 16)
                s.dma_start(dbg["d_hq"], hq_sc).then_inc(SL.h, 16)
                s.dma_start(dbg["d_sb"], warena[:, 50176:50176 + 4096]).then_inc(SL.h, 16)
                s.dma_start(dbg["d_pcls"], bass.AP(warena, 70272, [[WARENA, 16], [1, S]])).then_inc(SL.h, 16)
                s.dma_start(dbg["d_qf"], warena[:, 41984:41984 + 8192]).then_inc(SL.h, 16)
                s.dma_start(dbg["d_msk"], masks[:, :]).then_inc(SL.h, 16)
                s.dma_start(dbg["d_ctm"], warena[:, 63808:63808 + 1024]).then_inc(SL.h, 16)
                s.dma_start(dbg["d_cfm"], warena[:, 64832:64832 + 1024]).then_inc(SL.h, 16)
                s.dma_start(dbg["d_vs"], warena[:, 59392:59392 + 2112]).then_inc(SL.h, 16)
                s.dma_start(dbg["d_pts"], warena[:, 66048:66048 + 512]).then_inc(SL.h, 16)
                s.dma_start(dbg["d_w3"], warena[:, 16384:16384 + 8192]).then_inc(SL.h, 16)
                s.dma_start(dbg["d_wbfol"], wbf_sc[:, 57344:57344 + 8192]).then_inc(SL.h, 16)
                s.wait_ge(SL.h, dtgt)
            SL.bump(224)
            on("sync", sy_dbg)
        st_final = ST.n
        on("sync", lambda s, st_final=st_final: s.wait_ge(ST.h, st_final))

        with nc.Block() as block:
            @block.sync
            def _(sync):
                for f in Q["sync"]:
                    f(sync)

            @block.vector
            def _(vector):
                for f in Q["vector"]:
                    f(vector)

            @block.scalar
            def _(scalar):
                for f in Q["scalar"]:
                    f(scalar)

            @block.tensor
            def _(tensor):
                for f in Q["tensor"]:
                    f(tensor)

            @block.gpsimd
            def _(gpsimd):
                for f in Q["gpsimd"]:
                    f(gpsimd)
    return nc


def _chunked_fm(w):
    o, i = w.shape
    return np.ascontiguousarray(w.reshape(o, i // 128, 128).transpose(2, 1, 0).reshape(128, (i // 128) * o))


def kernel(**inputs):
    x = np.asarray(inputs["L0_attn_hidden_states"], dtype=np.float32)
    iwc = np.asarray(inputs["L0_attn_in_w_cls"], dtype=np.float32)
    iwl = np.asarray(inputs["L0_attn_in_w_loc"], dtype=np.float32)
    wmap = {
        "wqc": _chunked_fm(iwc[0:1024]),
        "wkc": _chunked_fm(iwc[1024:2048]),
        "wvc": _chunked_fm(iwc[2048:3072]),
        "woc": _chunked_fm(np.asarray(inputs["L0_attn_out_w_cls"], dtype=np.float32)),
        "wql": _chunked_fm(iwl[0:1024]),
        "wkl": _chunked_fm(iwl[1024:2048]),
        "wvl": _chunked_fm(iwl[2048:3072]),
        "wol": _chunked_fm(np.asarray(inputs["L0_attn_out_w_loc"], dtype=np.float32)),
        "wg": _chunked_fm(np.asarray(inputs["L0_ffn_w_gate"], dtype=np.float32)),
        "wu": _chunked_fm(np.asarray(inputs["L0_ffn_w_up"], dtype=np.float32)),
        "wdn": _chunked_fm(np.asarray(inputs["L0_ffn_w_down"], dtype=np.float32)),
        "bqkv_c": np.ascontiguousarray(np.asarray(inputs["L0_attn_in_b_cls"], dtype=np.float32)),
        "bqkv_l": np.ascontiguousarray(np.asarray(inputs["L0_attn_in_b_loc"], dtype=np.float32)),
        "boc": np.ascontiguousarray(np.asarray(inputs["L0_attn_out_b_cls"], dtype=np.float32)),
        "bol": np.ascontiguousarray(np.asarray(inputs["L0_attn_out_b_loc"], dtype=np.float32)),
    }
    nc = build_nc()
    in_maps = []
    for c in range(NCORES):
        m = dict(wmap)
        m["xin"] = np.ascontiguousarray(x[NSEQ * c:NSEQ * (c + 1)].reshape(T, E))
        in_maps.append(m)
    res = run_bass_kernel_spmd(nc, in_maps, core_ids=list(range(NCORES)))
    out = np.empty((16, S, E), dtype=np.float32)
    for c in range(NCORES):
        out[NSEQ * c:NSEQ * (c + 1)] = np.asarray(res.results[c]["y"]).reshape(NSEQ, S, E)
    return out


# version 09 (final)
# speedup vs baseline: 1.0692x; 1.0044x over previous
import math
import contextlib
import numpy as np
import concourse.bass as bass
import concourse.mybir as mybir
from concourse.bass_utils import run_bass_kernel_spmd

F32, BF16 = mybir.dt.float32, mybir.dt.bfloat16
AF = mybir.ActivationFunctionType
ALU = mybir.AluOpType
AX = mybir.AxisListType

NCORES = 8
S = 2048
NSEQ = 2
T = S * NSEQ
E = 1024
H = 16
INNER = 4096
NT = T // 128
RC = 12582912.0
NQT = 8
VPAD = 1056
DEBUG = False


class Ctr:
    def __init__(self, handle):
        self.h = handle
        self.n = 0

    def bump(self, k=1):
        self.n += k
        return self.n


def build_nc():
    nc = bass.Bass("TRN2", target_bir_lowering=False)

    xin = nc.dram_tensor("xin", [T, E], F32, kind="ExternalInput").ap()
    win = {}
    for nm in ["wqc", "wkc", "wvc", "woc", "wql", "wkl", "wvl", "wol"]:
        win[nm] = nc.dram_tensor(nm, [128, 8 * E], F32, kind="ExternalInput").ap()
    wg = nc.dram_tensor("wg", [128, 8 * INNER], F32, kind="ExternalInput").ap()
    wu = nc.dram_tensor("wu", [128, 8 * INNER], F32, kind="ExternalInput").ap()
    wdn = nc.dram_tensor("wdn", [128, 32 * E], F32, kind="ExternalInput").ap()
    bqkv_c = nc.dram_tensor("bqkv_c", [3 * E], F32, kind="ExternalInput").ap()
    bqkv_l = nc.dram_tensor("bqkv_l", [3 * E], F32, kind="ExternalInput").ap()
    boc = nc.dram_tensor("boc", [E], F32, kind="ExternalInput").ap()
    bol = nc.dram_tensor("bol", [E], F32, kind="ExternalInput").ap()
    y = nc.dram_tensor("y", [T, E], F32, kind="ExternalOutput").ap()

    xn_sc = nc.dram_tensor("xn_sc", [T, E], BF16, kind="Internal").ap()
    ai_sc = nc.dram_tensor("ai_sc", [T, E], BF16, kind="Internal").ap()
    hq_sc = nc.dram_tensor("hq_sc", [T, INNER], BF16, kind="Internal").ap()
    part_sc = nc.dram_tensor("part_sc", [T, E], F32, kind="Internal").ap()
    wgq_sc = nc.dram_tensor("wgq_sc", [128, 8 * INNER], BF16, kind="Internal").ap()
    wuq_sc = nc.dram_tensor("wuq_sc", [128, 8 * INNER], BF16, kind="Internal").ap()
    wdq_sc = nc.dram_tensor("wdq_sc", [128, 32 * E], BF16, kind="Internal").ap()
    wbf_sc = nc.dram_tensor("wbf_sc", [128, 64 * E], BF16, kind="Internal").ap()
    bbf_sc = nc.dram_tensor("bbf_sc", [6144], BF16, kind="Internal").ap()
    csum_sc = nc.dram_tensor("csum_sc", [256], F32, kind="Internal").ap()
    one_sc = nc.dram_tensor("one_sc", [8], F32, kind="Internal").ap()
    vloc_sc = nc.dram_tensor("vloc_sc", [2080, 2176], BF16, kind="Internal").ap()
    vcls_sc = nc.dram_tensor("vcls_sc", [2080, 2048], BF16, kind="Internal").ap()
    WOFF = {"wkc": 0, "wvc": 8192, "wqc": 16384, "woc": 24576,
            "wql": 32768, "wkl": 40960, "wvl": 49152, "wol": 57344}
    dbg = {}
    if DEBUG:
        dbg["d_xn"] = nc.dram_tensor("d_xn", [T, E], BF16, kind="ExternalOutput").ap()
        dbg["d_ai"] = nc.dram_tensor("d_ai", [T, E], BF16, kind="ExternalOutput").ap()
        dbg["d_part"] = nc.dram_tensor("d_part", [T, E], F32, kind="ExternalOutput").ap()
        dbg["d_vloc"] = nc.dram_tensor("d_vloc", [1040, 2176], BF16, kind="ExternalOutput").ap()
        dbg["d_vcls"] = nc.dram_tensor("d_vcls", [1040, 2048], BF16, kind="ExternalOutput").ap()
        dbg["d_hq"] = nc.dram_tensor("d_hq", [T, INNER], BF16, kind="ExternalOutput").ap()
        dbg["d_sb"] = nc.dram_tensor("d_sb", [128, 4096], BF16, kind="ExternalOutput").ap()
        dbg["d_pcls"] = nc.dram_tensor("d_pcls", [16, S], BF16, kind="ExternalOutput").ap()
        dbg["d_qf"] = nc.dram_tensor("d_qf", [128, 8192], BF16, kind="ExternalOutput").ap()
        dbg["d_msk"] = nc.dram_tensor("d_msk", [128, 768], BF16, kind="ExternalOutput").ap()
        dbg["d_ctm"] = nc.dram_tensor("d_ctm", [128, 1024], BF16, kind="ExternalOutput").ap()
        dbg["d_cfm"] = nc.dram_tensor("d_cfm", [128, 1024], BF16, kind="ExternalOutput").ap()
        dbg["d_vs"] = nc.dram_tensor("d_vs", [128, 2112], BF16, kind="ExternalOutput").ap()
        dbg["d_pts"] = nc.dram_tensor("d_pts", [128, 512], BF16, kind="ExternalOutput").ap()
        dbg["d_clsrows"] = nc.dram_tensor("d_clsrows", [16, 1040], F32, kind="ExternalOutput").ap()
        dbg["d_lb"] = nc.dram_tensor("d_lb", [16, 16], F32, kind="ExternalOutput").ap()
        dbg["d_outcls"] = nc.dram_tensor("d_outcls", [1, E], F32, kind="ExternalOutput").ap()
        dbg["d_cfm0"] = nc.dram_tensor("d_cfm0", [128, 8], BF16, kind="ExternalOutput").ap()
        dbg["d_scr0"] = nc.dram_tensor("d_scr0", [16, 1040], BF16, kind="ExternalOutput").ap()
        dbg["d_qhat"] = nc.dram_tensor("d_qhat", [128, 128], BF16, kind="ExternalOutput").ap()
        dbg["d_w3"] = nc.dram_tensor("d_w3", [128, 8192], BF16, kind="ExternalOutput").ap()
        dbg["d_st0"] = nc.dram_tensor("d_st0", [128, 11], F32, kind="ExternalOutput").ap()
        dbg["d_st1"] = nc.dram_tensor("d_st1", [128, 11], F32, kind="ExternalOutput").ap()
        dbg["d_wbfol"] = nc.dram_tensor("d_wbfol", [128, 8192], BF16, kind="ExternalOutput").ap()
        dbg["d_t0pts"] = nc.dram_tensor("d_t0pts", [128, 1024], BF16, kind="ExternalOutput").ap()
        dbg["d_t0ctm"] = nc.dram_tensor("d_t0ctm", [128, 1024], BF16, kind="ExternalOutput").ap()
        dbg["d_t0vs"] = nc.dram_tensor("d_t0vs", [128, 2112], BF16, kind="ExternalOutput").ap()

    es = contextlib.ExitStack()
    with es:
        ec = es.enter_context
        xbuf = [ec(nc.sbuf_tensor(f"xbuf{i}", [128, E], F32)) for i in range(2)]
        fstream = ec(nc.sbuf_tensor("fstream", [128, 4096], F32))
        evac1 = ec(nc.sbuf_tensor("evac1", [128, E], F32))
        gbuf = ec(nc.sbuf_tensor("gbuf", [128, 1024], F32))
        stats = ec(nc.sbuf_tensor("stats", [128, 44], F32))
        pstats = ec(nc.sbuf_tensor("pstats", [128, 24], F32))
        rs1g_all = ec(nc.sbuf_tensor("rs1g_all", [128, NT], F32))
        cc_all = ec(nc.sbuf_tensor("cc_all", [128, NT], F32))
        rs1_all = ec(nc.sbuf_tensor("rs1_all", [128, NT], F32))
        sdown_all = ec(nc.sbuf_tensor("sdown_all", [128, NT], F32))
        wsc = ec(nc.sbuf_tensor("wsc", [128, 12], F32))
        consts = ec(nc.sbuf_tensor("consts", [128, 8], F32))
        bo_bc = ec(nc.sbuf_tensor("bo_bc", [128, E], F32))
        ident = ec(nc.sbuf_tensor("ident", [128, 128], BF16))
        bfm = ec(nc.sbuf_tensor("bfm", [128, 48], BF16))
        brow = ec(nc.sbuf_tensor("brow", [2, E], F32))
        masks = ec(nc.sbuf_tensor("masks", [128, 768], BF16))
        zrow = ec(nc.sbuf_tensor("zrow", [128, 112], BF16))
        padt = ec(nc.sbuf_tensor("padt", [128, 256], F32))
        p0stage = ec(nc.sbuf_tensor("p0stage", [128, 4096], BF16))
        cshift = ec(nc.sbuf_tensor("cshift", [128, 18], F32))
        rowred = ec(nc.sbuf_tensor("rowred", [2, 256], F32))
        lbuf = ec(nc.sbuf_tensor("lbuf", [128, 16], F32))
        clsrows = ec(nc.sbuf_tensor("clsrows", [16, 1040], F32))
        qrow = ec(nc.sbuf_tensor("qrow", [1, E], BF16))
        outcls = ec(nc.sbuf_tensor("outcls", [1, E], F32))
        WARENA = 73728
        warena = ec(nc.sbuf_tensor("warena", [128, WARENA], BF16))

        SL = Ctr(ec(nc.semaphore("s_ld")))
        ST = Ctr(ec(nc.semaphore("s_st")))
        SP = Ctr(ec(nc.semaphore("s_pe")))
        SV = Ctr(ec(nc.semaphore("s_dve")))
        SA = Ctr(ec(nc.semaphore("s_act")))
        SO = Ctr(ec(nc.semaphore("s_pool")))
        SX = Ctr(ec(nc.semaphore("s_xld")))
        SH = Ctr(ec(nc.semaphore("s_hst")))

        pw0 = ec(nc.psum_tensor("pw0", [128, 1024], F32))
        pw1 = ec(nc.psum_tensor("pw1", [128, 1024], F32))
        px = ec(nc.psum_tensor("px", [128, 1024], F32))
        py = ec(nc.psum_tensor("py", [128, 512], F32))
        ptp = ec(nc.psum_tensor("ptp", [128, 256], BF16))
        psums = [pw0, pw1, px, py]

        Q = {"sync": [], "vector": [], "scalar": [], "tensor": [], "gpsimd": []}

        def on(eng, fn):
            Q[eng].append(fn)

        def wAP(col, pbase=0, dims=None):
            return bass.AP(warena, pbase * WARENA + col, dims)

        D = {}

        def init_phase():
            base0 = SO.n

            def gp(g, base0=base0):
                k = [base0]

                def step(instr):
                    instr.then_inc(SO.h, 1)
                    k[0] += 1
                    g.wait_ge(SO.h, k[0])
                step(g.memset(consts[:, 0:1], 1e-5))
                step(g.memset(consts[:, 1:2], math.log(127.0)))
                step(g.memset(consts[:, 2:3], -math.log(1e-5)))
                step(g.memset(consts[:, 3:4], math.log(127.0) - math.log(1e-5)))
                step(g.memset(consts[:, 5:6], -(math.log(127.0) - math.log(1e-5))))
                step(g.memset(consts[:, 4:5], 1.0))
                step(g.memset(ident[:, :], 0.0))
                step(g.affine_select(ident[:, :], ident[:, :], compare_op=ALU.not_equal,
                                     fill=1.0, base=0, pattern=[[-1, 128]], channel_multiplier=1))
                m = masks
                step(g.memset(m[:, 0:768], 1.0))
                for bc in (0, 256, 512):
                    step(g.affine_select(m[:, bc:bc + 128], m[:, bc:bc + 128], compare_op=ALU.is_ge,
                                         fill=0.0, base=0, pattern=[[-1, 128]], channel_multiplier=1))
                    step(g.affine_select(m[:, bc:bc + 128], m[:, bc:bc + 128], compare_op=ALU.is_ge,
                                         fill=0.0, base=32, pattern=[[1, 128]], channel_multiplier=-1))
                    bb = bc + 128
                    step(g.affine_select(m[:, bb:bb + 128], m[:, bb:bb + 128], compare_op=ALU.is_ge,
                                         fill=0.0, base=-96, pattern=[[1, 128]], channel_multiplier=-1))
                    step(g.affine_select(m[:, bb:bb + 128], m[:, bb:bb + 128], compare_op=ALU.is_ge,
                                         fill=0.0, base=128, pattern=[[-1, 128]], channel_multiplier=1))
                step(g.affine_select(m[:, 256:384], m[:, 256:384], compare_op=ALU.is_ge,
                                     fill=0.0, base=-17, pattern=[[0, 128]], channel_multiplier=1))
                step(g.affine_select(m[:, 640:768], m[:, 640:768], compare_op=ALU.is_ge,
                                     fill=0.0, base=15, pattern=[[0, 128]], channel_multiplier=-1))
                step(g.memset(zrow[:, :], 0.0))
                step(g.memset(padt[:, :], 0.0))
            SO.bump(25)
            D["init_pool"] = SO.n
            on("gpsimd", gp)
            ld = SL.n + 48

            def sy(s):
                s.dma_start(bo_bc[:, :], bass.AP(bol.tensor, 0, [[0, 128], [1, E]])).then_inc(SL.h, 16)
                s.dma_start(brow[0:1, :], bass.AP(boc.tensor, 0, [[E, 1], [1, E]])).then_inc(SL.h, 16)
                s.dma_start(brow[1:2, :], bass.AP(bol.tensor, 0, [[E, 1], [1, E]])).then_inc(SL.h, 16)
            SL.bump(48)
            D["init_ld"] = ld
            on("sync", sy)

        init_phase()

        def prep_phase():
            ld0 = SL.n + 32

            def sy1(s):
                s.dma_start(fstream[0:1, 0:3072], bass.AP(bqkv_c.tensor, 0, [[3072, 1], [1, 3072]])).then_inc(SL.h, 16)
                s.dma_start(fstream[1:2, 0:3072], bass.AP(bqkv_l.tensor, 0, [[3072, 1], [1, 3072]])).then_inc(SL.h, 16)
            SL.bump(32)
            on("sync", sy1)
            v_b = SV.bump()
            on("vector", lambda v, ld0=ld0: (
                v.wait_ge(SL.h, ld0),
                v.tensor_copy(warena[0:2, 0:3072], fstream[0:2, 0:3072]).then_inc(SV.h, 1)))
            st_b = ST.n + 32

            def sc1(sc, v_b=v_b):
                sc.wait_ge(SV.h, v_b)
                sc.dma_start(bass.AP(bbf_sc.tensor, 0, [[3072, 1], [1, 3072]]), warena[0:1, 0:3072]).then_inc(ST.h, 16)
                sc.dma_start(bass.AP(bbf_sc.tensor, 3072, [[3072, 1], [1, 3072]]), warena[1:2, 0:3072]).then_inc(ST.h, 16)
            ST.bump(32)
            on("scalar", sc1)

            def sy2(s, st_b=st_b):
                s.wait_ge(ST.h, st_b)
                s.dma_start(bfm[:, 0:48], bass.AP(bbf_sc.tensor, 0, [[128, 48], [1, 128]]),
                            transpose=True).then_inc(SL.h, 16)
            SL.bump(16)
            on("sync", sy2)
            D["bfm_ld"] = SL.n



            pa_state = {0: v_b, 1: v_b}
            pcnt = 0
            for wi, wsrc in enumerate([wg, wu, wdn]):
                on("vector", lambda v, wi=wi: v.memset(stats[:, 33 + wi:34 + wi], 0.0))
                for tix in range(16):
                    bufp = pcnt % 2
                    pcnt += 1
                    f0 = 2048 * bufp
                    prev_v = pa_state[bufp]
                    ldt = SL.n + 16

                    def syl(s, wsrc=wsrc, tix=tix, prev_v=prev_v, f0=f0):
                        s.wait_ge(SV.h, prev_v)
                        s.dma_start(fstream[:, f0:f0 + 2048],
                                    wsrc[:, 2048 * tix:2048 * (tix + 1)]).then_inc(SL.h, 16)
                    SL.bump(16)
                    on("sync", syl)
                    red_t = SV.bump()
                    SV.bump()

                    def vel(v, ldt=ldt, wi=wi, red_t=red_t, f0=f0):
                        v.wait_ge(SL.h, ldt)
                        v.reduce_sum(stats[:, 36:37], fstream[:, f0:f0 + 2048], axis=AX.X,
                                     apply_absolute_value=True).then_inc(SV.h, 1)
                        v.wait_ge(SV.h, red_t)
                        v.tensor_tensor(stats[:, 33 + wi:34 + wi], stats[:, 33 + wi:34 + wi],
                                        stats[:, 36:37], op=ALU.add).then_inc(SV.h, 1)
                    on("vector", vel)
                    pa_state[bufp] = SV.n
            v_sums = SV.n

            for wi in range(3):
                stA = ST.n + 16

                def sc_a(sc, wi=wi, v_sums=v_sums):
                    sc.wait_ge(SV.h, v_sums)
                    sc.dma_start(csum_sc[0:128], stats[:, 33 + wi:34 + wi]).then_inc(ST.h, 16)
                ST.bump(16)
                on("scalar", sc_a)
                stB = ST.n + 16

                def sc_b(sc, stA=stA):
                    sc.wait_ge(ST.h, stA)
                    sc.dma_start(rowred[0:1, 0:128],
                                 bass.AP(csum_sc.tensor, 0, [[128, 1], [1, 128]])).then_inc(ST.h, 16)
                ST.bump(16)
                on("scalar", sc_b)
                r_t = SV.bump()
                fin_t = SV.bump()

                def ve_r(v, stB=stB, wi=wi, r_t=r_t):
                    v.wait_ge(ST.h, stB)
                    v.reduce_sum(rowred[0:1, 128:129], rowred[0:1, 0:128], axis=AX.X).then_inc(SV.h, 1)
                    v.wait_ge(SV.h, r_t)
                    v.tensor_copy(padt[:, 128:192], padt[:, 192:256])
                    v.tensor_scalar(rowred[0:1, 140 + wi:141 + wi], rowred[0:1, 128:129],
                                    scalar1=1.0 / (4096.0 * 1024.0), scalar2=1e-5,
                                    op0=ALU.mult, op1=ALU.max)
                    v.tensor_copy(padt[:, 128:192], padt[:, 192:256]).then_inc(SV.h, 1)
                on("vector", ve_r)
                nxt = SV.n
                on("scalar", lambda sc, nxt=nxt: sc.wait_ge(SV.h, nxt))
            v_scales = SV.n
            stC = ST.n + 16

            def sc_c(sc, v_scales=v_scales):
                sc.wait_ge(SV.h, v_scales)
                sc.dma_start(one_sc[0:3], rowred[0:1, 140:143]).then_inc(ST.h, 16)
            ST.bump(16)
            on("scalar", sc_c)
            stD = ST.n + 48

            def sc_d(sc, stC=stC):
                sc.wait_ge(ST.h, stC)
                for wi in range(3):
                    sc.dma_start(wsc[:, 2 * wi + 1:2 * wi + 2],
                                 bass.AP(one_sc.tensor, wi, [[0, 128], [1, 1]])).then_inc(ST.h, 16)
            ST.bump(48)
            on("scalar", sc_d)
            sw_t = SA.bump()

            def ac_sw(a, stD=stD):
                a.wait_ge(ST.h, stD)
                for wi in range(3):
                    a.activation(wsc[:, 8 + wi:9 + wi], wsc[:, 2 * wi + 1:2 * wi + 2], AF.Ln)
                for wi in range(3):
                    a.activation(wsc[:, 2 * wi:2 * wi + 1], wsc[:, 8 + wi:9 + wi], AF.Exp, scale=-1.0)
                a.activation(padt[:, 0:64], padt[:, 64:128], AF.Identity).then_inc(SA.h, 1)
            on("scalar", ac_sw)
            D["sw_t"] = sw_t

            v_pa = SV.n
            pb_state = {0: (0, v_pa), 1: (0, v_pa)}
            pcnt = 0
            for wi, (wsrc, dst) in enumerate([(wg, None), (wu, None), (wdn, wdq_sc)]):
                for tix in range(16):
                    bufp = pcnt % 2
                    pcnt += 1
                    f0 = 2048 * bufp
                    w0 = (32768 * wi + 2048 * tix) if dst is None else (65536 + 2048 * bufp)
                    prev_st, prev_q = pb_state[bufp]
                    st_full = ST.n
                    ldt = SL.n + 16

                    def syl(s, wsrc=wsrc, tix=tix, prev_q=prev_q, f0=f0):
                        s.wait_ge(SV.h, prev_q)
                        s.dma_start(fstream[:, f0:f0 + 2048],
                                    wsrc[:, 2048 * tix:2048 * (tix + 1)]).then_inc(SL.h, 16)
                    SL.bump(16)
                    on("sync", syl)
                    q_t = SV.bump()

                    def vel(v, ldt=ldt, wi=wi, sw_t=sw_t, f0=f0, w0=w0, stw=st_full):
                        v.wait_ge(SL.h, ldt)
                        v.wait_ge(SA.h, sw_t)
                        v.wait_ge(ST.h, stw)
                        v.tensor_scalar(fstream[:, f0:f0 + 2048], fstream[:, f0:f0 + 2048],
                                        scalar1=wsc[:, 2 * wi:2 * wi + 1], scalar2=RC,
                                        op0=ALU.mult, op1=ALU.add)
                        v.tensor_scalar(fstream[:, f0:f0 + 2048], fstream[:, f0:f0 + 2048],
                                        scalar1=RC, scalar2=None, op0=ALU.subtract)
                        v.tensor_scalar(warena[:, w0:w0 + 2048], fstream[:, f0:f0 + 2048],
                                        scalar1=1.0, scalar2=-1.0, op0=ALU.min, op1=ALU.max).then_inc(SV.h, 1)
                    on("vector", vel)

                    if dst is not None:
                        def scl(sc, dst=dst, tix=tix, q_t=q_t, w0=w0):
                            sc.wait_ge(SV.h, q_t)
                            sc.dma_start(dst[:, 2048 * tix:2048 * (tix + 1)],
                                         warena[:, w0:w0 + 2048]).then_inc(ST.h, 16)
                        ST.bump(16)
                        on("scalar", scl)
                        pb_state[bufp] = (ST.n, q_t)
                    else:
                        pb_state[bufp] = (prev_st, q_t)
                    if wi == 1 and tix == 15:
                        D["wq_v"] = q_t
            D["wq_st"] = ST.n
            pb_v = SV.n

            D["pb_v"] = SV.n

        prep_phase()

        def p0_f1_phase():
            WG0, WU0 = 0, 32768
            AI = [65536, 69632]
            SUPER = 4
            p0_mark = {}
            p0_stnow = {}
            p0v = {}
            p0_aq = {}
            ai_free = {0: 0, 1: 0}
            ev_par = {0: (0, 0), 1: (0, 0)}
            st = {"hq_free": 0}
            lded = set()

            def emit_p0_load(j):
                if j in lded or j >= NT:
                    return
                lded.add(j)
                buf = j % 2

                def gyl(g, j=j, buf=buf):
                    pf = p0_mark.get(j - 2, (0, 0))
                    g.wait_ge(SV.h, pf[0])
                    g.wait_ge(SA.h, pf[1])
                    g.dma_start(xbuf[buf][:, :], xin[128 * j:128 * (j + 1), :]).then_inc(SX.h, 16)
                SX.bump(16)
                on("gpsimd", gyl)

            def emit_p0_tile(j):
                buf = j % 2
                sb0 = 11 * (j % 2)
                emit_p0_load(j)
                sum_t = SV.bump()
                ld_full = 16 * (j + 1)
                prev_aq = p0_aq.get(j - 1, 0)
                on("vector", lambda v, ld_full=ld_full, buf=buf, sb0=sb0, prev_aq=prev_aq: (
                    v.wait_ge(SX.h, ld_full),
                    v.wait_ge(SA.h, prev_aq),
                    v.reduce_sum(pstats[:, sb0 + 0:sb0 + 1], xbuf[buf][:, :], axis=AX.X).then_inc(SV.h, 1)))
                m_t = SV.bump()

                def ve_b(v, sum_t=sum_t, buf=buf, sb0=sb0):
                    v.wait_ge(SV.h, sum_t)
                    v.tensor_scalar(pstats[:, sb0 + 2:sb0 + 3], pstats[:, sb0 + 0:sb0 + 1],
                                    scalar1=1.0 / 1024, scalar2=None, op0=ALU.mult)
                    v.tensor_copy(padt[:, 128:192], padt[:, 192:256])
                    v.tensor_scalar(pstats[:, sb0 + 3:sb0 + 4], pstats[:, sb0 + 2:sb0 + 3],
                                    scalar1=pstats[:, sb0 + 2:sb0 + 3], scalar2=-1.0,
                                    op0=ALU.mult, op1=ALU.mult)
                    v.tensor_scalar(evac1[:, :], xbuf[buf][:, :], scalar1=pstats[:, sb0 + 2:sb0 + 3],
                                    scalar2=None, op0=ALU.subtract)
                    v.reduce_max(pstats[:, sb0 + 7:sb0 + 8], evac1[:, :], axis=AX.X,
                                 apply_absolute_value=True)
                    v.tensor_copy(padt[:, 128:192], padt[:, 192:256]).then_inc(SV.h, 1)
                on("vector", ve_b)
                a_st = SA.bump()

                def ac_a(a, j=j, buf=buf, m_t=m_t, sb0=sb0, prev_done=D.get("p0_done_prev", 0)):
                    a.wait_ge(SV.h, max(m_t, prev_done))
                    a.activation(evac1[:, :], xbuf[buf][:, :], AF.Square, accum_out=pstats[:, sb0 + 1:sb0 + 2])
                    a.activation(padt[:, 0:32], padt[:, 64:96], AF.Identity)
                    a.activation(pstats[:, sb0 + 4:sb0 + 5], pstats[:, sb0 + 1:sb0 + 2], AF.Identity,
                                 bias=pstats[:, sb0 + 3:sb0 + 4], scale=1.0 / 1024)
                    a.activation(padt[:, 0:32], padt[:, 64:96], AF.Identity)
                    a.activation(pstats[:, sb0 + 5:sb0 + 6], pstats[:, sb0 + 4:sb0 + 5], AF.Ln, bias=consts[:, 0:1])
                    a.activation(padt[:, 0:32], padt[:, 64:96], AF.Identity)
                    a.activation(pstats[:, sb0 + 6:sb0 + 7], pstats[:, sb0 + 5:sb0 + 6], AF.Exp, scale=-0.5)
                    a.activation(pstats[:, sb0 + 8:sb0 + 9], pstats[:, sb0 + 7:sb0 + 8], AF.Ln)
                    a.activation(padt[:, 0:32], padt[:, 64:96], AF.Identity)
                    a.activation(pstats[:, sb0 + 9:sb0 + 10], pstats[:, sb0 + 5:sb0 + 6], AF.Identity,
                                 scale=-0.5, bias=pstats[:, sb0 + 8:sb0 + 9])
                    a.activation(padt[:, 0:32], padt[:, 64:96], AF.Identity)
                    a.activation(pstats[:, sb0 + 9:sb0 + 10], pstats[:, sb0 + 9:sb0 + 10], AF.Relu,
                                 bias=consts[:, 2:3])
                    a.activation(padt[:, 0:32], padt[:, 64:96], AF.Identity)
                    a.activation(pstats[:, sb0 + 10:sb0 + 11], pstats[:, sb0 + 9:sb0 + 10], AF.Exp,
                                 scale=-1.0, bias=consts[:, 3:4])
                    a.activation(rs1_all[:, j:j + 1], pstats[:, sb0 + 9:sb0 + 10], AF.Exp,
                                 bias=consts[:, 5:6])
                    a.activation(padt[:, 0:64], padt[:, 64:128], AF.Identity).then_inc(SA.h, 1)
                on("scalar", ac_a)
                xn_t = SV.n + 2
                SV.bump(2)

                def ve_c(v, buf=buf, a_st=a_st, sb0=sb0):
                    v.wait_ge(SA.h, a_st)
                    v.tensor_scalar(evac1[:, :], xbuf[buf][:, :], scalar1=pstats[:, sb0 + 2:sb0 + 3],
                                    scalar2=pstats[:, sb0 + 6:sb0 + 7], op0=ALU.subtract,
                                    op1=ALU.mult).then_inc(SV.h, 2)
                on("vector", ve_c)
                p0_mark[j] = (xn_t, a_st)
                a_q = a_st
                p0_aq[j] = a_q
                if DEBUG and j in (0, 1):
                    def sc_dst(sc, j=j, sb0=sb0, a_q_=SA.n):
                        sc.wait_ge(SA.h, a_q_)
                        sc.dma_start(dbg["d_st%d" % j], pstats[:, sb0:sb0 + 11]).then_inc(ST.h, 16)
                    ST.bump(16)
                    on("scalar", sc_dst)
                done_t = SV.n + 2
                SV.bump(2)

                ps0 = 2048 * (j % 2)

                def ve_e(v, a_q=a_q, j=j, sb0=sb0, ps0=ps0, prev_st=p0_stnow.get(j - 2, 0)):
                    v.wait_ge(SA.h, a_q)
                    v.wait_ge(ST.h, prev_st)
                    v.tensor_copy(p0stage[:, ps0:ps0 + 1024], evac1[:, :])
                    v.tensor_scalar(evac1[:, :], evac1[:, :], scalar1=pstats[:, sb0 + 10:sb0 + 11],
                                    scalar2=RC, op0=ALU.mult, op1=ALU.add)
                    v.tensor_scalar(p0stage[:, ps0 + 1024:ps0 + 2048], evac1[:, :], scalar1=RC,
                                    scalar2=None, op0=ALU.subtract).then_inc(SV.h, 2)
                on("vector", ve_e)
                D["p0_done_prev"] = done_t

                def scl(sc, j=j, done_t=done_t, ps0=ps0):
                    sc.wait_ge(SV.h, done_t)
                    sc.dma_start(xn_sc[128 * j:128 * (j + 1), :], p0stage[:, ps0:ps0 + 1024]).then_inc(ST.h, 16)
                    sc.dma_start(ai_sc[128 * j:128 * (j + 1), :], p0stage[:, ps0 + 1024:ps0 + 2048]).then_inc(ST.h, 16)
                ST.bump(32)
                st_now = ST.n
                p0_stnow[j] = st_now
                on("scalar", scl)
                if j % 4 == 3:
                    mark = SV.bump()

                    def ve_mark(v, st_now=st_now):
                        v.wait_ge(ST.h, st_now)
                        v.tensor_copy(padt[:, 128:192], padt[:, 192:256]).then_inc(SV.h, 1)
                    on("vector", ve_mark)
                    p0v[j] = mark

            def emit_f1_super(sb):
                abuf = AI[sb % 2]
                rsg_sb = SV.bump()

                def ve_rs4(v, sb=sb, aq=p0_aq[4 * sb + 3]):
                    v.wait_ge(SA.h, aq)
                    v.tensor_scalar(rs1g_all[:, 4 * sb:4 * sb + 4], rs1_all[:, 4 * sb:4 * sb + 4],
                                    scalar1=wsc[:, 1:2], scalar2=None, op0=ALU.mult)
                    v.tensor_scalar(cc_all[:, 4 * sb:4 * sb + 4], rs1_all[:, 4 * sb:4 * sb + 4],
                                    scalar1=wsc[:, 3:4], scalar2=None, op0=ALU.mult)
                    v.tensor_copy(padt[:, 128:192], padt[:, 192:256]).then_inc(SV.h, 1)
                on("vector", ve_rs4)

                def syl(s, sb=sb, abuf=abuf, pf=ai_free[sb % 2], pv=p0v[4 * sb + 3]):
                    s.wait_ge(SV.h, pv)
                    s.wait_ge(SP.h, pf)
                    for c in range(8):
                        s.dma_start(warena[:, abuf + 512 * c:abuf + 512 * (c + 1)],
                                    bass.AP(ai_sc.tensor, (512 * sb) * E + 128 * c,
                                            [[E, 512], [1, 128]]), transpose=True).then_inc(SL.h, 16)
                SL.bump(128)
                ai_t = SL.n
                on("sync", syl)
                for jj in range(SUPER):
                    j = sb * SUPER + jj
                    for grp in range(8):
                        par = grp % 2
                        evd = ev_par[par]
                        g_t = SP.bump()
                        u_t = SP.bump()

                        def te(t, jj=jj, grp=grp, par=par, abuf=abuf, evd=evd, ai_t=ai_t, wq_v=D["wq_v"]):
                            t.wait_ge(SL.h, ai_t)
                            t.wait_ge(SV.h, max(wq_v, evd[1]))
                            t.wait_ge(SA.h, evd[0])
                            n0 = 512 * grp
                            for dstp, w0 in ((pw0, WG0), (pw1, WU0)):
                                last = None
                                for c in range(8):
                                    last = t.matmul(dstp[:, 512 * par:512 * (par + 1)],
                                                    wAP(abuf + 512 * c + 128 * jj, dims=[[WARENA, 128], [1, 128]]),
                                                    warena[:, w0 + 4096 * c + n0:w0 + 4096 * c + n0 + 512],
                                                    start=(c == 0), stop=(c == 7))
                                last.then_inc(SP.h, 1)
                        on("tensor", te)
                        a_silu = SA.bump()

                        def ac_s(a, g_t=g_t, j=j, par=par, rsg_sb=rsg_sb, evd=evd):
                            a.wait_ge(SP.h, g_t)
                            a.wait_ge(SV.h, max(rsg_sb, evd[1]))
                            a.activation(gbuf[:, 512 * par:512 * (par + 1)],
                                         pw0[:, 512 * par:512 * (par + 1)], AF.Silu,
                                         scale=rs1g_all[:, j:j + 1]).then_inc(SA.h, 1)
                        on("scalar", ac_s)
                        h_t = SV.bump()

                        def ve_h(v, u_t=u_t, a_silu=a_silu, grp=grp, par=par, hq_free=st["hq_free"]):
                            v.wait_ge(SP.h, u_t)
                            v.wait_ge(SA.h, a_silu)
                            if grp == 0:
                                v.wait_ge(SH.h, hq_free)
                            v.tensor_tensor(fstream[:, 512 * grp:512 * (grp + 1)],
                                            pw1[:, 512 * par:512 * (par + 1)],
                                            gbuf[:, 512 * par:512 * (par + 1)], op=ALU.mult).then_inc(SV.h, 1)
                        on("vector", ve_h)
                        ev_par[par] = (a_silu, h_t)
                    am_t = SV.bump()
                    on("vector", lambda v, am_t=am_t: (
                        v.reduce_max(stats[:, 38:39], fstream[:, :], axis=AX.X,
                                     apply_absolute_value=True).then_inc(SV.h, 1)))
                    mc_t = SV.bump()

                    def ve_mc(v, am_t=am_t, j=j):
                        v.wait_ge(SV.h, am_t)
                        v.tensor_scalar(stats[:, 39:40], stats[:, 38:39], scalar1=cc_all[:, j:j + 1],
                                        scalar2=1e-5, op0=ALU.mult, op1=ALU.max)
                        v.tensor_copy(padt[:, 128:192], padt[:, 192:256]).then_inc(SV.h, 1)
                    on("vector", ve_mc)
                    a_q2 = SA.bump()

                    def ac_q2(a, mc_t=mc_t, j=j):
                        a.wait_ge(SV.h, mc_t)
                        a.activation(stats[:, 40:41], stats[:, 39:40], AF.Ln)
                        a.activation(padt[:, 0:32], padt[:, 64:96], AF.Identity)
                        a.activation(stats[:, 41:42], stats[:, 40:41], AF.Exp, scale=-1.0)
                        a.activation(padt[:, 0:32], padt[:, 64:96], AF.Identity)
                        a.activation(stats[:, 42:43], stats[:, 41:42], AF.Identity,
                                     scale=cc_all[:, j:j + 1])
                        a.activation(padt[:, 0:64], padt[:, 64:128], AF.Identity).then_inc(SA.h, 1)
                    on("scalar", ac_q2)
                    hq_t = SV.n + 2
                    SV.bump(2)

                    def ve_hq(v, a_q2=a_q2, j=j):
                        v.wait_ge(SA.h, a_q2)
                        v.tensor_scalar(stats[:, 43:44], stats[:, 42:43], scalar1=127.0, scalar2=None, op0=ALU.mult)
                        v.tensor_scalar(sdown_all[:, j:j + 1], stats[:, 39:40], scalar1=wsc[:, 5:6],
                                        scalar2=1.0 / 127, op0=ALU.mult, op1=ALU.mult)
                        v.tensor_copy(padt[:, 128:192], padt[:, 192:256])
                        v.tensor_scalar(fstream[:, :], fstream[:, :], scalar1=stats[:, 43:44],
                                        scalar2=RC, op0=ALU.mult, op1=ALU.add)
                        v.tensor_scalar(fstream[:, :], fstream[:, :], scalar1=RC,
                                        scalar2=None, op0=ALU.subtract).then_inc(SV.h, 2)
                    on("vector", ve_hq)

                    def gpl(g, j=j, hq_t=hq_t):
                        g.wait_ge(SV.h, hq_t)
                        g.dma_start(hq_sc[128 * j:128 * (j + 1), :], fstream[:, :]).then_inc(SH.h, 16)
                    SH.bump(16)
                    st["hq_free"] = SH.n
                    on("gpsimd", gpl)
                ai_free[sb % 2] = SP.n

            for j in range(NT):
                emit_p0_tile(j)
                if j % 4 == 3:
                    emit_f1_super(j // 4)
            D["p0_st"] = ST.n
            D["f1_pe"] = SP.n
            D["hq_sh"] = SH.n

        p0_f1_phase()

        CVB = SX.n
        cv_mark = {}
        CV_SRC = [(nm, hv) for nm in ["wkc", "wvc", "wqc", "woc", "wql", "wkl", "wvl", "wol"]
                  for hv in range(4)]

        def emit_conv_tile(t):
            nm, hv = CV_SRC[t]
            if True:
                if True:
                    f0 = 2048 * (t % 2)

                    def gyl(g, nm=nm, hv=hv, t=t, f0=f0, hq_sh=D["hq_sh"]):
                        if t < 2:
                            g.wait_ge(SH.h, hq_sh)
                        else:
                            g.wait_ge(SV.h, cv_mark[t - 2])
                        g.dma_start(fstream[:, f0:f0 + 2048],
                                    win[nm][:, 2048 * hv:2048 * (hv + 1)]).then_inc(SX.h, 16)
                    SX.bump(16)
                    on("gpsimd", gyl)
                    st_full = ST.n
                    vvt = SV.bump()

                    def vv(v, t=t, f0=f0, st_full=st_full, CVB=CVB):
                        v.wait_ge(SX.h, CVB + 16 * (t + 1))
                        v.wait_ge(ST.h, st_full)
                        v.tensor_copy(p0stage[:, 0:2048], fstream[:, f0:f0 + 2048]).then_inc(SV.h, 1)
                    on("vector", vv)
                    cv_mark[t] = vvt

                    def scl(sc, nm=nm, hv=hv, vvt=vvt):
                        sc.wait_ge(SV.h, vvt)
                        sc.dma_start(wbf_sc[:, WOFF[nm] + 2048 * hv:WOFF[nm] + 2048 * (hv + 1)],
                                     p0stage[:, 0:2048]).then_inc(ST.h, 16)
                    ST.bump(16)
                    on("scalar", scl)
        def emit_conv_done():
            st_all = ST.n
            mk = SV.bump()

            def ve_mark(v, st_all=st_all):
                v.wait_ge(ST.h, st_all)
                v.tensor_copy(padt[:, 128:192], padt[:, 192:256]).then_inc(SV.h, 1)
            on("vector", ve_mark)
            D["wbf_v"] = mk
            D["wbf_st"] = st_all
            D["conv_v"] = SV.n

        def shifts_phase():
            for si, (bsrc, rcol) in enumerate([(bqkv_c, 144), (bqkv_l, 160)]):
                ldsh = SL.n + 32

                def sy_sh(s, bsrc=bsrc, conv_v=D["conv_v"]):
                    s.wait_ge(SV.h, conv_v)
                    s.dma_start(fstream[0:1, 2048:3072], bass.AP(bsrc.tensor, 0, [[1024, 1], [1, 1024]])).then_inc(SL.h, 16)
                    s.dma_start(fstream[0:1, 3072:4096], bass.AP(bsrc.tensor, 1024, [[1024, 1], [1, 1024]])).then_inc(SL.h, 16)
                SL.bump(32)
                on("sync", sy_sh)
                b0 = SV.n
                SV.bump(3)

                def ve_sh(v, ldsh=ldsh, rcol=rcol, b0=b0):
                    v.wait_ge(SL.h, ldsh)
                    v.tensor_tensor(fstream[0:1, 2048:3072], fstream[0:1, 2048:3072],
                                    fstream[0:1, 3072:4096], op=ALU.mult).then_inc(SV.h, 1)
                    v.wait_ge(SV.h, b0 + 1)
                    last = None
                    for h in range(16):
                        last = v.reduce_sum(rowred[0:1, rcol + h:rcol + h + 1],
                                            fstream[0:1, 2048 + 64 * h:2048 + 64 * (h + 1)], axis=AX.X)
                    last.then_inc(SV.h, 1)
                    v.wait_ge(SV.h, b0 + 2)
                    v.tensor_copy(padt[:, 128:192], padt[:, 192:256])
                    v.tensor_scalar(rowred[0:1, rcol:rcol + 16], rowred[0:1, rcol:rcol + 16],
                                    scalar1=-0.125, scalar2=-20.0, op0=ALU.mult, op1=ALU.add)
                    v.tensor_copy(padt[:, 128:192], padt[:, 192:256]).then_inc(SV.h, 1)
                on("vector", ve_sh)
            sh_v = SV.n
            stsh = ST.n + 16

            def sc_sh(sc, sh_v=sh_v):
                sc.wait_ge(SV.h, sh_v)
                sc.dma_start(csum_sc[160:192], rowred[0:1, 144:176]).then_inc(ST.h, 16)
            ST.bump(16)
            on("scalar", sc_sh)
            ldsh2 = SL.n + 32

            def sy_sh2(s, stsh=stsh):
                s.wait_ge(ST.h, stsh)
                s.dma_start(cshift[0:16, 16:17], bass.AP(csum_sc.tensor, 160, [[1, 16], [1, 1]])).then_inc(SL.h, 16)
                s.dma_start(cshift[:, 0:16], bass.AP(csum_sc.tensor, 176, [[0, 128], [1, 16]])).then_inc(SL.h, 16)
            SL.bump(32)
            on("sync", sy_sh2)
            D["sh_ld"] = SL.n


        def f2_phase():
            WD0 = 0
            HF = [32768, 49152]
            SUPER = 4
            ld_w = SL.n + 16

            def syw(s, f1_pe=D["f1_pe"]):
                s.wait_ge(SP.h, f1_pe)
                s.dma_start(warena[:, WD0:WD0 + 32768], wdq_sc[:, :]).then_inc(SL.h, 16)
            SL.bump(16)
            on("sync", syw)
            hf_free = {0: D["f1_pe"], 1: D["f1_pe"]}
            x_free = {0: 0, 1: 0}
            ev_par = {0: 0, 1: 0}
            for sb in range(NT // SUPER):
                hbuf = HF[sb % 2]

                def syl(s, sb=sb, hbuf=hbuf, pf=hf_free[sb % 2], hq_sh=D["hq_sh"]):
                    s.wait_ge(SH.h, hq_sh)
                    s.wait_ge(SP.h, pf)
                    for c in range(32):
                        s.dma_start(warena[:, hbuf + 512 * c:hbuf + 512 * (c + 1)],
                                    bass.AP(hq_sc.tensor, (512 * sb) * INNER + 128 * c,
                                            [[INNER, 512], [1, 128]]), transpose=True).then_inc(SL.h, 16)
                SL.bump(16 * 32)
                hld = SL.n
                on("sync", syl)
                for jj in range(SUPER):
                    j = sb * SUPER + jj
                    par = j % 2
                    pdn = pw0 if par == 0 else px
                    buf = j % 2
                    xld = SL.n + 16

                    def syx(s, j=j, buf=buf, xf=x_free[buf]):
                        s.wait_ge(SV.h, xf)
                        s.dma_start(xbuf[buf][:, :], xin[128 * j:128 * (j + 1), :]).then_inc(SL.h, 16)
                    SL.bump(16)
                    on("sync", syx)
                    dn_t = SP.bump()

                    def te(t, jj=jj, hbuf=hbuf, hld=hld, ld_w=ld_w, evd=ev_par[par], pdn=pdn):
                        t.wait_ge(SL.h, max(hld, ld_w))
                        t.wait_ge(SV.h, evd)
                        last = None
                        for half in range(2):
                            n0 = 512 * half
                            for c in range(32):
                                last = t.matmul(pdn[:, n0:n0 + 512],
                                                wAP(hbuf + 512 * c + 128 * jj, dims=[[WARENA, 128], [1, 128]]),
                                                warena[:, WD0 + 1024 * c + n0:WD0 + 1024 * c + n0 + 512],
                                                start=(c == 0), stop=(c == 31))
                        last.then_inc(SP.h, 1)
                    on("tensor", te)
                    out_t = SV.bump()

                    def ve_o(v, dn_t=dn_t, xld=xld, j=j, buf=buf, pdn=pdn):
                        v.wait_ge(SP.h, dn_t)
                        v.wait_ge(SL.h, xld)
                        v.tensor_scalar(evac1[:, :], pdn[:, :], scalar1=sdown_all[:, j:j + 1],
                                        scalar2=None, op0=ALU.mult)
                        v.tensor_tensor(evac1[:, :], evac1[:, :], xbuf[buf][:, :], op=ALU.add)
                        v.tensor_tensor(evac1[:, :], evac1[:, :], bo_bc[:, :], op=ALU.add).then_inc(SV.h, 1)
                    on("vector", ve_o)
                    ev_par[par] = out_t
                    x_free[buf] = out_t

                    def scl(sc, j=j, out_t=out_t):
                        sc.wait_ge(SV.h, out_t)
                        sc.dma_start(part_sc[128 * j:128 * (j + 1), :], evac1[:, :]).then_inc(ST.h, 16)
                    ST.bump(16)
                    st_now = ST.n
                    on("scalar", scl)
                    on("vector", lambda v, st_now=st_now: v.wait_ge(ST.h, st_now))
                    emit_conv_tile(j)
                hf_free[sb % 2] = SP.n
            D["part_st"] = ST.n
            D["f2_pe"] = SP.n
            emit_conv_done()

        f2_phase()
        shifts_phase()

        def attn_phase():
            def gones(g, wbf_st=D["wbf_st"]):
                g.wait_ge(ST.h, wbf_st)
                g.memset(p0stage[0:1, 0:2176], 1.0).then_inc(SO.h, 1)
            D["ones_rdy"] = SO.bump()
            on("gpsimd", gones)
            A_W1, A_W2, A_W3, A_W4 = 0, 8192, 16384, 24576
            A_XNF = 32768
            A_QF = 41984
            A_KF = 50176
            A_VS = 59392
            A_KCB = 61504
            A_PT = 63552
            A_CTM = 63808
            A_CFM = 64832
            A_QHAT = 65856
            A_SCR = 66048
            A_ONES = 68096
            A_PCLS = 70272

            def pclsAP(col, n):
                return bass.AP(warena, A_PCLS + col, [[WARENA, 16], [1, n]])

            def wslot_load(slot, nm, after_pe=None):
                def syl(s, slot=slot, nm=nm, after_pe=after_pe, wbf_v=D["wbf_v"]):
                    s.wait_ge(SV.h, wbf_v)
                    if after_pe is not None:
                        s.wait_ge(SP.h, after_pe)
                    s.dma_start(warena[:, slot:slot + 8192],
                                wbf_sc[:, WOFF[nm]:WOFF[nm] + 8192]).then_inc(SL.h, 16)
                SL.bump(16)
                on("sync", syl)
                return SL.n

            for seq in range(NSEQ):
                tb = seq * S
                seq_barrier = SP.n
                seq_sv = SV.n
                seq_sa = SA.n
                seq_st = ST.n
                on("tensor", lambda t, sv=seq_sv, sa=seq_sa: (t.wait_ge(SV.h, sv), t.wait_ge(SA.h, sa)))
                ld_vl = wslot_load(A_W1, "wvl", after_pe=seq_barrier)
                ld_vc = wslot_load(A_W2, "wvc", after_pe=seq_barrier)
                vbase_l = seq * 1040 * 2176
                vbase_c = seq * 1040 * 2048

                def sc_pad(sc, vbase_l=vbase_l, vbase_c=vbase_c, ip=D["init_pool"]):
                    sc.wait_ge(SO.h, max(ip, D["ones_rdy"]))
                    for r in range(9):
                        rr = min(128 * r, 1040 - 128)
                        sc.dma_start(bass.AP(vloc_sc.tensor, vbase_l + rr * 2176, [[2176, 128], [1, 16]]),
                                     zrow[:, 0:16]).then_inc(ST.h, 16)
                        sc.dma_start(bass.AP(vloc_sc.tensor, vbase_l + rr * 2176 + 2064, [[2176, 128], [1, 112]]),
                                     zrow[:, 0:112]).then_inc(ST.h, 16)
                    for hh in range(16):
                        sc.dma_start(bass.AP(vloc_sc.tensor, vbase_l + (65 * hh + 64) * 2176, [[2176, 1], [1, 2176]]),
                                     p0stage[0:1, 0:2176]).then_inc(ST.h, 16)
                        sc.dma_start(bass.AP(vcls_sc.tensor, vbase_c + (65 * hh + 64) * 2048, [[2048, 1], [1, 2048]]),
                                     p0stage[0:1, 0:2048]).then_inc(ST.h, 16)
                ST.bump(16 * 50)
                on("scalar", sc_pad)

                for vh in range(2):
                    hs = vh * 1024
                    xnf_prev = SP.n

                    def syx(s, hs=hs, xnf_prev=xnf_prev, p0_st=D["p0_st"], tb=tb):
                        s.wait_ge(ST.h, p0_st)
                        s.wait_ge(SP.h, xnf_prev)
                        for c in range(8):
                            s.dma_start(warena[:, A_XNF + 1152 * c:A_XNF + 1152 * c + 1024],
                                        bass.AP(xn_sc.tensor, (tb + hs) * E + 128 * c,
                                                [[E, 1024], [1, 128]]), transpose=True).then_inc(SL.h, 16)
                    SL.bump(128)
                    xnf_t = SL.n
                    on("sync", syx)
                    ci_par = 0
                    for wslot, bcol, dst_sc, dcols, wld in [(A_W1, 40, vloc_sc, 2176, ld_vl),
                                                           (A_W2, 16, vcls_sc, 2048, ld_vc)]:
                        for c in range(8):
                            par = ci_par % 2
                            ci_par += 1
                            pp = px if par == 0 else pw0
                            prev_ev = SA.n
                            pj = SP.bump()

                            def te(t, wslot=wslot, c=c, xnf_t=xnf_t, wld=wld, prev_ev=prev_ev, pp=pp):
                                t.wait_ge(SL.h, max(xnf_t, wld))
                                t.wait_ge(SA.h, prev_ev)
                                last = None
                                for half in range(2):
                                    n0 = 512 * half
                                    for ci in range(8):
                                        last = t.matmul(pp[:, n0:n0 + 512],
                                                        warena[:, wslot + 1024 * ci + 128 * c:wslot + 1024 * ci + 128 * (c + 1)],
                                                        warena[:, A_XNF + 1152 * ci + n0:A_XNF + 1152 * ci + n0 + 512],
                                                        start=(ci == 0), stop=(ci == 7))
                                last.then_inc(SP.h, 1)
                            on("tensor", te)
                            prev_st2 = ST.n
                            ev_t = SA.bump()

                            def ac_ev(a, pj=pj, bcol=bcol, c=c, bfm_ld=D["bfm_ld"], pp=pp, prev_st2=prev_st2):
                                a.wait_ge(SP.h, pj)
                                a.wait_ge(SL.h, bfm_ld)
                                a.wait_ge(ST.h, prev_st2)
                                a.activation(warena[:, A_SCR:A_SCR + 1024], pp[:, 0:1024], AF.Identity,
                                             bias=bfm[:, bcol + c:bcol + c + 1]).then_inc(SA.h, 1)
                            on("scalar", ac_ev)

                            def sc_st(sc, ev_t=ev_t, c=c, dst_sc=dst_sc, dcols=dcols, hs=hs, seq=seq):
                                sc.wait_ge(SA.h, ev_t)
                                coff = seq * 1040 * dcols + (16 if dcols == 2176 else 0) + hs
                                for g in range(2):
                                    sc.dma_start(bass.AP(dst_sc.tensor, (130 * c + 65 * g) * dcols + coff,
                                                         [[dcols, 64], [1, 1024]]),
                                                 bass.AP(warena, (64 * g) * WARENA + A_SCR,
                                                         [[WARENA, 64], [1, 1024]])).then_inc(ST.h, 16)
                            ST.bump(32)
                            on("scalar", sc_st)
                a0_st = ST.n
                a0_pe = SP.n

                ld_kc = wslot_load(A_W1, "wkc", after_pe=a0_pe)
                ld_ql = wslot_load(A_W2, "wql", after_pe=a0_pe)
                ld_kl = wslot_load(A_W4, "wkl", after_pe=seq_barrier)
                ld_qc = wslot_load(A_W3, "wqc", after_pe=seq_barrier)
                wol_ld = [None]
                qh_done = [0]
                pfree = {0: 0, 1: 0}
                vs_tiles = {}
                ctxev = {0: 0, 1: 0}
                for hf in range(2):
                    hs = hf * 1024
                    xnf_prev = SP.n

                    def syx(s, hf=hf, hs=hs, xnf_prev=xnf_prev, tb=tb):
                        s.wait_ge(SP.h, xnf_prev)
                        lo = max(0, hs - 16)
                        hi = min(S, hs + 1136)
                        rows = hi - lo
                        cofs = lo - hs + 16
                        for c in range(8):
                            s.dma_start(warena[:, A_XNF + 1152 * c + cofs:A_XNF + 1152 * c + cofs + rows],
                                        bass.AP(xn_sc.tensor, (tb + lo) * E + 128 * c,
                                                [[E, rows], [1, 128]]), transpose=True).then_inc(SL.h, 16)
                    SL.bump(128)
                    xnf_t = SL.n
                    on("sync", syx)
                    xpad_t = SO.bump()

                    def gp_xpad(g, hf=hf, xnf_prev=xnf_prev):
                        g.wait_ge(SP.h, xnf_prev)
                        last = None
                        for c in range(8):
                            if hf == 0:
                                last = g.memset(warena[:, A_XNF + 1152 * c:A_XNF + 1152 * c + 16], 0.0)
                            else:
                                last = g.memset(warena[:, A_XNF + 1152 * c + 1040:A_XNF + 1152 * (c + 1)], 0.0)
                        last.then_inc(SO.h, 1)
                    on("gpsimd", gp_xpad)

                    ci_par = 0
                    for dst0, wld, wslot, bbase, width, cofs in [(A_QF, ld_ql, A_W2, 24, 1024, 16),
                                                                (A_KF, ld_kl, A_W4, 32, 1152, 0)]:
                        for c in range(8):
                            par = ci_par % 2
                            ci_par += 1
                            pp = px if par == 0 else pw0
                            p3 = (pw1, 0) if par == 0 else (pw1, 512)
                            prev_ev = SA.n
                            pj = SP.bump()

                            def te(t, c=c, wslot=wslot, width=width, cofs=cofs, xnf_t=xnf_t,
                                   wld=wld, xpad_t=xpad_t, prev_ev=prev_ev, pp=pp, p3=p3):
                                t.wait_ge(SL.h, max(xnf_t, wld))
                                t.wait_ge(SO.h, xpad_t)
                                t.wait_ge(SA.h, prev_ev)
                                spans = [(0, 512, pp, 0), (512, 512, pp, 512)]
                                if width == 1152:
                                    spans.append((1024, 128, p3[0], p3[1]))
                                last = None
                                for n0, nl, dstp, d0 in spans:
                                    for ci in range(8):
                                        last = t.matmul(dstp[:, d0:d0 + nl],
                                                        warena[:, wslot + 1024 * ci + 128 * c:wslot + 1024 * ci + 128 * (c + 1)],
                                                        warena[:, A_XNF + 1152 * ci + cofs + n0:A_XNF + 1152 * ci + cofs + n0 + nl],
                                                        start=(ci == 0), stop=(ci == 7))
                                last.then_inc(SP.h, 1)
                            on("tensor", te)
                            ev_t = SA.bump()

                            def ac_ev(a, pj=pj, c=c, dst0=dst0, width=width, bbase=bbase,
                                      pp=pp, p3=p3, bfm_ld=D["bfm_ld"]):
                                a.wait_ge(SP.h, pj)
                                a.wait_ge(SL.h, bfm_ld)
                                last = a.activation(warena[:, dst0 + width * c:dst0 + width * c + min(width, 1024)],
                                                    pp[:, 0:min(width, 1024)], AF.Identity,
                                                    bias=bfm[:, bbase + c:bbase + c + 1])
                                if width == 1152:
                                    last = a.activation(warena[:, dst0 + width * c + 1024:dst0 + width * (c + 1)],
                                                        p3[0][:, p3[1]:p3[1] + 128], AF.Identity,
                                                        bias=bfm[:, bbase + c:bbase + c + 1])
                                last.then_inc(SA.h, 1)
                            on("scalar", ac_ev)
                    kq_ev = SA.n
                    kq_pe = SP.n

                    v_q = 0
                    if hf == 0:
                        ld_bq = SL.n + 16

                        def sy_bq(s, wbf_v=D["wbf_v"]):
                            s.wait_ge(SV.h, wbf_v)
                            s.dma_start(qrow[0:1, 0:1024],
                                        bass.AP(bbf_sc.tensor, 0, [[1024, 1], [1, 1024]])).then_inc(SL.h, 16)
                        SL.bump(16)
                        on("sync", sy_bq)
                        pj_q = SP.bump()

                        def te_q(t, xnf_t=xnf_t, ld_qc=ld_qc, kq_ev=kq_ev):
                            t.wait_ge(SL.h, max(xnf_t, ld_qc))
                            t.wait_ge(SA.h, kq_ev)
                            last = None
                            for half in range(2):
                                n0 = 512 * half
                                for ci in range(8):
                                    last = t.matmul(px[0:1, n0:n0 + 512],
                                                    wAP(A_XNF + 1152 * ci + 16, dims=[[WARENA, 128], [1, 1]]),
                                                    warena[:, A_W3 + 1024 * ci + n0:A_W3 + 1024 * ci + n0 + 512],
                                                    start=(ci == 0), stop=(ci == 7))
                            last.then_inc(SP.h, 1)
                        on("tensor", te_q)
                        v_q = SV.bump()

                        def ve_q(v, pj_q=pj_q, ld_bq=ld_bq):
                            v.wait_ge(SP.h, pj_q)
                            v.wait_ge(SL.h, ld_bq)
                            v.tensor_tensor(qrow[0:1, 0:1024], px[0:1, 0:1024], qrow[0:1, 0:1024],
                                            op=ALU.add).then_inc(SV.h, 1)
                        on("vector", ve_q)
                        qh0 = SO.bump()
                        on("gpsimd", lambda g, v_q=v_q: (
                            g.memset(warena[:, A_QHAT:A_QHAT + 128], 0.0).then_inc(SO.h, 1)))
                        tq_t = SP.bump()

                        def te_tq(t, v_q=v_q):
                            t.wait_ge(SV.h, v_q)
                            last = None
                            for c in range(8):
                                last = t.transpose(ptp[:, 2 * c:2 * c + 1], qrow[0:1, 128 * c:128 * (c + 1)], ident[0:1, 0:1])
                            last.then_inc(SP.h, 1)
                        on("tensor", te_tq)
                        qh_t = SV.bump()

                        def ve_qh(v, tq_t=tq_t, qh0=qh0):
                            v.wait_ge(SP.h, tq_t)
                            v.wait_ge(SO.h, qh0)
                            last = None
                            for c in range(8):
                                last = v.tensor_copy(wAP(A_QHAT + 16 * c + 2 * c, dims=[[WARENA, 64], [1, 1]]),
                                                     ptp[0:64, 2 * c:2 * c + 1])
                                last = v.tensor_copy(wAP(A_QHAT + 16 * c + 2 * c + 1, pbase=64,
                                                         dims=[[WARENA, 64], [1, 1]]),
                                                     ptp[64:128, 2 * c:2 * c + 1])
                            last.then_inc(SV.h, 1)
                        on("vector", ve_qh)
                        qh_done[0] = qh_t

                    for c in range(8):
                        kcb = A_KCB + 1024 * (c % 2)
                        prev_ev = SA.n
                        kc_pj = SP.bump()

                        def te_kc(t, c=c, xnf_t=xnf_t, ld_kc=ld_kc, xpad_t=xpad_t,
                                  prev_ev=prev_ev, v_q=v_q):
                            t.wait_ge(SL.h, max(xnf_t, ld_kc))
                            t.wait_ge(SO.h, xpad_t)
                            t.wait_ge(SA.h, prev_ev)
                            if c == 0 and v_q:
                                t.wait_ge(SV.h, v_q)
                            last = None
                            for half in range(2):
                                n0 = 512 * half
                                for ci in range(8):
                                    last = t.matmul(px[:, n0:n0 + 512],
                                                    warena[:, A_W1 + 1024 * ci + 128 * c:A_W1 + 1024 * ci + 128 * (c + 1)],
                                                    warena[:, A_XNF + 1152 * ci + 16 + n0:A_XNF + 1152 * ci + 16 + n0 + 512],
                                                    start=(ci == 0), stop=(ci == 7))
                            last.then_inc(SP.h, 1)
                        on("tensor", te_kc)
                        kc_ev = SA.bump()

                        def ac_kc(a, kc_pj=kc_pj, kcb=kcb, c=c, bfm_ld=D["bfm_ld"]):
                            a.wait_ge(SP.h, kc_pj)
                            a.wait_ge(SL.h, bfm_ld)
                            a.activation(warena[:, kcb:kcb + 1024], px[:, 0:1024], AF.Identity,
                                         bias=bfm[:, 8 + c:8 + c + 1]).then_inc(SA.h, 1)
                        on("scalar", ac_kc)
                        sc_t = SP.bump()

                        def te_sc(t, kc_ev=kc_ev, c=c, kcb=kcb, qh=qh_done[0]):
                            t.wait_ge(SA.h, kc_ev)
                            t.wait_ge(SV.h, qh)
                            last = None
                            for half in range(2):
                                last = t.matmul(pw1[0:16, 512 * half:512 * half + 512],
                                                wAP(A_QHAT + 16 * c, dims=[[WARENA, 128], [1, 16]]),
                                                warena[:, kcb + 512 * half:kcb + 512 * (half + 1)],
                                                start=(c == 0), stop=(c == 7))
                            last.then_inc(SP.h, 1)
                        on("tensor", te_sc)
                    cls_sc_pe = SP.n
                    e_t = SA.bump()

                    def ac_ecls(a, cls_sc_pe=cls_sc_pe, hf=hf, shl=D["sh_ld"]):
                        a.wait_ge(SP.h, cls_sc_pe)
                        a.wait_ge(SL.h, shl)
                        a.activation(pclsAP(1024 * hf, 512), pw1[0:16, 0:512],
                                     AF.Exp, scale=0.125, bias=cshift[0:16, 16:17])
                        a.activation(pclsAP(1024 * hf + 512, 512), pw1[0:16, 512:1024],
                                     AF.Exp, scale=0.125, bias=cshift[0:16, 16:17]).then_inc(SA.h, 1)
                    on("scalar", ac_ecls)
                    for kk in range(8):
                        prev_cp = SV.n
                        ttp = SP.bump()

                        def te_tp(t, e_t=e_t, hf=hf, kk=kk, prev_cp=prev_cp):
                            t.wait_ge(SA.h, e_t)
                            t.wait_ge(SV.h, prev_cp)
                            t.transpose(ptp[:, 16:32],
                                        pclsAP(1024 * hf + 128 * kk, 128),
                                        ident[0:16, 0:16]).then_inc(SP.h, 1)
                        on("tensor", te_tp)
                        cp_t = SV.bump()

                        def ve_cp(v, ttp=ttp, hf=hf, kk=kk):
                            v.wait_ge(SP.h, ttp)
                            v.tensor_copy(warena[:, A_PT + 16 * (8 * hf + kk):A_PT + 16 * (8 * hf + kk + 1)],
                                          ptp[:, 16:32]).then_inc(SV.h, 1)
                        on("vector", ve_cp)

                    if hf == 0:
                        wol_ld[0] = wslot_load(A_W3, "wol", after_pe=SP.n)

                    pmask = {}
                    seq_state_scx = {}
                    for jj in range(NQT):
                        qs = hs + 128 * jj
                        gj = qs // 128
                        for which in range(2):
                            gv = gj + which
                            key = f"t{gv}"
                            if key in vs_tiles:
                                continue
                            prev_pe = SP.n

                            def syv(s, gv=gv, prev_pe=prev_pe, a0_st=a0_st, seq=seq):
                                s.wait_ge(ST.h, a0_st)
                                s.wait_ge(SP.h, prev_pe)
                                s.dma_start(warena[:, A_VS + VPAD * (gv % 2):A_VS + VPAD * (gv % 2) + 1040],
                                            bass.AP(vloc_sc.tensor, seq * 1040 * 2176 + 128 * gv,
                                                    [[2176, 1040], [1, 128]]),
                                            transpose=True).then_inc(SL.h, 16)
                            SL.bump(16)
                            vs_tiles.pop(f"t{gv - 2}", None)
                            vs_tiles[key] = SL.n
                            on("sync", syv)
                        vs_t = max(vs_tiles[f"t{gj}"], vs_tiles[f"t{gj + 1}"])
                        mbase = 256 if gj == 0 else (512 if gj == 15 else 0)
                        av_last = {}
                        psc_map = [(pw1, 0), (pw1, 512), (py, 0)]
                        for ph in range(H + 3):
                            if ph < H:
                                h = ph
                                hh, hb = h // 2, 64 * (h % 2)
                                pscT, psc0 = psc_map[h % 3]
                                pts = A_SCR + 256 * (h % 4)
                                prev_exp = seq_state_scx.get(h % 3, e_t)
                                s_t = SP.bump()

                                def te_s(t, h=h, hh=hh, hb=hb, jj=jj, pscT=pscT, psc0=psc0,
                                         prev_exp=prev_exp, kq_ev=kq_ev, vs_t=vs_t):
                                    t.wait_ge(SA.h, kq_ev)
                                    t.wait_ge(SA.h, prev_exp)
                                    t.wait_ge(SL.h, vs_t)
                                    last = None
                                    for ch in range(2):
                                        last = t.matmul(pscT[:, psc0 + 128 * ch:psc0 + 128 * (ch + 1)],
                                                        wAP(A_KF + 1152 * hh + 128 * (jj + ch), pbase=hb,
                                                            dims=[[WARENA, 64], [1, 128]]),
                                                        wAP(A_QF + 1024 * hh + 128 * jj, pbase=hb,
                                                            dims=[[WARENA, 64], [1, 128]]),
                                                        start=True, stop=True)
                                    last.then_inc(SP.h, 1)
                                on("tensor", te_s)
                                ex_t = SA.bump()

                                def ac_e(a, s_t=s_t, pscT=pscT, psc0=psc0, pts=pts, h=h, shl=D["sh_ld"]):
                                    a.wait_ge(SP.h, s_t)
                                    a.wait_ge(SL.h, shl)
                                    a.activation(warena[:, pts:pts + 256], pscT[:, psc0:psc0 + 256],
                                                 AF.Exp, scale=0.125, bias=cshift[:, h:h + 1]).then_inc(SA.h, 1)
                                on("scalar", ac_e)
                                seq_state_scx[h % 3] = ex_t
                                m_t = SV.bump()

                                def ve_m(v, ex_t=ex_t, pts=pts, mbase=mbase, ip=D["init_pool"]):
                                    v.wait_ge(SA.h, ex_t)
                                    v.wait_ge(SO.h, ip)
                                    v.tensor_tensor(warena[:, pts:pts + 256], warena[:, pts:pts + 256],
                                                    masks[:, mbase:mbase + 256], op=ALU.mult).then_inc(SV.h, 1)
                                on("vector", ve_m)
                                pmask[h] = m_t
                            h2 = ph - 3
                            if h2 < 0 or h2 >= H:
                                continue
                            h = h2
                            pts = A_SCR + 256 * (h % 4)
                            pctx0 = 65 * (h % 4) + 512 * ((h // 4) % 2)
                            av_t = SP.bump()

                            def te_av(t, m_t=pmask[h], pts=pts, h=h, gj=gj, pctx0=pctx0,
                                      prev_ev=ctxev[(h // 4) % 2]):
                                t.wait_ge(SV.h, m_t)
                                t.wait_ge(SV.h, prev_ev)
                                last = None
                                for ch in range(2):
                                    vslot = A_VS + VPAD * ((gj + ch) % 2)
                                    last = t.matmul(px[:, pctx0:pctx0 + 65],
                                                    warena[:, pts + 128 * ch:pts + 128 * (ch + 1)],
                                                    warena[:, vslot + 65 * h:vslot + 65 * (h + 1)],
                                                    start=(ch == 0), stop=(ch == 1))
                                last.then_inc(SP.h, 1)
                            on("tensor", te_av)
                            av_last[h] = av_t
                            if h % 4 != 3:
                                continue
                            wv = h // 4
                            avs = av_last[4 * wv + 3]
                            lc_t = SV.bump()

                            def ve_l(v, avs=avs, wv=wv):
                                v.wait_ge(SP.h, avs)
                                last = None
                                for k in range(4):
                                    h = 4 * wv + k
                                    pctx0 = 65 * (h % 4) + 512 * ((h // 4) % 2)
                                    last = v.tensor_copy(lbuf[:, k:k + 1], px[:, pctx0 + 64:pctx0 + 65])
                                v.tensor_copy(padt[:, 128:192], padt[:, 192:256]).then_inc(SV.h, 1)
                            on("vector", ve_l)
                            il_t = SA.bump()

                            def ac_il(a, lc_t=lc_t):
                                a.wait_ge(SV.h, lc_t)
                                a.activation(lbuf[:, 8:12], lbuf[:, 0:4], AF.Ln, scale=2.0 ** -32)
                                a.activation(padt[:, 0:32], padt[:, 64:96], AF.Identity)
                                a.activation(lbuf[:, 8:12], lbuf[:, 8:12], AF.Exp, scale=-1.0)
                                a.activation(padt[:, 0:64], padt[:, 64:128], AF.Identity).then_inc(SA.h, 1)
                            on("scalar", ac_il)
                            ce_t = SV.bump()

                            def ve_ce(v, il_t=il_t, wv=wv):
                                v.wait_ge(SA.h, il_t)
                                last = None
                                for k in range(4):
                                    h = 4 * wv + k
                                    pctx0 = 65 * (h % 4) + 512 * ((h // 4) % 2)
                                    last = v.tensor_scalar(warena[:, A_CTM + 64 * h:A_CTM + 64 * (h + 1)],
                                                           px[:, pctx0:pctx0 + 64],
                                                           scalar1=lbuf[:, 8 + k:9 + k], scalar2=2.0 ** -32,
                                                           op0=ALU.mult, op1=ALU.mult)
                                last.then_inc(SV.h, 1)
                            on("vector", ve_ce)
                            ctxev[wv % 2] = ce_t
                        ctx_done = SV.n
                        for c in range(8):
                            prev_cp = SV.n
                            tt = SP.bump()

                            def te_t(t, c=c, ctx_done=ctx_done, prev_cp=prev_cp):
                                t.wait_ge(SV.h, max(ctx_done, prev_cp))
                                t.transpose(ptp[:, 128 * (c % 2):128 * (c % 2) + 128],
                                            warena[:, A_CTM + 128 * c:A_CTM + 128 * (c + 1)],
                                            ident[:, :]).then_inc(SP.h, 1)
                            on("tensor", te_t)
                            cp_t = SV.bump()

                            def ve_t(v, tt=tt, c=c):
                                v.wait_ge(SP.h, tt)
                                v.tensor_copy(warena[:, A_CFM + 128 * c:A_CFM + 128 * (c + 1)],
                                              ptp[:, 128 * (c % 2):128 * (c % 2) + 128]).then_inc(SV.h, 1)
                            on("vector", ve_t)
                        cfm_done = SV.n
                        buf = jj % 2

                        def sy_p(s, qs=qs, buf=buf, pf=pfree[buf], pst=D["part_st"], tb=tb):
                            s.wait_ge(ST.h, pst)
                            s.wait_ge(SV.h, pf)
                            s.dma_start(xbuf[buf][:, :], part_sc[tb + qs:tb + qs + 128, :]).then_inc(SL.h, 16)
                        SL.bump(16)
                        pld = SL.n
                        on("sync", sy_p)
                        prev_fo = SV.n
                        op_t = SP.bump()

                        def te_op(t, cfm_done=cfm_done, wol=wol_ld[0], prev_fo=prev_fo):
                            t.wait_ge(SV.h, cfm_done)
                            t.wait_ge(SL.h, wol)
                            t.wait_ge(SV.h, prev_fo)
                            last = None
                            for half in range(2):
                                n0 = 512 * half
                                for ci in range(8):
                                    last = t.matmul(pw0[:, n0:n0 + 512],
                                                    warena[:, A_CFM + 128 * ci:A_CFM + 128 * (ci + 1)],
                                                    warena[:, A_W3 + 1024 * ci + n0:A_W3 + 1024 * ci + n0 + 512],
                                                    start=(ci == 0), stop=(ci == 7))
                            last.then_inc(SP.h, 1)
                        on("tensor", te_op)
                        fo_t = SV.bump()

                        def ve_fo(v, op_t=op_t, pld=pld, buf=buf):
                            v.wait_ge(SP.h, op_t)
                            v.wait_ge(SL.h, pld)
                            v.scalar_tensor_tensor(evac1[:, :], pw0[:, :], 1.0, xbuf[buf][:, :],
                                                   op0=ALU.mult, op1=ALU.add).then_inc(SV.h, 1)
                        on("vector", ve_fo)
                        pfree[buf] = fo_t

                        def sc_y(sc, qs=qs, fo_t=fo_t, tb=tb):
                            sc.wait_ge(SV.h, fo_t)
                            sc.dma_start(y[tb + qs:tb + qs + 128, :], evac1[:, :]).then_inc(ST.h, 16)
                        ST.bump(16)
                        st_now = ST.n
                        on("scalar", sc_y)
                        on("vector", lambda v, st_now=st_now: v.wait_ge(ST.h, st_now))
                        if DEBUG and seq == 0 and hf == 0 and jj == 0:
                            def sc_t0(sc, fo_t=fo_t):
                                sc.wait_ge(SV.h, fo_t)
                                sc.dma_start(dbg["d_t0pts"], warena[:, A_SCR:A_SCR + 1024]).then_inc(ST.h, 16)
                                sc.dma_start(dbg["d_t0ctm"], warena[:, A_CTM:A_CTM + 1024]).then_inc(ST.h, 16)
                                sc.dma_start(dbg["d_t0vs"], warena[:, A_VS:A_VS + 2112]).then_inc(ST.h, 16)
                            ST.bump(48)
                            on("scalar", sc_t0)

                ld_woc = wslot_load(A_W1, "woc", after_pe=SP.n)
                prev_sv = SV.n
                prev_sa2 = SA.n
                for g in range(16):
                    prev_pe = SP.n

                    def syv(s, g=g, prev_pe=prev_pe, a0_st=a0_st, seq=seq):
                        s.wait_ge(ST.h, a0_st)
                        s.wait_ge(SP.h, prev_pe)
                        s.dma_start(warena[:, A_VS + VPAD * (g % 2):A_VS + VPAD * (g % 2) + 1040],
                                    bass.AP(vcls_sc.tensor, seq * 1040 * 2048 + 128 * g,
                                            [[2048, 1040], [1, 128]]),
                                    transpose=True).then_inc(SL.h, 16)
                    SL.bump(16)
                    vld = SL.n
                    on("sync", syv)
                    cav = SP.bump()

                    def te_cav(t, g=g, vld=vld, prev_sv=prev_sv, prev_sa2=prev_sa2):
                        t.wait_ge(SL.h, vld)
                        if g == 0:
                            t.wait_ge(SV.h, prev_sv)
                            t.wait_ge(SA.h, prev_sa2)
                        vslot = A_VS + VPAD * (g % 2)
                        last = None
                        for dstp, n0, nl in [(pw0[0:16, 0:455], 0, 455), (pw0[0:16, 512:967], 455, 455),
                                             (py[0:16, 0:130], 910, 130)]:
                            last = t.matmul(dstp, warena[:, A_PT + 16 * g:A_PT + 16 * (g + 1)],
                                            warena[:, vslot + n0:vslot + n0 + nl],
                                            start=(g == 0), stop=(g == 15))
                        last.then_inc(SP.h, 1)
                    on("tensor", te_cav)
                cav_done = SP.n
                lr_t = SV.bump()
                cv_t = SV.bump()

                def ve_cv(v, cav_done=cav_done, lr_t=lr_t):
                    v.wait_ge(SP.h, cav_done)
                    v.reduce_sum(lbuf[0:16, 15:16], pclsAP(0, S), axis=AX.X).then_inc(SV.h, 1)
                    v.tensor_copy(clsrows[0:16, 0:455], pw0[0:16, 0:455])
                    v.tensor_copy(clsrows[0:16, 455:910], pw0[0:16, 512:967])
                    v.tensor_copy(clsrows[0:16, 910:1040], py[0:16, 0:130])
                    v.tensor_copy(padt[:, 128:192], padt[:, 192:256]).then_inc(SV.h, 1)
                on("vector", ve_cv)
                il_t = SA.bump()

                def ac_il2(a, cv_t=cv_t):
                    a.wait_ge(SV.h, cv_t)
                    a.activation(lbuf[0:16, 14:15], lbuf[0:16, 15:16], AF.Ln, scale=2.0 ** -32)
                    a.activation(padt[:, 0:32], padt[:, 64:96], AF.Identity)
                    a.activation(lbuf[0:16, 14:15], lbuf[0:16, 14:15], AF.Exp, scale=-1.0)
                    a.activation(padt[:, 0:64], padt[:, 64:128], AF.Identity).then_inc(SA.h, 1)
                on("scalar", ac_il2)
                sc_v = SV.bump()
                prev_stg = D.get("stg", 0)

                def ve_sc(v, il_t=il_t, prev_stg=prev_stg):
                    v.wait_ge(SA.h, il_t)
                    v.wait_ge(ST.h, prev_stg)
                    v.tensor_scalar(warena[0:16, A_SCR:A_SCR + 1040], clsrows[0:16, 0:1040],
                                    scalar1=lbuf[0:16, 14:15], scalar2=2.0 ** -32,
                                    op0=ALU.mult, op1=ALU.mult).then_inc(SV.h, 1)
                on("vector", ve_sc)
                st_g = ST.n + 16 * 16

                def sc_g(sc, sc_v=sc_v):
                    sc.wait_ge(SV.h, sc_v)
                    for c in range(8):
                        for g in range(2):
                            h = 2 * c + g
                            sc.dma_start(bass.AP(warena, (64 * g) * WARENA + A_CFM + c,
                                                 [[WARENA, 64], [1, 1]]),
                                         bass.AP(warena, h * WARENA + A_SCR + 65 * h,
                                                 [[WARENA, 1], [1, 64]])).then_inc(ST.h, 16)
                ST.bump(16 * 16)
                D["stg"] = ST.n
                on("scalar", sc_g)
                ocp = SP.bump()

                def te_oc(t, st_g=st_g, ld_woc=ld_woc):
                    t.wait_ge(ST.h, st_g)
                    t.wait_ge(SL.h, ld_woc)
                    last = None
                    for half in range(2):
                        n0 = 512 * half
                        for ci in range(8):
                            dstp = py[0:1, 0:512] if half == 0 else pw1[0:1, 0:512]
                            last = t.matmul(dstp,
                                            wAP(A_CFM + ci, dims=[[WARENA, 128], [1, 1]]),
                                            warena[:, A_W1 + 1024 * ci + n0:A_W1 + 1024 * ci + n0 + 512],
                                            start=(ci == 0), stop=(ci == 7))
                    last.then_inc(SP.h, 1)
                on("tensor", te_oc)

                def sy_pr(s, pst=D["part_st"], tb=tb, pf=pfree[0]):
                    s.wait_ge(ST.h, pst)
                    s.wait_ge(SV.h, pf)
                    s.dma_start(xbuf[0][0:1, :], part_sc[tb:tb + 1, :]).then_inc(SL.h, 16)
                SL.bump(16)
                pld2 = SL.n
                on("sync", sy_pr)
                oc_t = SV.bump()
                prev_r0 = D.get("r0_st", 0)

                def ve_oc(v, ocp=ocp, pld2=pld2, prev_r0=prev_r0):
                    v.wait_ge(SP.h, ocp)
                    v.wait_ge(SL.h, pld2)
                    v.wait_ge(ST.h, prev_r0)
                    v.tensor_copy(outcls[0:1, 0:512], py[0:1, 0:512])
                    v.tensor_copy(outcls[0:1, 512:1024], pw1[0:1, 0:512])
                    v.tensor_tensor(outcls[0:1, :], outcls[0:1, :], brow[0:1, :], op=ALU.add)
                    v.tensor_tensor(outcls[0:1, :], outcls[0:1, :], bo_bc[0:1, :], op=ALU.subtract)
                    v.tensor_tensor(outcls[0:1, :], outcls[0:1, :], xbuf[0][0:1, :], op=ALU.add).then_inc(SV.h, 1)
                on("vector", ve_oc)
                ystore = ST.n

                def sc_r0(sc, oc_t=oc_t, ystore=ystore, tb=tb):
                    sc.wait_ge(SV.h, oc_t)
                    sc.wait_ge(ST.h, ystore)
                    sc.dma_start(y[tb:tb + 1, :], outcls[0:1, :]).then_inc(ST.h, 16)
                ST.bump(16)
                D["r0_st"] = ST.n
                on("scalar", sc_r0)
                if DEBUG and seq == 0:
                    def sc_dbgc(sc, oc_t=oc_t, sc_v=sc_v, stg=D["stg"], cv_t=cv_t):
                        sc.wait_ge(SV.h, oc_t)
                        sc.wait_ge(ST.h, stg)
                        sc.dma_start(dbg["d_clsrows"], clsrows[0:16, 0:1040]).then_inc(ST.h, 16)
                        sc.dma_start(dbg["d_lb"], lbuf[0:16, 0:16]).then_inc(ST.h, 16)
                        sc.dma_start(dbg["d_outcls"], outcls[0:1, :]).then_inc(ST.h, 16)
                        sc.dma_start(dbg["d_cfm0"], warena[:, 64832:64832 + 8]).then_inc(ST.h, 16)
                        sc.dma_start(dbg["d_scr0"], warena[0:16, 66048:66048 + 1040]).then_inc(ST.h, 16)
                        sc.dma_start(dbg["d_qhat"], warena[:, 65856:65856 + 128]).then_inc(ST.h, 16)
                    ST.bump(96)
                    D["r0_st"] = ST.n
                    on("scalar", sc_dbgc)

        attn_phase()

        if DEBUG:
            dtgt = SL.n + 256

            def sy_dbg(s, stf=ST.n, dtgt=dtgt):
                s.wait_ge(ST.h, stf)
                s.dma_start(dbg["d_xn"], xn_sc).then_inc(SL.h, 16)
                s.dma_start(dbg["d_ai"], ai_sc).then_inc(SL.h, 16)
                s.dma_start(dbg["d_part"], part_sc).then_inc(SL.h, 16)
                s.dma_start(dbg["d_vloc"], bass.AP(vloc_sc.tensor, 0, [[2176, 1040], [1, 2176]])).then_inc(SL.h, 16)
                s.dma_start(dbg["d_vcls"], bass.AP(vcls_sc.tensor, 0, [[2048, 1040], [1, 2048]])).then_inc(SL.h, 16)
                s.dma_start(dbg["d_hq"], hq_sc).then_inc(SL.h, 16)
                s.dma_start(dbg["d_sb"], warena[:, 50176:50176 + 4096]).then_inc(SL.h, 16)
                s.dma_start(dbg["d_pcls"], bass.AP(warena, 70272, [[WARENA, 16], [1, S]])).then_inc(SL.h, 16)
                s.dma_start(dbg["d_qf"], warena[:, 41984:41984 + 8192]).then_inc(SL.h, 16)
                s.dma_start(dbg["d_msk"], masks[:, :]).then_inc(SL.h, 16)
                s.dma_start(dbg["d_ctm"], warena[:, 63808:63808 + 1024]).then_inc(SL.h, 16)
                s.dma_start(dbg["d_cfm"], warena[:, 64832:64832 + 1024]).then_inc(SL.h, 16)
                s.dma_start(dbg["d_vs"], warena[:, 59392:59392 + 2112]).then_inc(SL.h, 16)
                s.dma_start(dbg["d_pts"], warena[:, 66048:66048 + 512]).then_inc(SL.h, 16)
                s.dma_start(dbg["d_w3"], warena[:, 16384:16384 + 8192]).then_inc(SL.h, 16)
                s.dma_start(dbg["d_wbfol"], wbf_sc[:, 57344:57344 + 8192]).then_inc(SL.h, 16)
                s.wait_ge(SL.h, dtgt)
            SL.bump(224)
            on("sync", sy_dbg)
        st_final = ST.n
        on("sync", lambda s, st_final=st_final: s.wait_ge(ST.h, st_final))

        with nc.Block() as block:
            @block.sync
            def _(sync):
                for f in Q["sync"]:
                    f(sync)

            @block.vector
            def _(vector):
                for f in Q["vector"]:
                    f(vector)

            @block.scalar
            def _(scalar):
                for f in Q["scalar"]:
                    f(scalar)

            @block.tensor
            def _(tensor):
                for f in Q["tensor"]:
                    f(tensor)

            @block.gpsimd
            def _(gpsimd):
                for f in Q["gpsimd"]:
                    f(gpsimd)
    return nc


def _chunked_fm(w):
    o, i = w.shape
    return np.ascontiguousarray(w.reshape(o, i // 128, 128).transpose(2, 1, 0).reshape(128, (i // 128) * o))


def kernel(**inputs):
    x = np.asarray(inputs["L0_attn_hidden_states"], dtype=np.float32)
    iwc = np.asarray(inputs["L0_attn_in_w_cls"], dtype=np.float32)
    iwl = np.asarray(inputs["L0_attn_in_w_loc"], dtype=np.float32)
    wmap = {
        "wqc": _chunked_fm(iwc[0:1024]),
        "wkc": _chunked_fm(iwc[1024:2048]),
        "wvc": _chunked_fm(iwc[2048:3072]),
        "woc": _chunked_fm(np.asarray(inputs["L0_attn_out_w_cls"], dtype=np.float32)),
        "wql": _chunked_fm(iwl[0:1024]),
        "wkl": _chunked_fm(iwl[1024:2048]),
        "wvl": _chunked_fm(iwl[2048:3072]),
        "wol": _chunked_fm(np.asarray(inputs["L0_attn_out_w_loc"], dtype=np.float32)),
        "wg": _chunked_fm(np.asarray(inputs["L0_ffn_w_gate"], dtype=np.float32)),
        "wu": _chunked_fm(np.asarray(inputs["L0_ffn_w_up"], dtype=np.float32)),
        "wdn": _chunked_fm(np.asarray(inputs["L0_ffn_w_down"], dtype=np.float32)),
        "bqkv_c": np.ascontiguousarray(np.asarray(inputs["L0_attn_in_b_cls"], dtype=np.float32)),
        "bqkv_l": np.ascontiguousarray(np.asarray(inputs["L0_attn_in_b_loc"], dtype=np.float32)),
        "boc": np.ascontiguousarray(np.asarray(inputs["L0_attn_out_b_cls"], dtype=np.float32)),
        "bol": np.ascontiguousarray(np.asarray(inputs["L0_attn_out_b_loc"], dtype=np.float32)),
    }
    nc = build_nc()
    in_maps = []
    for c in range(NCORES):
        m = dict(wmap)
        m["xin"] = np.ascontiguousarray(x[NSEQ * c:NSEQ * (c + 1)].reshape(T, E))
        in_maps.append(m)
    res = run_bass_kernel_spmd(nc, in_maps, core_ids=list(range(NCORES)))
    out = np.empty((16, S, E), dtype=np.float32)
    for c in range(NCORES):
        out[NSEQ * c:NSEQ * (c + 1)] = np.asarray(res.results[c]["y"]).reshape(NSEQ, S, E)
    return out


# version 10
# speedup vs baseline: 1.0740x; 1.0044x over previous
import math
import contextlib
import numpy as np
import concourse.bass as bass
import concourse.mybir as mybir
from concourse.bass_utils import run_bass_kernel_spmd

F32, BF16 = mybir.dt.float32, mybir.dt.bfloat16
AF = mybir.ActivationFunctionType
ALU = mybir.AluOpType
AX = mybir.AxisListType

NCORES = 8
S = 2048
NSEQ = 2
T = S * NSEQ
E = 1024
H = 16
INNER = 4096
NT = T // 128
RC = 12582912.0
NQT = 8
VPAD = 1056
DEBUG = False


class Ctr:
    def __init__(self, handle):
        self.h = handle
        self.n = 0

    def bump(self, k=1):
        self.n += k
        return self.n


def build_nc():
    nc = bass.Bass("TRN2", target_bir_lowering=False)

    xin = nc.dram_tensor("xin", [T, E], F32, kind="ExternalInput").ap()
    win = {}
    for nm in ["wqc", "wkc", "wvc", "woc", "wql", "wkl", "wvl", "wol"]:
        win[nm] = nc.dram_tensor(nm, [128, 8 * E], F32, kind="ExternalInput").ap()
    wg = nc.dram_tensor("wg", [128, 8 * INNER], F32, kind="ExternalInput").ap()
    wu = nc.dram_tensor("wu", [128, 8 * INNER], F32, kind="ExternalInput").ap()
    wdn = nc.dram_tensor("wdn", [128, 32 * E], F32, kind="ExternalInput").ap()
    bqkv_c = nc.dram_tensor("bqkv_c", [3 * E], F32, kind="ExternalInput").ap()
    bqkv_l = nc.dram_tensor("bqkv_l", [3 * E], F32, kind="ExternalInput").ap()
    boc = nc.dram_tensor("boc", [E], F32, kind="ExternalInput").ap()
    bol = nc.dram_tensor("bol", [E], F32, kind="ExternalInput").ap()
    y = nc.dram_tensor("y", [T, E], F32, kind="ExternalOutput").ap()

    xn_sc = nc.dram_tensor("xn_sc", [T, E], BF16, kind="Internal").ap()
    ai_sc = nc.dram_tensor("ai_sc", [T, E], BF16, kind="Internal").ap()
    hq_sc = nc.dram_tensor("hq_sc", [T, INNER], BF16, kind="Internal").ap()
    part_sc = nc.dram_tensor("part_sc", [T, E], F32, kind="Internal").ap()
    wgq_sc = nc.dram_tensor("wgq_sc", [128, 8 * INNER], BF16, kind="Internal").ap()
    wuq_sc = nc.dram_tensor("wuq_sc", [128, 8 * INNER], BF16, kind="Internal").ap()
    wdq_sc = nc.dram_tensor("wdq_sc", [128, 32 * E], BF16, kind="Internal").ap()
    wbf_sc = nc.dram_tensor("wbf_sc", [128, 64 * E], BF16, kind="Internal").ap()
    bbf_sc = nc.dram_tensor("bbf_sc", [6144], BF16, kind="Internal").ap()
    csum_sc = nc.dram_tensor("csum_sc", [256], F32, kind="Internal").ap()
    one_sc = nc.dram_tensor("one_sc", [8], F32, kind="Internal").ap()
    vloc_sc = nc.dram_tensor("vloc_sc", [2080, 2176], BF16, kind="Internal").ap()
    vcls_sc = nc.dram_tensor("vcls_sc", [2080, 2048], BF16, kind="Internal").ap()
    WOFF = {"wkc": 0, "wvc": 8192, "wqc": 16384, "woc": 24576,
            "wql": 32768, "wkl": 40960, "wvl": 49152, "wol": 57344}
    dbg = {}
    if DEBUG:
        dbg["d_xn"] = nc.dram_tensor("d_xn", [T, E], BF16, kind="ExternalOutput").ap()
        dbg["d_ai"] = nc.dram_tensor("d_ai", [T, E], BF16, kind="ExternalOutput").ap()
        dbg["d_part"] = nc.dram_tensor("d_part", [T, E], F32, kind="ExternalOutput").ap()
        dbg["d_vloc"] = nc.dram_tensor("d_vloc", [1040, 2176], BF16, kind="ExternalOutput").ap()
        dbg["d_vcls"] = nc.dram_tensor("d_vcls", [1040, 2048], BF16, kind="ExternalOutput").ap()
        dbg["d_hq"] = nc.dram_tensor("d_hq", [T, INNER], BF16, kind="ExternalOutput").ap()
        dbg["d_sb"] = nc.dram_tensor("d_sb", [128, 4096], BF16, kind="ExternalOutput").ap()
        dbg["d_pcls"] = nc.dram_tensor("d_pcls", [16, S], BF16, kind="ExternalOutput").ap()
        dbg["d_qf"] = nc.dram_tensor("d_qf", [128, 8192], BF16, kind="ExternalOutput").ap()
        dbg["d_msk"] = nc.dram_tensor("d_msk", [128, 768], BF16, kind="ExternalOutput").ap()
        dbg["d_ctm"] = nc.dram_tensor("d_ctm", [128, 1024], BF16, kind="ExternalOutput").ap()
        dbg["d_cfm"] = nc.dram_tensor("d_cfm", [128, 1024], BF16, kind="ExternalOutput").ap()
        dbg["d_vs"] = nc.dram_tensor("d_vs", [128, 2112], BF16, kind="ExternalOutput").ap()
        dbg["d_pts"] = nc.dram_tensor("d_pts", [128, 512], BF16, kind="ExternalOutput").ap()
        dbg["d_clsrows"] = nc.dram_tensor("d_clsrows", [16, 1040], F32, kind="ExternalOutput").ap()
        dbg["d_lb"] = nc.dram_tensor("d_lb", [16, 16], F32, kind="ExternalOutput").ap()
        dbg["d_outcls"] = nc.dram_tensor("d_outcls", [1, E], F32, kind="ExternalOutput").ap()
        dbg["d_cfm0"] = nc.dram_tensor("d_cfm0", [128, 8], BF16, kind="ExternalOutput").ap()
        dbg["d_scr0"] = nc.dram_tensor("d_scr0", [16, 1040], BF16, kind="ExternalOutput").ap()
        dbg["d_qhat"] = nc.dram_tensor("d_qhat", [128, 128], BF16, kind="ExternalOutput").ap()
        dbg["d_w3"] = nc.dram_tensor("d_w3", [128, 8192], BF16, kind="ExternalOutput").ap()
        dbg["d_st0"] = nc.dram_tensor("d_st0", [128, 11], F32, kind="ExternalOutput").ap()
        dbg["d_st1"] = nc.dram_tensor("d_st1", [128, 11], F32, kind="ExternalOutput").ap()
        dbg["d_wbfol"] = nc.dram_tensor("d_wbfol", [128, 8192], BF16, kind="ExternalOutput").ap()
        dbg["d_t0pts"] = nc.dram_tensor("d_t0pts", [128, 1024], BF16, kind="ExternalOutput").ap()
        dbg["d_t0ctm"] = nc.dram_tensor("d_t0ctm", [128, 1024], BF16, kind="ExternalOutput").ap()
        dbg["d_t0vs"] = nc.dram_tensor("d_t0vs", [128, 2112], BF16, kind="ExternalOutput").ap()

    es = contextlib.ExitStack()
    with es:
        ec = es.enter_context
        xbuf = [ec(nc.sbuf_tensor(f"xbuf{i}", [128, E], F32)) for i in range(2)]
        fstream = ec(nc.sbuf_tensor("fstream", [128, 4096], F32))
        evac1 = ec(nc.sbuf_tensor("evac1", [128, E], F32))
        gbuf = ec(nc.sbuf_tensor("gbuf", [128, 1024], F32))
        stats = ec(nc.sbuf_tensor("stats", [128, 44], F32))
        pstats = ec(nc.sbuf_tensor("pstats", [128, 24], F32))
        rs1g_all = ec(nc.sbuf_tensor("rs1g_all", [128, NT], F32))
        cc_all = ec(nc.sbuf_tensor("cc_all", [128, NT], F32))
        rs1_all = ec(nc.sbuf_tensor("rs1_all", [128, NT], F32))
        sdown_all = ec(nc.sbuf_tensor("sdown_all", [128, NT], F32))
        wsc = ec(nc.sbuf_tensor("wsc", [128, 12], F32))
        consts = ec(nc.sbuf_tensor("consts", [128, 8], F32))
        bo_bc = ec(nc.sbuf_tensor("bo_bc", [128, E], F32))
        ident = ec(nc.sbuf_tensor("ident", [128, 128], BF16))
        bfm = ec(nc.sbuf_tensor("bfm", [128, 48], BF16))
        brow = ec(nc.sbuf_tensor("brow", [2, E], F32))
        masks = ec(nc.sbuf_tensor("masks", [128, 768], BF16))
        zrow = ec(nc.sbuf_tensor("zrow", [128, 112], BF16))
        padt = ec(nc.sbuf_tensor("padt", [128, 256], F32))
        p0stage = ec(nc.sbuf_tensor("p0stage", [128, 4096], BF16))
        cshift = ec(nc.sbuf_tensor("cshift", [128, 18], F32))
        rowred = ec(nc.sbuf_tensor("rowred", [2, 256], F32))
        lbuf = ec(nc.sbuf_tensor("lbuf", [128, 16], F32))
        clsrows = ec(nc.sbuf_tensor("clsrows", [16, 1040], F32))
        qrow = ec(nc.sbuf_tensor("qrow", [1, E], BF16))
        outcls = ec(nc.sbuf_tensor("outcls", [1, E], F32))
        WARENA = 73728
        warena = ec(nc.sbuf_tensor("warena", [128, WARENA], BF16))

        SL = Ctr(ec(nc.semaphore("s_ld")))
        ST = Ctr(ec(nc.semaphore("s_st")))
        SP = Ctr(ec(nc.semaphore("s_pe")))
        SV = Ctr(ec(nc.semaphore("s_dve")))
        SA = Ctr(ec(nc.semaphore("s_act")))
        SO = Ctr(ec(nc.semaphore("s_pool")))
        SX = Ctr(ec(nc.semaphore("s_xld")))
        SH = Ctr(ec(nc.semaphore("s_hst")))

        pw0 = ec(nc.psum_tensor("pw0", [128, 1024], F32))
        pw1 = ec(nc.psum_tensor("pw1", [128, 1024], F32))
        px = ec(nc.psum_tensor("px", [128, 1024], F32))
        py = ec(nc.psum_tensor("py", [128, 512], F32))
        ptp = ec(nc.psum_tensor("ptp", [128, 256], BF16))
        psums = [pw0, pw1, px, py]

        Q = {"sync": [], "vector": [], "scalar": [], "tensor": [], "gpsimd": []}

        def on(eng, fn):
            Q[eng].append(fn)

        def wAP(col, pbase=0, dims=None):
            return bass.AP(warena, pbase * WARENA + col, dims)

        D = {}

        def init_phase():
            base0 = SO.n

            def gp(g, base0=base0):
                k = [base0]

                def step(instr):
                    instr.then_inc(SO.h, 1)
                    k[0] += 1
                    g.wait_ge(SO.h, k[0])
                step(g.memset(consts[:, 0:1], 1e-5))
                step(g.memset(consts[:, 1:2], math.log(127.0)))
                step(g.memset(consts[:, 2:3], -math.log(1e-5)))
                step(g.memset(consts[:, 3:4], math.log(127.0) - math.log(1e-5)))
                step(g.memset(consts[:, 5:6], -(math.log(127.0) - math.log(1e-5))))
                step(g.memset(consts[:, 4:5], 1.0))
                step(g.memset(ident[:, :], 0.0))
                step(g.affine_select(ident[:, :], ident[:, :], compare_op=ALU.not_equal,
                                     fill=1.0, base=0, pattern=[[-1, 128]], channel_multiplier=1))
                m = masks
                step(g.memset(m[:, 0:768], 1.0))
                for bc in (0, 256, 512):
                    step(g.affine_select(m[:, bc:bc + 128], m[:, bc:bc + 128], compare_op=ALU.is_ge,
                                         fill=0.0, base=0, pattern=[[-1, 128]], channel_multiplier=1))
                    step(g.affine_select(m[:, bc:bc + 128], m[:, bc:bc + 128], compare_op=ALU.is_ge,
                                         fill=0.0, base=32, pattern=[[1, 128]], channel_multiplier=-1))
                    bb = bc + 128
                    step(g.affine_select(m[:, bb:bb + 128], m[:, bb:bb + 128], compare_op=ALU.is_ge,
                                         fill=0.0, base=-96, pattern=[[1, 128]], channel_multiplier=-1))
                    step(g.affine_select(m[:, bb:bb + 128], m[:, bb:bb + 128], compare_op=ALU.is_ge,
                                         fill=0.0, base=128, pattern=[[-1, 128]], channel_multiplier=1))
                step(g.affine_select(m[:, 256:384], m[:, 256:384], compare_op=ALU.is_ge,
                                     fill=0.0, base=-17, pattern=[[0, 128]], channel_multiplier=1))
                step(g.affine_select(m[:, 640:768], m[:, 640:768], compare_op=ALU.is_ge,
                                     fill=0.0, base=15, pattern=[[0, 128]], channel_multiplier=-1))
                step(g.memset(zrow[:, :], 0.0))
                step(g.memset(padt[:, :], 0.0))
            SO.bump(25)
            D["init_pool"] = SO.n
            on("gpsimd", gp)
            ld = SL.n + 48

            def sy(s):
                s.dma_start(bo_bc[:, :], bass.AP(bol.tensor, 0, [[0, 128], [1, E]])).then_inc(SL.h, 16)
                s.dma_start(brow[0:1, :], bass.AP(boc.tensor, 0, [[E, 1], [1, E]])).then_inc(SL.h, 16)
                s.dma_start(brow[1:2, :], bass.AP(bol.tensor, 0, [[E, 1], [1, E]])).then_inc(SL.h, 16)
            SL.bump(48)
            D["init_ld"] = ld
            on("sync", sy)

        init_phase()

        def prep_phase():
            ld0 = SL.n + 32

            def sy1(s):
                s.dma_start(fstream[0:1, 0:3072], bass.AP(bqkv_c.tensor, 0, [[3072, 1], [1, 3072]])).then_inc(SL.h, 16)
                s.dma_start(fstream[1:2, 0:3072], bass.AP(bqkv_l.tensor, 0, [[3072, 1], [1, 3072]])).then_inc(SL.h, 16)
            SL.bump(32)
            on("sync", sy1)
            v_b = SV.bump()
            on("vector", lambda v, ld0=ld0: (
                v.wait_ge(SL.h, ld0),
                v.tensor_copy(warena[0:2, 0:3072], fstream[0:2, 0:3072]).then_inc(SV.h, 1)))
            st_b = ST.n + 32

            def sc1(sc, v_b=v_b):
                sc.wait_ge(SV.h, v_b)
                sc.dma_start(bass.AP(bbf_sc.tensor, 0, [[3072, 1], [1, 3072]]), warena[0:1, 0:3072]).then_inc(ST.h, 16)
                sc.dma_start(bass.AP(bbf_sc.tensor, 3072, [[3072, 1], [1, 3072]]), warena[1:2, 0:3072]).then_inc(ST.h, 16)
            ST.bump(32)
            on("scalar", sc1)

            def sy2(s, st_b=st_b):
                s.wait_ge(ST.h, st_b)
                s.dma_start(bfm[:, 0:48], bass.AP(bbf_sc.tensor, 0, [[128, 48], [1, 128]]),
                            transpose=True).then_inc(SL.h, 16)
            SL.bump(16)
            on("sync", sy2)
            D["bfm_ld"] = SL.n



            pa_state = {0: v_b, 1: v_b}
            pcnt = 0
            for wi, wsrc in enumerate([wg, wu, wdn]):
                on("vector", lambda v, wi=wi: v.memset(stats[:, 33 + wi:34 + wi], 0.0))
                for tix in range(16):
                    bufp = pcnt % 2
                    pcnt += 1
                    f0 = 2048 * bufp
                    prev_v = pa_state[bufp]
                    sem = SL if bufp == 0 else SH
                    ldt = sem.n + 16

                    def syl(s, wsrc=wsrc, tix=tix, prev_v=prev_v, f0=f0, sem=sem):
                        s.wait_ge(SV.h, prev_v)
                        s.dma_start(fstream[:, f0:f0 + 2048],
                                    wsrc[:, 2048 * tix:2048 * (tix + 1)]).then_inc(sem.h, 16)
                    sem.bump(16)
                    on("sync" if bufp == 0 else "gpsimd", syl)
                    red_t = SV.bump()
                    SV.bump()

                    def vel(v, ldt=ldt, wi=wi, red_t=red_t, f0=f0, sem=sem):
                        v.wait_ge(sem.h, ldt)
                        v.reduce_sum(stats[:, 36:37], fstream[:, f0:f0 + 2048], axis=AX.X,
                                     apply_absolute_value=True).then_inc(SV.h, 1)
                        v.wait_ge(SV.h, red_t)
                        v.tensor_tensor(stats[:, 33 + wi:34 + wi], stats[:, 33 + wi:34 + wi],
                                        stats[:, 36:37], op=ALU.add).then_inc(SV.h, 1)
                    on("vector", vel)
                    pa_state[bufp] = SV.n
            v_sums = SV.n

            for wi in range(3):
                stA = ST.n + 16

                def sc_a(sc, wi=wi, v_sums=v_sums):
                    sc.wait_ge(SV.h, v_sums)
                    sc.dma_start(csum_sc[0:128], stats[:, 33 + wi:34 + wi]).then_inc(ST.h, 16)
                ST.bump(16)
                on("scalar", sc_a)
                stB = ST.n + 16

                def sc_b(sc, stA=stA):
                    sc.wait_ge(ST.h, stA)
                    sc.dma_start(rowred[0:1, 0:128],
                                 bass.AP(csum_sc.tensor, 0, [[128, 1], [1, 128]])).then_inc(ST.h, 16)
                ST.bump(16)
                on("scalar", sc_b)
                r_t = SV.bump()
                fin_t = SV.bump()

                def ve_r(v, stB=stB, wi=wi, r_t=r_t):
                    v.wait_ge(ST.h, stB)
                    v.reduce_sum(rowred[0:1, 128:129], rowred[0:1, 0:128], axis=AX.X).then_inc(SV.h, 1)
                    v.wait_ge(SV.h, r_t)
                    v.tensor_copy(padt[:, 128:192], padt[:, 192:256])
                    v.tensor_scalar(rowred[0:1, 140 + wi:141 + wi], rowred[0:1, 128:129],
                                    scalar1=1.0 / (4096.0 * 1024.0), scalar2=1e-5,
                                    op0=ALU.mult, op1=ALU.max)
                    v.tensor_copy(padt[:, 128:192], padt[:, 192:256]).then_inc(SV.h, 1)
                on("vector", ve_r)
                nxt = SV.n
                on("scalar", lambda sc, nxt=nxt: sc.wait_ge(SV.h, nxt))
            v_scales = SV.n
            stC = ST.n + 16

            def sc_c(sc, v_scales=v_scales):
                sc.wait_ge(SV.h, v_scales)
                sc.dma_start(one_sc[0:3], rowred[0:1, 140:143]).then_inc(ST.h, 16)
            ST.bump(16)
            on("scalar", sc_c)
            stD = ST.n + 48

            def sc_d(sc, stC=stC):
                sc.wait_ge(ST.h, stC)
                for wi in range(3):
                    sc.dma_start(wsc[:, 2 * wi + 1:2 * wi + 2],
                                 bass.AP(one_sc.tensor, wi, [[0, 128], [1, 1]])).then_inc(ST.h, 16)
            ST.bump(48)
            on("scalar", sc_d)
            sw_t = SA.bump()

            def ac_sw(a, stD=stD):
                a.wait_ge(ST.h, stD)
                for wi in range(3):
                    a.activation(wsc[:, 8 + wi:9 + wi], wsc[:, 2 * wi + 1:2 * wi + 2], AF.Ln)
                for wi in range(3):
                    a.activation(wsc[:, 2 * wi:2 * wi + 1], wsc[:, 8 + wi:9 + wi], AF.Exp, scale=-1.0)
                a.activation(padt[:, 0:64], padt[:, 64:128], AF.Identity).then_inc(SA.h, 1)
            on("scalar", ac_sw)
            D["sw_t"] = sw_t

            v_pa = SV.n
            pb_state = {0: (0, v_pa), 1: (0, v_pa)}
            pcnt = 0
            for wi, (wsrc, dst) in enumerate([(wg, None), (wu, None), (wdn, wdq_sc)]):
                for tix in range(16):
                    bufp = pcnt % 2
                    pcnt += 1
                    f0 = 2048 * bufp
                    w0 = (32768 * wi + 2048 * tix) if dst is None else (65536 + 2048 * bufp)
                    prev_st, prev_q = pb_state[bufp]
                    st_full = ST.n
                    sem = SL if bufp == 0 else SH
                    ldt = sem.n + 16

                    def syl(s, wsrc=wsrc, tix=tix, prev_q=prev_q, f0=f0, sem=sem):
                        s.wait_ge(SV.h, prev_q)
                        s.dma_start(fstream[:, f0:f0 + 2048],
                                    wsrc[:, 2048 * tix:2048 * (tix + 1)]).then_inc(sem.h, 16)
                    sem.bump(16)
                    on("sync" if bufp == 0 else "gpsimd", syl)
                    q_t = SV.bump()

                    def vel(v, ldt=ldt, wi=wi, sw_t=sw_t, f0=f0, w0=w0, stw=st_full, sem=sem):
                        v.wait_ge(sem.h, ldt)
                        v.wait_ge(SA.h, sw_t)
                        v.wait_ge(ST.h, stw)
                        v.tensor_scalar(fstream[:, f0:f0 + 2048], fstream[:, f0:f0 + 2048],
                                        scalar1=wsc[:, 2 * wi:2 * wi + 1], scalar2=RC,
                                        op0=ALU.mult, op1=ALU.add)
                        v.tensor_scalar(fstream[:, f0:f0 + 2048], fstream[:, f0:f0 + 2048],
                                        scalar1=RC, scalar2=None, op0=ALU.subtract)
                        v.tensor_scalar(warena[:, w0:w0 + 2048], fstream[:, f0:f0 + 2048],
                                        scalar1=1.0, scalar2=-1.0, op0=ALU.min, op1=ALU.max).then_inc(SV.h, 1)
                    on("vector", vel)

                    if dst is not None:
                        def scl(sc, dst=dst, tix=tix, q_t=q_t, w0=w0):
                            sc.wait_ge(SV.h, q_t)
                            sc.dma_start(dst[:, 2048 * tix:2048 * (tix + 1)],
                                         warena[:, w0:w0 + 2048]).then_inc(ST.h, 16)
                        ST.bump(16)
                        on("scalar", scl)
                        pb_state[bufp] = (ST.n, q_t)
                    else:
                        pb_state[bufp] = (prev_st, q_t)
                    if wi == 1 and tix == 15:
                        D["wq_v"] = q_t
            D["wq_st"] = ST.n
            pb_v = SV.n

            D["pb_v"] = SV.n

        prep_phase()

        def p0_f1_phase():
            WG0, WU0 = 0, 32768
            AI = [65536, 69632]
            SUPER = 4
            p0_mark = {}
            p0_stnow = {}
            p0v = {}
            p0_aq = {}
            ai_free = {0: 0, 1: 0}
            ev_par = {0: (0, 0), 1: (0, 0)}
            st = {"hq_free": 0}
            lded = set()

            def emit_p0_load(j):
                if j in lded or j >= NT:
                    return
                lded.add(j)
                buf = j % 2

                def gyl(g, j=j, buf=buf):
                    pf = p0_mark.get(j - 2, (0, 0))
                    g.wait_ge(SV.h, pf[0])
                    g.wait_ge(SA.h, pf[1])
                    g.dma_start(xbuf[buf][:, :], xin[128 * j:128 * (j + 1), :]).then_inc(SX.h, 16)
                SX.bump(16)
                on("gpsimd", gyl)

            def emit_p0_tile(j):
                buf = j % 2
                sb0 = 11 * (j % 2)
                emit_p0_load(j)
                sum_t = SV.bump()
                ld_full = 16 * (j + 1)
                prev_aq = p0_aq.get(j - 1, 0)
                on("vector", lambda v, ld_full=ld_full, buf=buf, sb0=sb0, prev_aq=prev_aq: (
                    v.wait_ge(SX.h, ld_full),
                    v.wait_ge(SA.h, prev_aq),
                    v.reduce_sum(pstats[:, sb0 + 0:sb0 + 1], xbuf[buf][:, :], axis=AX.X).then_inc(SV.h, 1)))
                m_t = SV.bump()

                def ve_b(v, sum_t=sum_t, buf=buf, sb0=sb0):
                    v.wait_ge(SV.h, sum_t)
                    v.tensor_scalar(pstats[:, sb0 + 2:sb0 + 3], pstats[:, sb0 + 0:sb0 + 1],
                                    scalar1=1.0 / 1024, scalar2=None, op0=ALU.mult)
                    v.tensor_copy(padt[:, 128:192], padt[:, 192:256])
                    v.tensor_scalar(pstats[:, sb0 + 3:sb0 + 4], pstats[:, sb0 + 2:sb0 + 3],
                                    scalar1=pstats[:, sb0 + 2:sb0 + 3], scalar2=-1.0,
                                    op0=ALU.mult, op1=ALU.mult)
                    v.tensor_scalar(evac1[:, :], xbuf[buf][:, :], scalar1=pstats[:, sb0 + 2:sb0 + 3],
                                    scalar2=None, op0=ALU.subtract)
                    v.reduce_max(pstats[:, sb0 + 7:sb0 + 8], evac1[:, :], axis=AX.X,
                                 apply_absolute_value=True)
                    v.tensor_copy(padt[:, 128:192], padt[:, 192:256]).then_inc(SV.h, 1)
                on("vector", ve_b)
                a_st = SA.bump()

                def ac_a(a, j=j, buf=buf, m_t=m_t, sb0=sb0, prev_done=D.get("p0_done_prev", 0)):
                    a.wait_ge(SV.h, max(m_t, prev_done))
                    a.activation(evac1[:, :], xbuf[buf][:, :], AF.Square, accum_out=pstats[:, sb0 + 1:sb0 + 2])
                    a.activation(padt[:, 0:32], padt[:, 64:96], AF.Identity)
                    a.activation(pstats[:, sb0 + 4:sb0 + 5], pstats[:, sb0 + 1:sb0 + 2], AF.Identity,
                                 bias=pstats[:, sb0 + 3:sb0 + 4], scale=1.0 / 1024)
                    a.activation(padt[:, 0:32], padt[:, 64:96], AF.Identity)
                    a.activation(pstats[:, sb0 + 5:sb0 + 6], pstats[:, sb0 + 4:sb0 + 5], AF.Ln, bias=consts[:, 0:1])
                    a.activation(padt[:, 0:32], padt[:, 64:96], AF.Identity)
                    a.activation(pstats[:, sb0 + 6:sb0 + 7], pstats[:, sb0 + 5:sb0 + 6], AF.Exp, scale=-0.5)
                    a.activation(pstats[:, sb0 + 8:sb0 + 9], pstats[:, sb0 + 7:sb0 + 8], AF.Ln)
                    a.activation(padt[:, 0:32], padt[:, 64:96], AF.Identity)
                    a.activation(pstats[:, sb0 + 9:sb0 + 10], pstats[:, sb0 + 5:sb0 + 6], AF.Identity,
                                 scale=-0.5, bias=pstats[:, sb0 + 8:sb0 + 9])
                    a.activation(padt[:, 0:32], padt[:, 64:96], AF.Identity)
                    a.activation(pstats[:, sb0 + 9:sb0 + 10], pstats[:, sb0 + 9:sb0 + 10], AF.Relu,
                                 bias=consts[:, 2:3])
                    a.activation(padt[:, 0:32], padt[:, 64:96], AF.Identity)
                    a.activation(pstats[:, sb0 + 10:sb0 + 11], pstats[:, sb0 + 9:sb0 + 10], AF.Exp,
                                 scale=-1.0, bias=consts[:, 3:4])
                    a.activation(rs1_all[:, j:j + 1], pstats[:, sb0 + 9:sb0 + 10], AF.Exp,
                                 bias=consts[:, 5:6])
                    a.activation(padt[:, 0:64], padt[:, 64:128], AF.Identity).then_inc(SA.h, 1)
                on("scalar", ac_a)
                xn_t = SV.n + 2
                SV.bump(2)

                def ve_c(v, buf=buf, a_st=a_st, sb0=sb0):
                    v.wait_ge(SA.h, a_st)
                    v.tensor_scalar(evac1[:, :], xbuf[buf][:, :], scalar1=pstats[:, sb0 + 2:sb0 + 3],
                                    scalar2=pstats[:, sb0 + 6:sb0 + 7], op0=ALU.subtract,
                                    op1=ALU.mult).then_inc(SV.h, 2)
                on("vector", ve_c)
                p0_mark[j] = (xn_t, a_st)
                a_q = a_st
                p0_aq[j] = a_q
                if DEBUG and j in (0, 1):
                    def sc_dst(sc, j=j, sb0=sb0, a_q_=SA.n):
                        sc.wait_ge(SA.h, a_q_)
                        sc.dma_start(dbg["d_st%d" % j], pstats[:, sb0:sb0 + 11]).then_inc(ST.h, 16)
                    ST.bump(16)
                    on("scalar", sc_dst)
                done_t = SV.n + 2
                SV.bump(2)

                ps0 = 2048 * (j % 2)

                def ve_e(v, a_q=a_q, j=j, sb0=sb0, ps0=ps0, prev_st=p0_stnow.get(j - 2, 0)):
                    v.wait_ge(SA.h, a_q)
                    v.wait_ge(ST.h, prev_st)
                    v.tensor_copy(p0stage[:, ps0:ps0 + 1024], evac1[:, :])
                    v.tensor_scalar(evac1[:, :], evac1[:, :], scalar1=pstats[:, sb0 + 10:sb0 + 11],
                                    scalar2=RC, op0=ALU.mult, op1=ALU.add)
                    v.tensor_scalar(p0stage[:, ps0 + 1024:ps0 + 2048], evac1[:, :], scalar1=RC,
                                    scalar2=None, op0=ALU.subtract).then_inc(SV.h, 2)
                on("vector", ve_e)
                D["p0_done_prev"] = done_t

                def scl(sc, j=j, done_t=done_t, ps0=ps0):
                    sc.wait_ge(SV.h, done_t)
                    sc.dma_start(xn_sc[128 * j:128 * (j + 1), :], p0stage[:, ps0:ps0 + 1024]).then_inc(ST.h, 16)
                    sc.dma_start(ai_sc[128 * j:128 * (j + 1), :], p0stage[:, ps0 + 1024:ps0 + 2048]).then_inc(ST.h, 16)
                ST.bump(32)
                st_now = ST.n
                p0_stnow[j] = st_now
                on("scalar", scl)
                if j % 4 == 3:
                    mark = SV.bump()

                    def ve_mark(v, st_now=st_now):
                        v.wait_ge(ST.h, st_now)
                        v.tensor_copy(padt[:, 128:192], padt[:, 192:256]).then_inc(SV.h, 1)
                    on("vector", ve_mark)
                    p0v[j] = mark

            def emit_f1_super(sb):
                abuf = AI[sb % 2]
                rsg_sb = SV.bump()

                def ve_rs4(v, sb=sb, aq=p0_aq[4 * sb + 3]):
                    v.wait_ge(SA.h, aq)
                    v.tensor_scalar(rs1g_all[:, 4 * sb:4 * sb + 4], rs1_all[:, 4 * sb:4 * sb + 4],
                                    scalar1=wsc[:, 1:2], scalar2=None, op0=ALU.mult)
                    v.tensor_scalar(cc_all[:, 4 * sb:4 * sb + 4], rs1_all[:, 4 * sb:4 * sb + 4],
                                    scalar1=wsc[:, 3:4], scalar2=None, op0=ALU.mult)
                    v.tensor_copy(padt[:, 128:192], padt[:, 192:256]).then_inc(SV.h, 1)
                on("vector", ve_rs4)

                def syl(s, sb=sb, abuf=abuf, pf=ai_free[sb % 2], pv=p0v[4 * sb + 3]):
                    s.wait_ge(SV.h, pv)
                    s.wait_ge(SP.h, pf)
                    for c in range(8):
                        s.dma_start(warena[:, abuf + 512 * c:abuf + 512 * (c + 1)],
                                    bass.AP(ai_sc.tensor, (512 * sb) * E + 128 * c,
                                            [[E, 512], [1, 128]]), transpose=True).then_inc(SL.h, 16)
                SL.bump(128)
                ai_t = SL.n
                on("sync", syl)
                for jj in range(SUPER):
                    j = sb * SUPER + jj
                    for grp in range(8):
                        par = grp % 2
                        evd = ev_par[par]
                        g_t = SP.bump()
                        u_t = SP.bump()

                        def te(t, jj=jj, grp=grp, par=par, abuf=abuf, evd=evd, ai_t=ai_t, wq_v=D["wq_v"]):
                            t.wait_ge(SL.h, ai_t)
                            t.wait_ge(SV.h, max(wq_v, evd[1]))
                            t.wait_ge(SA.h, evd[0])
                            n0 = 512 * grp
                            for dstp, w0 in ((pw0, WG0), (pw1, WU0)):
                                last = None
                                for c in range(8):
                                    last = t.matmul(dstp[:, 512 * par:512 * (par + 1)],
                                                    wAP(abuf + 512 * c + 128 * jj, dims=[[WARENA, 128], [1, 128]]),
                                                    warena[:, w0 + 4096 * c + n0:w0 + 4096 * c + n0 + 512],
                                                    start=(c == 0), stop=(c == 7))
                                last.then_inc(SP.h, 1)
                        on("tensor", te)
                        a_silu = SA.bump()

                        def ac_s(a, g_t=g_t, j=j, par=par, rsg_sb=rsg_sb, evd=evd):
                            a.wait_ge(SP.h, g_t)
                            a.wait_ge(SV.h, max(rsg_sb, evd[1]))
                            a.activation(gbuf[:, 512 * par:512 * (par + 1)],
                                         pw0[:, 512 * par:512 * (par + 1)], AF.Silu,
                                         scale=rs1g_all[:, j:j + 1]).then_inc(SA.h, 1)
                        on("scalar", ac_s)
                        h_t = SV.bump()

                        def ve_h(v, u_t=u_t, a_silu=a_silu, grp=grp, par=par, hq_free=st["hq_free"]):
                            v.wait_ge(SP.h, u_t)
                            v.wait_ge(SA.h, a_silu)
                            if grp == 0:
                                v.wait_ge(SH.h, hq_free)
                            v.tensor_tensor(fstream[:, 512 * grp:512 * (grp + 1)],
                                            pw1[:, 512 * par:512 * (par + 1)],
                                            gbuf[:, 512 * par:512 * (par + 1)], op=ALU.mult).then_inc(SV.h, 1)
                        on("vector", ve_h)
                        ev_par[par] = (a_silu, h_t)
                    am_t = SV.bump()
                    on("vector", lambda v, am_t=am_t: (
                        v.reduce_max(stats[:, 38:39], fstream[:, :], axis=AX.X,
                                     apply_absolute_value=True).then_inc(SV.h, 1)))
                    mc_t = SV.bump()

                    def ve_mc(v, am_t=am_t, j=j):
                        v.wait_ge(SV.h, am_t)
                        v.tensor_scalar(stats[:, 39:40], stats[:, 38:39], scalar1=cc_all[:, j:j + 1],
                                        scalar2=1e-5, op0=ALU.mult, op1=ALU.max)
                        v.tensor_copy(padt[:, 128:192], padt[:, 192:256]).then_inc(SV.h, 1)
                    on("vector", ve_mc)
                    a_q2 = SA.bump()

                    def ac_q2(a, mc_t=mc_t, j=j):
                        a.wait_ge(SV.h, mc_t)
                        a.activation(stats[:, 40:41], stats[:, 39:40], AF.Ln)
                        a.activation(padt[:, 0:32], padt[:, 64:96], AF.Identity)
                        a.activation(stats[:, 41:42], stats[:, 40:41], AF.Exp, scale=-1.0)
                        a.activation(padt[:, 0:32], padt[:, 64:96], AF.Identity)
                        a.activation(stats[:, 42:43], stats[:, 41:42], AF.Identity,
                                     scale=cc_all[:, j:j + 1])
                        a.activation(padt[:, 0:64], padt[:, 64:128], AF.Identity).then_inc(SA.h, 1)
                    on("scalar", ac_q2)
                    hq_t = SV.n + 2
                    SV.bump(2)

                    def ve_hq(v, a_q2=a_q2, j=j):
                        v.wait_ge(SA.h, a_q2)
                        v.tensor_scalar(stats[:, 43:44], stats[:, 42:43], scalar1=127.0, scalar2=None, op0=ALU.mult)
                        v.tensor_scalar(sdown_all[:, j:j + 1], stats[:, 39:40], scalar1=wsc[:, 5:6],
                                        scalar2=1.0 / 127, op0=ALU.mult, op1=ALU.mult)
                        v.tensor_copy(padt[:, 128:192], padt[:, 192:256])
                        v.tensor_scalar(fstream[:, :], fstream[:, :], scalar1=stats[:, 43:44],
                                        scalar2=RC, op0=ALU.mult, op1=ALU.add)
                        v.tensor_scalar(fstream[:, :], fstream[:, :], scalar1=RC,
                                        scalar2=None, op0=ALU.subtract).then_inc(SV.h, 2)
                    on("vector", ve_hq)

                    def gpl(g, j=j, hq_t=hq_t):
                        g.wait_ge(SV.h, hq_t)
                        g.dma_start(hq_sc[128 * j:128 * (j + 1), :], fstream[:, :]).then_inc(SH.h, 16)
                    SH.bump(16)
                    st["hq_free"] = SH.n
                    on("gpsimd", gpl)
                ai_free[sb % 2] = SP.n

            for j in range(NT):
                emit_p0_tile(j)
                if j % 4 == 3:
                    emit_f1_super(j // 4)
            D["p0_st"] = ST.n
            D["f1_pe"] = SP.n
            D["hq_sh"] = SH.n

        p0_f1_phase()

        CVB = SX.n
        cv_mark = {}
        CV_SRC = [(nm, hv) for nm in ["wkc", "wvc", "wqc", "woc", "wql", "wkl", "wvl", "wol"]
                  for hv in range(4)]

        def emit_conv_tile(t):
            nm, hv = CV_SRC[t]
            if True:
                if True:
                    f0 = 2048 * (t % 2)

                    def gyl(g, nm=nm, hv=hv, t=t, f0=f0, hq_sh=D["hq_sh"]):
                        if t < 2:
                            g.wait_ge(SH.h, hq_sh)
                        else:
                            g.wait_ge(SV.h, cv_mark[t - 2])
                        g.dma_start(fstream[:, f0:f0 + 2048],
                                    win[nm][:, 2048 * hv:2048 * (hv + 1)]).then_inc(SX.h, 16)
                    SX.bump(16)
                    on("gpsimd", gyl)
                    st_full = ST.n
                    vvt = SV.bump()

                    def vv(v, t=t, f0=f0, st_full=st_full, CVB=CVB):
                        v.wait_ge(SX.h, CVB + 16 * (t + 1))
                        v.wait_ge(ST.h, st_full)
                        v.tensor_copy(p0stage[:, 0:2048], fstream[:, f0:f0 + 2048]).then_inc(SV.h, 1)
                    on("vector", vv)
                    cv_mark[t] = vvt

                    def scl(sc, nm=nm, hv=hv, vvt=vvt):
                        sc.wait_ge(SV.h, vvt)
                        sc.dma_start(wbf_sc[:, WOFF[nm] + 2048 * hv:WOFF[nm] + 2048 * (hv + 1)],
                                     p0stage[:, 0:2048]).then_inc(ST.h, 16)
                    ST.bump(16)
                    on("scalar", scl)
        def emit_conv_done():
            st_all = ST.n
            mk = SV.bump()

            def ve_mark(v, st_all=st_all):
                v.wait_ge(ST.h, st_all)
                v.tensor_copy(padt[:, 128:192], padt[:, 192:256]).then_inc(SV.h, 1)
            on("vector", ve_mark)
            D["wbf_v"] = mk
            D["wbf_st"] = st_all
            D["conv_v"] = SV.n

        def shifts_phase():
            for si, (bsrc, rcol) in enumerate([(bqkv_c, 144), (bqkv_l, 160)]):
                ldsh = SL.n + 32

                def sy_sh(s, bsrc=bsrc, conv_v=D["conv_v"]):
                    s.wait_ge(SV.h, conv_v)
                    s.dma_start(fstream[0:1, 2048:3072], bass.AP(bsrc.tensor, 0, [[1024, 1], [1, 1024]])).then_inc(SL.h, 16)
                    s.dma_start(fstream[0:1, 3072:4096], bass.AP(bsrc.tensor, 1024, [[1024, 1], [1, 1024]])).then_inc(SL.h, 16)
                SL.bump(32)
                on("sync", sy_sh)
                b0 = SV.n
                SV.bump(3)

                def ve_sh(v, ldsh=ldsh, rcol=rcol, b0=b0):
                    v.wait_ge(SL.h, ldsh)
                    v.tensor_tensor(fstream[0:1, 2048:3072], fstream[0:1, 2048:3072],
                                    fstream[0:1, 3072:4096], op=ALU.mult).then_inc(SV.h, 1)
                    v.wait_ge(SV.h, b0 + 1)
                    last = None
                    for h in range(16):
                        last = v.reduce_sum(rowred[0:1, rcol + h:rcol + h + 1],
                                            fstream[0:1, 2048 + 64 * h:2048 + 64 * (h + 1)], axis=AX.X)
                    last.then_inc(SV.h, 1)
                    v.wait_ge(SV.h, b0 + 2)
                    v.tensor_copy(padt[:, 128:192], padt[:, 192:256])
                    v.tensor_scalar(rowred[0:1, rcol:rcol + 16], rowred[0:1, rcol:rcol + 16],
                                    scalar1=-0.125, scalar2=-20.0, op0=ALU.mult, op1=ALU.add)
                    v.tensor_copy(padt[:, 128:192], padt[:, 192:256]).then_inc(SV.h, 1)
                on("vector", ve_sh)
            sh_v = SV.n
            stsh = ST.n + 16

            def sc_sh(sc, sh_v=sh_v):
                sc.wait_ge(SV.h, sh_v)
                sc.dma_start(csum_sc[160:192], rowred[0:1, 144:176]).then_inc(ST.h, 16)
            ST.bump(16)
            on("scalar", sc_sh)
            ldsh2 = SL.n + 32

            def sy_sh2(s, stsh=stsh):
                s.wait_ge(ST.h, stsh)
                s.dma_start(cshift[0:16, 16:17], bass.AP(csum_sc.tensor, 160, [[1, 16], [1, 1]])).then_inc(SL.h, 16)
                s.dma_start(cshift[:, 0:16], bass.AP(csum_sc.tensor, 176, [[0, 128], [1, 16]])).then_inc(SL.h, 16)
            SL.bump(32)
            on("sync", sy_sh2)
            D["sh_ld"] = SL.n


        def f2_phase():
            WD0 = 0
            HF = [32768, 49152]
            SUPER = 4
            ld_w = SL.n + 16

            def syw(s, f1_pe=D["f1_pe"]):
                s.wait_ge(SP.h, f1_pe)
                s.dma_start(warena[:, WD0:WD0 + 32768], wdq_sc[:, :]).then_inc(SL.h, 16)
            SL.bump(16)
            on("sync", syw)
            hf_free = {0: D["f1_pe"], 1: D["f1_pe"]}
            x_free = {0: 0, 1: 0}
            ev_par = {0: 0, 1: 0}
            for sb in range(NT // SUPER):
                hbuf = HF[sb % 2]

                def syl(s, sb=sb, hbuf=hbuf, pf=hf_free[sb % 2], hq_sh=D["hq_sh"]):
                    s.wait_ge(SH.h, hq_sh)
                    s.wait_ge(SP.h, pf)
                    for c in range(32):
                        s.dma_start(warena[:, hbuf + 512 * c:hbuf + 512 * (c + 1)],
                                    bass.AP(hq_sc.tensor, (512 * sb) * INNER + 128 * c,
                                            [[INNER, 512], [1, 128]]), transpose=True).then_inc(SL.h, 16)
                SL.bump(16 * 32)
                hld = SL.n
                on("sync", syl)
                for jj in range(SUPER):
                    j = sb * SUPER + jj
                    par = j % 2
                    pdn = pw0 if par == 0 else px
                    buf = j % 2
                    xld = SL.n + 16

                    def syx(s, j=j, buf=buf, xf=x_free[buf]):
                        s.wait_ge(SV.h, xf)
                        s.dma_start(xbuf[buf][:, :], xin[128 * j:128 * (j + 1), :]).then_inc(SL.h, 16)
                    SL.bump(16)
                    on("sync", syx)
                    dn_t = SP.bump()

                    def te(t, jj=jj, hbuf=hbuf, hld=hld, ld_w=ld_w, evd=ev_par[par], pdn=pdn):
                        t.wait_ge(SL.h, max(hld, ld_w))
                        t.wait_ge(SV.h, evd)
                        last = None
                        for half in range(2):
                            n0 = 512 * half
                            for c in range(32):
                                last = t.matmul(pdn[:, n0:n0 + 512],
                                                wAP(hbuf + 512 * c + 128 * jj, dims=[[WARENA, 128], [1, 128]]),
                                                warena[:, WD0 + 1024 * c + n0:WD0 + 1024 * c + n0 + 512],
                                                start=(c == 0), stop=(c == 31))
                        last.then_inc(SP.h, 1)
                    on("tensor", te)
                    out_t = SV.bump()

                    def ve_o(v, dn_t=dn_t, xld=xld, j=j, buf=buf, pdn=pdn):
                        v.wait_ge(SP.h, dn_t)
                        v.wait_ge(SL.h, xld)
                        v.tensor_scalar(evac1[:, :], pdn[:, :], scalar1=sdown_all[:, j:j + 1],
                                        scalar2=None, op0=ALU.mult)
                        v.tensor_tensor(evac1[:, :], evac1[:, :], xbuf[buf][:, :], op=ALU.add)
                        v.tensor_tensor(evac1[:, :], evac1[:, :], bo_bc[:, :], op=ALU.add).then_inc(SV.h, 1)
                    on("vector", ve_o)
                    ev_par[par] = out_t
                    x_free[buf] = out_t

                    def scl(sc, j=j, out_t=out_t):
                        sc.wait_ge(SV.h, out_t)
                        sc.dma_start(part_sc[128 * j:128 * (j + 1), :], evac1[:, :]).then_inc(ST.h, 16)
                    ST.bump(16)
                    st_now = ST.n
                    on("scalar", scl)
                    on("vector", lambda v, st_now=st_now: v.wait_ge(ST.h, st_now))
                    emit_conv_tile(j)
                hf_free[sb % 2] = SP.n
            D["part_st"] = ST.n
            D["f2_pe"] = SP.n
            emit_conv_done()

        f2_phase()
        shifts_phase()

        def attn_phase():
            def gones(g, wbf_st=D["wbf_st"]):
                g.wait_ge(ST.h, wbf_st)
                g.memset(p0stage[0:1, 0:2176], 1.0).then_inc(SO.h, 1)
            D["ones_rdy"] = SO.bump()
            on("gpsimd", gones)
            A_W1, A_W2, A_W3, A_W4 = 0, 8192, 16384, 24576
            A_XNF = 32768
            A_QF = 41984
            A_KF = 50176
            A_VS = 59392
            A_KCB = 61504
            A_PT = 63552
            A_CTM = 63808
            A_CFM = 64832
            A_QHAT = 65856
            A_SCR = 66048
            A_ONES = 68096
            A_PCLS = 70272

            def pclsAP(col, n):
                return bass.AP(warena, A_PCLS + col, [[WARENA, 16], [1, n]])

            def wslot_load(slot, nm, after_pe=None):
                def syl(s, slot=slot, nm=nm, after_pe=after_pe, wbf_v=D["wbf_v"]):
                    s.wait_ge(SV.h, wbf_v)
                    if after_pe is not None:
                        s.wait_ge(SP.h, after_pe)
                    s.dma_start(warena[:, slot:slot + 8192],
                                wbf_sc[:, WOFF[nm]:WOFF[nm] + 8192]).then_inc(SL.h, 16)
                SL.bump(16)
                on("sync", syl)
                return SL.n

            for seq in range(NSEQ):
                tb = seq * S
                seq_barrier = SP.n
                seq_sv = SV.n
                seq_sa = SA.n
                seq_st = ST.n
                on("tensor", lambda t, sv=seq_sv, sa=seq_sa: (t.wait_ge(SV.h, sv), t.wait_ge(SA.h, sa)))
                ld_vl = wslot_load(A_W1, "wvl", after_pe=seq_barrier)
                ld_vc = wslot_load(A_W2, "wvc", after_pe=seq_barrier)
                vbase_l = seq * 1040 * 2176
                vbase_c = seq * 1040 * 2048

                def sc_pad(sc, vbase_l=vbase_l, vbase_c=vbase_c, ip=D["init_pool"]):
                    sc.wait_ge(SO.h, max(ip, D["ones_rdy"]))
                    for r in range(9):
                        rr = min(128 * r, 1040 - 128)
                        sc.dma_start(bass.AP(vloc_sc.tensor, vbase_l + rr * 2176, [[2176, 128], [1, 16]]),
                                     zrow[:, 0:16]).then_inc(ST.h, 16)
                        sc.dma_start(bass.AP(vloc_sc.tensor, vbase_l + rr * 2176 + 2064, [[2176, 128], [1, 112]]),
                                     zrow[:, 0:112]).then_inc(ST.h, 16)
                    for hh in range(16):
                        sc.dma_start(bass.AP(vloc_sc.tensor, vbase_l + (65 * hh + 64) * 2176, [[2176, 1], [1, 2176]]),
                                     p0stage[0:1, 0:2176]).then_inc(ST.h, 16)
                        sc.dma_start(bass.AP(vcls_sc.tensor, vbase_c + (65 * hh + 64) * 2048, [[2048, 1], [1, 2048]]),
                                     p0stage[0:1, 0:2048]).then_inc(ST.h, 16)
                ST.bump(16 * 50)
                on("scalar", sc_pad)

                for vh in range(2):
                    hs = vh * 1024
                    xnf_prev = SP.n

                    def syx(s, hs=hs, xnf_prev=xnf_prev, p0_st=D["p0_st"], tb=tb):
                        s.wait_ge(ST.h, p0_st)
                        s.wait_ge(SP.h, xnf_prev)
                        for c in range(8):
                            s.dma_start(warena[:, A_XNF + 1152 * c:A_XNF + 1152 * c + 1024],
                                        bass.AP(xn_sc.tensor, (tb + hs) * E + 128 * c,
                                                [[E, 1024], [1, 128]]), transpose=True).then_inc(SL.h, 16)
                    SL.bump(128)
                    xnf_t = SL.n
                    on("sync", syx)
                    ci_par = 0
                    for wslot, bcol, dst_sc, dcols, wld in [(A_W1, 40, vloc_sc, 2176, ld_vl),
                                                           (A_W2, 16, vcls_sc, 2048, ld_vc)]:
                        for c in range(8):
                            par = ci_par % 2
                            ci_par += 1
                            pp = px if par == 0 else pw0
                            prev_ev = SA.n
                            pj = SP.bump()

                            def te(t, wslot=wslot, c=c, xnf_t=xnf_t, wld=wld, prev_ev=prev_ev, pp=pp):
                                t.wait_ge(SL.h, max(xnf_t, wld))
                                t.wait_ge(SA.h, prev_ev)
                                last = None
                                for half in range(2):
                                    n0 = 512 * half
                                    for ci in range(8):
                                        last = t.matmul(pp[:, n0:n0 + 512],
                                                        warena[:, wslot + 1024 * ci + 128 * c:wslot + 1024 * ci + 128 * (c + 1)],
                                                        warena[:, A_XNF + 1152 * ci + n0:A_XNF + 1152 * ci + n0 + 512],
                                                        start=(ci == 0), stop=(ci == 7))
                                last.then_inc(SP.h, 1)
                            on("tensor", te)
                            prev_st2 = ST.n
                            ev_t = SA.bump()

                            def ac_ev(a, pj=pj, bcol=bcol, c=c, bfm_ld=D["bfm_ld"], pp=pp, prev_st2=prev_st2):
                                a.wait_ge(SP.h, pj)
                                a.wait_ge(SL.h, bfm_ld)
                                a.wait_ge(ST.h, prev_st2)
                                a.activation(warena[:, A_SCR:A_SCR + 1024], pp[:, 0:1024], AF.Identity,
                                             bias=bfm[:, bcol + c:bcol + c + 1]).then_inc(SA.h, 1)
                            on("scalar", ac_ev)

                            def sc_st(sc, ev_t=ev_t, c=c, dst_sc=dst_sc, dcols=dcols, hs=hs, seq=seq):
                                sc.wait_ge(SA.h, ev_t)
                                coff = seq * 1040 * dcols + (16 if dcols == 2176 else 0) + hs
                                for g in range(2):
                                    sc.dma_start(bass.AP(dst_sc.tensor, (130 * c + 65 * g) * dcols + coff,
                                                         [[dcols, 64], [1, 1024]]),
                                                 bass.AP(warena, (64 * g) * WARENA + A_SCR,
                                                         [[WARENA, 64], [1, 1024]])).then_inc(ST.h, 16)
                            ST.bump(32)
                            on("scalar", sc_st)
                a0_st = ST.n
                a0_pe = SP.n

                ld_kc = wslot_load(A_W1, "wkc", after_pe=a0_pe)
                ld_ql = wslot_load(A_W2, "wql", after_pe=a0_pe)
                ld_kl = wslot_load(A_W4, "wkl", after_pe=seq_barrier)
                ld_qc = wslot_load(A_W3, "wqc", after_pe=seq_barrier)
                wol_ld = [None]
                qh_done = [0]
                pfree = {0: 0, 1: 0}
                vs_tiles = {}
                ctxev = {0: 0, 1: 0}
                for hf in range(2):
                    hs = hf * 1024
                    xnf_prev = SP.n

                    def syx(s, hf=hf, hs=hs, xnf_prev=xnf_prev, tb=tb):
                        s.wait_ge(SP.h, xnf_prev)
                        lo = max(0, hs - 16)
                        hi = min(S, hs + 1136)
                        rows = hi - lo
                        cofs = lo - hs + 16
                        for c in range(8):
                            s.dma_start(warena[:, A_XNF + 1152 * c + cofs:A_XNF + 1152 * c + cofs + rows],
                                        bass.AP(xn_sc.tensor, (tb + lo) * E + 128 * c,
                                                [[E, rows], [1, 128]]), transpose=True).then_inc(SL.h, 16)
                    SL.bump(128)
                    xnf_t = SL.n
                    on("sync", syx)
                    xpad_t = SO.bump()

                    def gp_xpad(g, hf=hf, xnf_prev=xnf_prev):
                        g.wait_ge(SP.h, xnf_prev)
                        last = None
                        for c in range(8):
                            if hf == 0:
                                last = g.memset(warena[:, A_XNF + 1152 * c:A_XNF + 1152 * c + 16], 0.0)
                            else:
                                last = g.memset(warena[:, A_XNF + 1152 * c + 1040:A_XNF + 1152 * (c + 1)], 0.0)
                        last.then_inc(SO.h, 1)
                    on("gpsimd", gp_xpad)

                    ci_par = 0
                    for dst0, wld, wslot, bbase, width, cofs in [(A_QF, ld_ql, A_W2, 24, 1024, 16),
                                                                (A_KF, ld_kl, A_W4, 32, 1152, 0)]:
                        for c in range(8):
                            par = ci_par % 2
                            ci_par += 1
                            pp = px if par == 0 else pw0
                            p3 = (pw1, 0) if par == 0 else (pw1, 512)
                            prev_ev = SA.n
                            pj = SP.bump()

                            def te(t, c=c, wslot=wslot, width=width, cofs=cofs, xnf_t=xnf_t,
                                   wld=wld, xpad_t=xpad_t, prev_ev=prev_ev, pp=pp, p3=p3):
                                t.wait_ge(SL.h, max(xnf_t, wld))
                                t.wait_ge(SO.h, xpad_t)
                                t.wait_ge(SA.h, prev_ev)
                                spans = [(0, 512, pp, 0), (512, 512, pp, 512)]
                                if width == 1152:
                                    spans.append((1024, 128, p3[0], p3[1]))
                                last = None
                                for n0, nl, dstp, d0 in spans:
                                    for ci in range(8):
                                        last = t.matmul(dstp[:, d0:d0 + nl],
                                                        warena[:, wslot + 1024 * ci + 128 * c:wslot + 1024 * ci + 128 * (c + 1)],
                                                        warena[:, A_XNF + 1152 * ci + cofs + n0:A_XNF + 1152 * ci + cofs + n0 + nl],
                                                        start=(ci == 0), stop=(ci == 7))
                                last.then_inc(SP.h, 1)
                            on("tensor", te)
                            ev_t = SA.bump()

                            def ac_ev(a, pj=pj, c=c, dst0=dst0, width=width, bbase=bbase,
                                      pp=pp, p3=p3, bfm_ld=D["bfm_ld"]):
                                a.wait_ge(SP.h, pj)
                                a.wait_ge(SL.h, bfm_ld)
                                last = a.activation(warena[:, dst0 + width * c:dst0 + width * c + min(width, 1024)],
                                                    pp[:, 0:min(width, 1024)], AF.Identity,
                                                    bias=bfm[:, bbase + c:bbase + c + 1])
                                if width == 1152:
                                    last = a.activation(warena[:, dst0 + width * c + 1024:dst0 + width * (c + 1)],
                                                        p3[0][:, p3[1]:p3[1] + 128], AF.Identity,
                                                        bias=bfm[:, bbase + c:bbase + c + 1])
                                last.then_inc(SA.h, 1)
                            on("scalar", ac_ev)
                    kq_ev = SA.n
                    kq_pe = SP.n

                    v_q = 0
                    if hf == 0:
                        ld_bq = SL.n + 16

                        def sy_bq(s, wbf_v=D["wbf_v"]):
                            s.wait_ge(SV.h, wbf_v)
                            s.dma_start(qrow[0:1, 0:1024],
                                        bass.AP(bbf_sc.tensor, 0, [[1024, 1], [1, 1024]])).then_inc(SL.h, 16)
                        SL.bump(16)
                        on("sync", sy_bq)
                        pj_q = SP.bump()

                        def te_q(t, xnf_t=xnf_t, ld_qc=ld_qc, kq_ev=kq_ev):
                            t.wait_ge(SL.h, max(xnf_t, ld_qc))
                            t.wait_ge(SA.h, kq_ev)
                            last = None
                            for half in range(2):
                                n0 = 512 * half
                                for ci in range(8):
                                    last = t.matmul(px[0:1, n0:n0 + 512],
                                                    wAP(A_XNF + 1152 * ci + 16, dims=[[WARENA, 128], [1, 1]]),
                                                    warena[:, A_W3 + 1024 * ci + n0:A_W3 + 1024 * ci + n0 + 512],
                                                    start=(ci == 0), stop=(ci == 7))
                            last.then_inc(SP.h, 1)
                        on("tensor", te_q)
                        v_q = SV.bump()

                        def ve_q(v, pj_q=pj_q, ld_bq=ld_bq):
                            v.wait_ge(SP.h, pj_q)
                            v.wait_ge(SL.h, ld_bq)
                            v.tensor_tensor(qrow[0:1, 0:1024], px[0:1, 0:1024], qrow[0:1, 0:1024],
                                            op=ALU.add).then_inc(SV.h, 1)
                        on("vector", ve_q)
                        qh0 = SO.bump()
                        on("gpsimd", lambda g, v_q=v_q: (
                            g.memset(warena[:, A_QHAT:A_QHAT + 128], 0.0).then_inc(SO.h, 1)))
                        tq_t = SP.bump()

                        def te_tq(t, v_q=v_q):
                            t.wait_ge(SV.h, v_q)
                            last = None
                            for c in range(8):
                                last = t.transpose(ptp[:, 2 * c:2 * c + 1], qrow[0:1, 128 * c:128 * (c + 1)], ident[0:1, 0:1])
                            last.then_inc(SP.h, 1)
                        on("tensor", te_tq)
                        qh_t = SV.bump()

                        def ve_qh(v, tq_t=tq_t, qh0=qh0):
                            v.wait_ge(SP.h, tq_t)
                            v.wait_ge(SO.h, qh0)
                            last = None
                            for c in range(8):
                                last = v.tensor_copy(wAP(A_QHAT + 16 * c + 2 * c, dims=[[WARENA, 64], [1, 1]]),
                                                     ptp[0:64, 2 * c:2 * c + 1])
                                last = v.tensor_copy(wAP(A_QHAT + 16 * c + 2 * c + 1, pbase=64,
                                                         dims=[[WARENA, 64], [1, 1]]),
                                                     ptp[64:128, 2 * c:2 * c + 1])
                            last.then_inc(SV.h, 1)
                        on("vector", ve_qh)
                        qh_done[0] = qh_t

                    for c in range(8):
                        kcb = A_KCB + 1024 * (c % 2)
                        prev_ev = SA.n
                        kc_pj = SP.bump()

                        def te_kc(t, c=c, xnf_t=xnf_t, ld_kc=ld_kc, xpad_t=xpad_t,
                                  prev_ev=prev_ev, v_q=v_q):
                            t.wait_ge(SL.h, max(xnf_t, ld_kc))
                            t.wait_ge(SO.h, xpad_t)
                            t.wait_ge(SA.h, prev_ev)
                            if c == 0 and v_q:
                                t.wait_ge(SV.h, v_q)
                            last = None
                            for half in range(2):
                                n0 = 512 * half
                                for ci in range(8):
                                    last = t.matmul(px[:, n0:n0 + 512],
                                                    warena[:, A_W1 + 1024 * ci + 128 * c:A_W1 + 1024 * ci + 128 * (c + 1)],
                                                    warena[:, A_XNF + 1152 * ci + 16 + n0:A_XNF + 1152 * ci + 16 + n0 + 512],
                                                    start=(ci == 0), stop=(ci == 7))
                            last.then_inc(SP.h, 1)
                        on("tensor", te_kc)
                        kc_ev = SA.bump()

                        def ac_kc(a, kc_pj=kc_pj, kcb=kcb, c=c, bfm_ld=D["bfm_ld"]):
                            a.wait_ge(SP.h, kc_pj)
                            a.wait_ge(SL.h, bfm_ld)
                            a.activation(warena[:, kcb:kcb + 1024], px[:, 0:1024], AF.Identity,
                                         bias=bfm[:, 8 + c:8 + c + 1]).then_inc(SA.h, 1)
                        on("scalar", ac_kc)
                        sc_t = SP.bump()

                        def te_sc(t, kc_ev=kc_ev, c=c, kcb=kcb, qh=qh_done[0]):
                            t.wait_ge(SA.h, kc_ev)
                            t.wait_ge(SV.h, qh)
                            last = None
                            for half in range(2):
                                last = t.matmul(pw1[0:16, 512 * half:512 * half + 512],
                                                wAP(A_QHAT + 16 * c, dims=[[WARENA, 128], [1, 16]]),
                                                warena[:, kcb + 512 * half:kcb + 512 * (half + 1)],
                                                start=(c == 0), stop=(c == 7))
                            last.then_inc(SP.h, 1)
                        on("tensor", te_sc)
                    cls_sc_pe = SP.n
                    e_t = SA.bump()

                    def ac_ecls(a, cls_sc_pe=cls_sc_pe, hf=hf, shl=D["sh_ld"]):
                        a.wait_ge(SP.h, cls_sc_pe)
                        a.wait_ge(SL.h, shl)
                        a.activation(pclsAP(1024 * hf, 512), pw1[0:16, 0:512],
                                     AF.Exp, scale=0.125, bias=cshift[0:16, 16:17])
                        a.activation(pclsAP(1024 * hf + 512, 512), pw1[0:16, 512:1024],
                                     AF.Exp, scale=0.125, bias=cshift[0:16, 16:17]).then_inc(SA.h, 1)
                    on("scalar", ac_ecls)
                    for kk in range(8):
                        prev_cp = SV.n
                        ttp = SP.bump()

                        def te_tp(t, e_t=e_t, hf=hf, kk=kk, prev_cp=prev_cp):
                            t.wait_ge(SA.h, e_t)
                            t.wait_ge(SV.h, prev_cp)
                            t.transpose(ptp[:, 16:32],
                                        pclsAP(1024 * hf + 128 * kk, 128),
                                        ident[0:16, 0:16]).then_inc(SP.h, 1)
                        on("tensor", te_tp)
                        cp_t = SV.bump()

                        def ve_cp(v, ttp=ttp, hf=hf, kk=kk):
                            v.wait_ge(SP.h, ttp)
                            v.tensor_copy(warena[:, A_PT + 16 * (8 * hf + kk):A_PT + 16 * (8 * hf + kk + 1)],
                                          ptp[:, 16:32]).then_inc(SV.h, 1)
                        on("vector", ve_cp)

                    if hf == 0:
                        wol_ld[0] = wslot_load(A_W3, "wol", after_pe=SP.n)

                    pmask = {}
                    seq_state_scx = {}
                    for jj in range(NQT):
                        qs = hs + 128 * jj
                        gj = qs // 128
                        for which in range(2):
                            gv = gj + which
                            key = f"t{gv}"
                            if key in vs_tiles:
                                continue
                            prev_pe = SP.n

                            def syv(s, gv=gv, prev_pe=prev_pe, a0_st=a0_st, seq=seq):
                                s.wait_ge(ST.h, a0_st)
                                s.wait_ge(SP.h, prev_pe)
                                s.dma_start(warena[:, A_VS + VPAD * (gv % 2):A_VS + VPAD * (gv % 2) + 1040],
                                            bass.AP(vloc_sc.tensor, seq * 1040 * 2176 + 128 * gv,
                                                    [[2176, 1040], [1, 128]]),
                                            transpose=True).then_inc(SL.h, 16)
                            SL.bump(16)
                            vs_tiles.pop(f"t{gv - 2}", None)
                            vs_tiles[key] = SL.n
                            on("sync", syv)
                        vs_t = max(vs_tiles[f"t{gj}"], vs_tiles[f"t{gj + 1}"])
                        mbase = 256 if gj == 0 else (512 if gj == 15 else 0)
                        av_last = {}
                        psc_map = [(pw1, 0), (pw1, 512), (py, 0)]
                        for ph in range(H + 3):
                            if ph < H:
                                h = ph
                                hh, hb = h // 2, 64 * (h % 2)
                                pscT, psc0 = psc_map[h % 3]
                                pts = A_SCR + 256 * (h % 4)
                                prev_exp = seq_state_scx.get(h % 3, e_t)
                                s_t = SP.bump()

                                def te_s(t, h=h, hh=hh, hb=hb, jj=jj, pscT=pscT, psc0=psc0,
                                         prev_exp=prev_exp, kq_ev=kq_ev, vs_t=vs_t):
                                    t.wait_ge(SA.h, kq_ev)
                                    t.wait_ge(SA.h, prev_exp)
                                    t.wait_ge(SL.h, vs_t)
                                    last = None
                                    for ch in range(2):
                                        last = t.matmul(pscT[:, psc0 + 128 * ch:psc0 + 128 * (ch + 1)],
                                                        wAP(A_KF + 1152 * hh + 128 * (jj + ch), pbase=hb,
                                                            dims=[[WARENA, 64], [1, 128]]),
                                                        wAP(A_QF + 1024 * hh + 128 * jj, pbase=hb,
                                                            dims=[[WARENA, 64], [1, 128]]),
                                                        start=True, stop=True)
                                    last.then_inc(SP.h, 1)
                                on("tensor", te_s)
                                ex_t = SA.bump()

                                def ac_e(a, s_t=s_t, pscT=pscT, psc0=psc0, pts=pts, h=h, shl=D["sh_ld"]):
                                    a.wait_ge(SP.h, s_t)
                                    a.wait_ge(SL.h, shl)
                                    a.activation(warena[:, pts:pts + 256], pscT[:, psc0:psc0 + 256],
                                                 AF.Exp, scale=0.125, bias=cshift[:, h:h + 1]).then_inc(SA.h, 1)
                                on("scalar", ac_e)
                                seq_state_scx[h % 3] = ex_t
                                m_t = SV.bump()

                                def ve_m(v, ex_t=ex_t, pts=pts, mbase=mbase, ip=D["init_pool"]):
                                    v.wait_ge(SA.h, ex_t)
                                    v.wait_ge(SO.h, ip)
                                    v.tensor_tensor(warena[:, pts:pts + 256], warena[:, pts:pts + 256],
                                                    masks[:, mbase:mbase + 256], op=ALU.mult).then_inc(SV.h, 1)
                                on("vector", ve_m)
                                pmask[h] = m_t
                            h2 = ph - 3
                            if h2 < 0 or h2 >= H:
                                continue
                            h = h2
                            pts = A_SCR + 256 * (h % 4)
                            pctx0 = 65 * (h % 4) + 512 * ((h // 4) % 2)
                            av_t = SP.bump()

                            def te_av(t, m_t=pmask[h], pts=pts, h=h, gj=gj, pctx0=pctx0,
                                      prev_ev=ctxev[(h // 4) % 2]):
                                t.wait_ge(SV.h, m_t)
                                t.wait_ge(SV.h, prev_ev)
                                last = None
                                for ch in range(2):
                                    vslot = A_VS + VPAD * ((gj + ch) % 2)
                                    last = t.matmul(px[:, pctx0:pctx0 + 65],
                                                    warena[:, pts + 128 * ch:pts + 128 * (ch + 1)],
                                                    warena[:, vslot + 65 * h:vslot + 65 * (h + 1)],
                                                    start=(ch == 0), stop=(ch == 1))
                                last.then_inc(SP.h, 1)
                            on("tensor", te_av)
                            av_last[h] = av_t
                            if h % 4 != 3:
                                continue
                            wv = h // 4
                            avs = av_last[4 * wv + 3]
                            lc_t = SV.bump()

                            def ve_l(v, avs=avs, wv=wv):
                                v.wait_ge(SP.h, avs)
                                last = None
                                for k in range(4):
                                    h = 4 * wv + k
                                    pctx0 = 65 * (h % 4) + 512 * ((h // 4) % 2)
                                    last = v.tensor_copy(lbuf[:, k:k + 1], px[:, pctx0 + 64:pctx0 + 65])
                                v.tensor_copy(padt[:, 128:192], padt[:, 192:256]).then_inc(SV.h, 1)
                            on("vector", ve_l)
                            il_t = SA.bump()

                            def ac_il(a, lc_t=lc_t):
                                a.wait_ge(SV.h, lc_t)
                                a.activation(lbuf[:, 8:12], lbuf[:, 0:4], AF.Ln, scale=2.0 ** -32)
                                a.activation(padt[:, 0:32], padt[:, 64:96], AF.Identity)
                                a.activation(lbuf[:, 8:12], lbuf[:, 8:12], AF.Exp, scale=-1.0)
                                a.activation(padt[:, 0:64], padt[:, 64:128], AF.Identity).then_inc(SA.h, 1)
                            on("scalar", ac_il)
                            ce_t = SV.bump()

                            def ve_ce(v, il_t=il_t, wv=wv):
                                v.wait_ge(SA.h, il_t)
                                last = None
                                for k in range(4):
                                    h = 4 * wv + k
                                    pctx0 = 65 * (h % 4) + 512 * ((h // 4) % 2)
                                    last = v.tensor_scalar(warena[:, A_CTM + 64 * h:A_CTM + 64 * (h + 1)],
                                                           px[:, pctx0:pctx0 + 64],
                                                           scalar1=lbuf[:, 8 + k:9 + k], scalar2=2.0 ** -32,
                                                           op0=ALU.mult, op1=ALU.mult)
                                last.then_inc(SV.h, 1)
                            on("vector", ve_ce)
                            ctxev[wv % 2] = ce_t
                        ctx_done = SV.n
                        for c in range(8):
                            prev_cp = SV.n
                            tt = SP.bump()

                            def te_t(t, c=c, ctx_done=ctx_done, prev_cp=prev_cp):
                                t.wait_ge(SV.h, max(ctx_done, prev_cp))
                                t.transpose(ptp[:, 128 * (c % 2):128 * (c % 2) + 128],
                                            warena[:, A_CTM + 128 * c:A_CTM + 128 * (c + 1)],
                                            ident[:, :]).then_inc(SP.h, 1)
                            on("tensor", te_t)
                            cp_t = SV.bump()

                            def ve_t(v, tt=tt, c=c):
                                v.wait_ge(SP.h, tt)
                                v.tensor_copy(warena[:, A_CFM + 128 * c:A_CFM + 128 * (c + 1)],
                                              ptp[:, 128 * (c % 2):128 * (c % 2) + 128]).then_inc(SV.h, 1)
                            on("vector", ve_t)
                        cfm_done = SV.n
                        buf = jj % 2

                        def sy_p(s, qs=qs, buf=buf, pf=pfree[buf], pst=D["part_st"], tb=tb):
                            s.wait_ge(ST.h, pst)
                            s.wait_ge(SV.h, pf)
                            s.dma_start(xbuf[buf][:, :], part_sc[tb + qs:tb + qs + 128, :]).then_inc(SL.h, 16)
                        SL.bump(16)
                        pld = SL.n
                        on("sync", sy_p)
                        prev_fo = SV.n
                        op_t = SP.bump()

                        def te_op(t, cfm_done=cfm_done, wol=wol_ld[0], prev_fo=prev_fo):
                            t.wait_ge(SV.h, cfm_done)
                            t.wait_ge(SL.h, wol)
                            t.wait_ge(SV.h, prev_fo)
                            last = None
                            for half in range(2):
                                n0 = 512 * half
                                for ci in range(8):
                                    last = t.matmul(pw0[:, n0:n0 + 512],
                                                    warena[:, A_CFM + 128 * ci:A_CFM + 128 * (ci + 1)],
                                                    warena[:, A_W3 + 1024 * ci + n0:A_W3 + 1024 * ci + n0 + 512],
                                                    start=(ci == 0), stop=(ci == 7))
                            last.then_inc(SP.h, 1)
                        on("tensor", te_op)
                        fo_t = SV.bump()

                        def ve_fo(v, op_t=op_t, pld=pld, buf=buf):
                            v.wait_ge(SP.h, op_t)
                            v.wait_ge(SL.h, pld)
                            v.scalar_tensor_tensor(evac1[:, :], pw0[:, :], 1.0, xbuf[buf][:, :],
                                                   op0=ALU.mult, op1=ALU.add).then_inc(SV.h, 1)
                        on("vector", ve_fo)
                        pfree[buf] = fo_t

                        def sc_y(sc, qs=qs, fo_t=fo_t, tb=tb):
                            sc.wait_ge(SV.h, fo_t)
                            sc.dma_start(y[tb + qs:tb + qs + 128, :], evac1[:, :]).then_inc(ST.h, 16)
                        ST.bump(16)
                        st_now = ST.n
                        on("scalar", sc_y)
                        on("vector", lambda v, st_now=st_now: v.wait_ge(ST.h, st_now))
                        if DEBUG and seq == 0 and hf == 0 and jj == 0:
                            def sc_t0(sc, fo_t=fo_t):
                                sc.wait_ge(SV.h, fo_t)
                                sc.dma_start(dbg["d_t0pts"], warena[:, A_SCR:A_SCR + 1024]).then_inc(ST.h, 16)
                                sc.dma_start(dbg["d_t0ctm"], warena[:, A_CTM:A_CTM + 1024]).then_inc(ST.h, 16)
                                sc.dma_start(dbg["d_t0vs"], warena[:, A_VS:A_VS + 2112]).then_inc(ST.h, 16)
                            ST.bump(48)
                            on("scalar", sc_t0)

                ld_woc = wslot_load(A_W1, "woc", after_pe=SP.n)
                prev_sv = SV.n
                prev_sa2 = SA.n
                for g in range(16):
                    prev_pe = SP.n

                    def syv(s, g=g, prev_pe=prev_pe, a0_st=a0_st, seq=seq):
                        s.wait_ge(ST.h, a0_st)
                        s.wait_ge(SP.h, prev_pe)
                        s.dma_start(warena[:, A_VS + VPAD * (g % 2):A_VS + VPAD * (g % 2) + 1040],
                                    bass.AP(vcls_sc.tensor, seq * 1040 * 2048 + 128 * g,
                                            [[2048, 1040], [1, 128]]),
                                    transpose=True).then_inc(SL.h, 16)
                    SL.bump(16)
                    vld = SL.n
                    on("sync", syv)
                    cav = SP.bump()

                    def te_cav(t, g=g, vld=vld, prev_sv=prev_sv, prev_sa2=prev_sa2):
                        t.wait_ge(SL.h, vld)
                        if g == 0:
                            t.wait_ge(SV.h, prev_sv)
                            t.wait_ge(SA.h, prev_sa2)
                        vslot = A_VS + VPAD * (g % 2)
                        last = None
                        for dstp, n0, nl in [(pw0[0:16, 0:455], 0, 455), (pw0[0:16, 512:967], 455, 455),
                                             (py[0:16, 0:130], 910, 130)]:
                            last = t.matmul(dstp, warena[:, A_PT + 16 * g:A_PT + 16 * (g + 1)],
                                            warena[:, vslot + n0:vslot + n0 + nl],
                                            start=(g == 0), stop=(g == 15))
                        last.then_inc(SP.h, 1)
                    on("tensor", te_cav)
                cav_done = SP.n
                lr_t = SV.bump()
                cv_t = SV.bump()

                def ve_cv(v, cav_done=cav_done, lr_t=lr_t):
                    v.wait_ge(SP.h, cav_done)
                    v.reduce_sum(lbuf[0:16, 15:16], pclsAP(0, S), axis=AX.X).then_inc(SV.h, 1)
                    v.tensor_copy(clsrows[0:16, 0:455], pw0[0:16, 0:455])
                    v.tensor_copy(clsrows[0:16, 455:910], pw0[0:16, 512:967])
                    v.tensor_copy(clsrows[0:16, 910:1040], py[0:16, 0:130])
                    v.tensor_copy(padt[:, 128:192], padt[:, 192:256]).then_inc(SV.h, 1)
                on("vector", ve_cv)
                il_t = SA.bump()

                def ac_il2(a, cv_t=cv_t):
                    a.wait_ge(SV.h, cv_t)
                    a.activation(lbuf[0:16, 14:15], lbuf[0:16, 15:16], AF.Ln, scale=2.0 ** -32)
                    a.activation(padt[:, 0:32], padt[:, 64:96], AF.Identity)
                    a.activation(lbuf[0:16, 14:15], lbuf[0:16, 14:15], AF.Exp, scale=-1.0)
                    a.activation(padt[:, 0:64], padt[:, 64:128], AF.Identity).then_inc(SA.h, 1)
                on("scalar", ac_il2)
                sc_v = SV.bump()
                prev_stg = D.get("stg", 0)

                def ve_sc(v, il_t=il_t, prev_stg=prev_stg):
                    v.wait_ge(SA.h, il_t)
                    v.wait_ge(ST.h, prev_stg)
                    v.tensor_scalar(warena[0:16, A_SCR:A_SCR + 1040], clsrows[0:16, 0:1040],
                                    scalar1=lbuf[0:16, 14:15], scalar2=2.0 ** -32,
                                    op0=ALU.mult, op1=ALU.mult).then_inc(SV.h, 1)
                on("vector", ve_sc)
                st_g = ST.n + 16 * 16

                def sc_g(sc, sc_v=sc_v):
                    sc.wait_ge(SV.h, sc_v)
                    for c in range(8):
                        for g in range(2):
                            h = 2 * c + g
                            sc.dma_start(bass.AP(warena, (64 * g) * WARENA + A_CFM + c,
                                                 [[WARENA, 64], [1, 1]]),
                                         bass.AP(warena, h * WARENA + A_SCR + 65 * h,
                                                 [[WARENA, 1], [1, 64]])).then_inc(ST.h, 16)
                ST.bump(16 * 16)
                D["stg"] = ST.n
                on("scalar", sc_g)
                ocp = SP.bump()

                def te_oc(t, st_g=st_g, ld_woc=ld_woc):
                    t.wait_ge(ST.h, st_g)
                    t.wait_ge(SL.h, ld_woc)
                    last = None
                    for half in range(2):
                        n0 = 512 * half
                        for ci in range(8):
                            dstp = py[0:1, 0:512] if half == 0 else pw1[0:1, 0:512]
                            last = t.matmul(dstp,
                                            wAP(A_CFM + ci, dims=[[WARENA, 128], [1, 1]]),
                                            warena[:, A_W1 + 1024 * ci + n0:A_W1 + 1024 * ci + n0 + 512],
                                            start=(ci == 0), stop=(ci == 7))
                    last.then_inc(SP.h, 1)
                on("tensor", te_oc)

                def sy_pr(s, pst=D["part_st"], tb=tb, pf=pfree[0]):
                    s.wait_ge(ST.h, pst)
                    s.wait_ge(SV.h, pf)
                    s.dma_start(xbuf[0][0:1, :], part_sc[tb:tb + 1, :]).then_inc(SL.h, 16)
                SL.bump(16)
                pld2 = SL.n
                on("sync", sy_pr)
                oc_t = SV.bump()
                prev_r0 = D.get("r0_st", 0)

                def ve_oc(v, ocp=ocp, pld2=pld2, prev_r0=prev_r0):
                    v.wait_ge(SP.h, ocp)
                    v.wait_ge(SL.h, pld2)
                    v.wait_ge(ST.h, prev_r0)
                    v.tensor_copy(outcls[0:1, 0:512], py[0:1, 0:512])
                    v.tensor_copy(outcls[0:1, 512:1024], pw1[0:1, 0:512])
                    v.tensor_tensor(outcls[0:1, :], outcls[0:1, :], brow[0:1, :], op=ALU.add)
                    v.tensor_tensor(outcls[0:1, :], outcls[0:1, :], bo_bc[0:1, :], op=ALU.subtract)
                    v.tensor_tensor(outcls[0:1, :], outcls[0:1, :], xbuf[0][0:1, :], op=ALU.add).then_inc(SV.h, 1)
                on("vector", ve_oc)
                ystore = ST.n

                def sc_r0(sc, oc_t=oc_t, ystore=ystore, tb=tb):
                    sc.wait_ge(SV.h, oc_t)
                    sc.wait_ge(ST.h, ystore)
                    sc.dma_start(y[tb:tb + 1, :], outcls[0:1, :]).then_inc(ST.h, 16)
                ST.bump(16)
                D["r0_st"] = ST.n
                on("scalar", sc_r0)
                if DEBUG and seq == 0:
                    def sc_dbgc(sc, oc_t=oc_t, sc_v=sc_v, stg=D["stg"], cv_t=cv_t):
                        sc.wait_ge(SV.h, oc_t)
                        sc.wait_ge(ST.h, stg)
                        sc.dma_start(dbg["d_clsrows"], clsrows[0:16, 0:1040]).then_inc(ST.h, 16)
                        sc.dma_start(dbg["d_lb"], lbuf[0:16, 0:16]).then_inc(ST.h, 16)
                        sc.dma_start(dbg["d_outcls"], outcls[0:1, :]).then_inc(ST.h, 16)
                        sc.dma_start(dbg["d_cfm0"], warena[:, 64832:64832 + 8]).then_inc(ST.h, 16)
                        sc.dma_start(dbg["d_scr0"], warena[0:16, 66048:66048 + 1040]).then_inc(ST.h, 16)
                        sc.dma_start(dbg["d_qhat"], warena[:, 65856:65856 + 128]).then_inc(ST.h, 16)
                    ST.bump(96)
                    D["r0_st"] = ST.n
                    on("scalar", sc_dbgc)

        attn_phase()

        if DEBUG:
            dtgt = SL.n + 256

            def sy_dbg(s, stf=ST.n, dtgt=dtgt):
                s.wait_ge(ST.h, stf)
                s.dma_start(dbg["d_xn"], xn_sc).then_inc(SL.h, 16)
                s.dma_start(dbg["d_ai"], ai_sc).then_inc(SL.h, 16)
                s.dma_start(dbg["d_part"], part_sc).then_inc(SL.h, 16)
                s.dma_start(dbg["d_vloc"], bass.AP(vloc_sc.tensor, 0, [[2176, 1040], [1, 2176]])).then_inc(SL.h, 16)
                s.dma_start(dbg["d_vcls"], bass.AP(vcls_sc.tensor, 0, [[2048, 1040], [1, 2048]])).then_inc(SL.h, 16)
                s.dma_start(dbg["d_hq"], hq_sc).then_inc(SL.h, 16)
                s.dma_start(dbg["d_sb"], warena[:, 50176:50176 + 4096]).then_inc(SL.h, 16)
                s.dma_start(dbg["d_pcls"], bass.AP(warena, 70272, [[WARENA, 16], [1, S]])).then_inc(SL.h, 16)
                s.dma_start(dbg["d_qf"], warena[:, 41984:41984 + 8192]).then_inc(SL.h, 16)
                s.dma_start(dbg["d_msk"], masks[:, :]).then_inc(SL.h, 16)
                s.dma_start(dbg["d_ctm"], warena[:, 63808:63808 + 1024]).then_inc(SL.h, 16)
                s.dma_start(dbg["d_cfm"], warena[:, 64832:64832 + 1024]).then_inc(SL.h, 16)
                s.dma_start(dbg["d_vs"], warena[:, 59392:59392 + 2112]).then_inc(SL.h, 16)
                s.dma_start(dbg["d_pts"], warena[:, 66048:66048 + 512]).then_inc(SL.h, 16)
                s.dma_start(dbg["d_w3"], warena[:, 16384:16384 + 8192]).then_inc(SL.h, 16)
                s.dma_start(dbg["d_wbfol"], wbf_sc[:, 57344:57344 + 8192]).then_inc(SL.h, 16)
                s.wait_ge(SL.h, dtgt)
            SL.bump(224)
            on("sync", sy_dbg)
        st_final = ST.n
        on("sync", lambda s, st_final=st_final: s.wait_ge(ST.h, st_final))

        with nc.Block() as block:
            @block.sync
            def _(sync):
                for f in Q["sync"]:
                    f(sync)

            @block.vector
            def _(vector):
                for f in Q["vector"]:
                    f(vector)

            @block.scalar
            def _(scalar):
                for f in Q["scalar"]:
                    f(scalar)

            @block.tensor
            def _(tensor):
                for f in Q["tensor"]:
                    f(tensor)

            @block.gpsimd
            def _(gpsimd):
                for f in Q["gpsimd"]:
                    f(gpsimd)
    return nc


def _chunked_fm(w):
    o, i = w.shape
    return np.ascontiguousarray(w.reshape(o, i // 128, 128).transpose(2, 1, 0).reshape(128, (i // 128) * o))


def kernel(**inputs):
    x = np.asarray(inputs["L0_attn_hidden_states"], dtype=np.float32)
    iwc = np.asarray(inputs["L0_attn_in_w_cls"], dtype=np.float32)
    iwl = np.asarray(inputs["L0_attn_in_w_loc"], dtype=np.float32)
    wmap = {
        "wqc": _chunked_fm(iwc[0:1024]),
        "wkc": _chunked_fm(iwc[1024:2048]),
        "wvc": _chunked_fm(iwc[2048:3072]),
        "woc": _chunked_fm(np.asarray(inputs["L0_attn_out_w_cls"], dtype=np.float32)),
        "wql": _chunked_fm(iwl[0:1024]),
        "wkl": _chunked_fm(iwl[1024:2048]),
        "wvl": _chunked_fm(iwl[2048:3072]),
        "wol": _chunked_fm(np.asarray(inputs["L0_attn_out_w_loc"], dtype=np.float32)),
        "wg": _chunked_fm(np.asarray(inputs["L0_ffn_w_gate"], dtype=np.float32)),
        "wu": _chunked_fm(np.asarray(inputs["L0_ffn_w_up"], dtype=np.float32)),
        "wdn": _chunked_fm(np.asarray(inputs["L0_ffn_w_down"], dtype=np.float32)),
        "bqkv_c": np.ascontiguousarray(np.asarray(inputs["L0_attn_in_b_cls"], dtype=np.float32)),
        "bqkv_l": np.ascontiguousarray(np.asarray(inputs["L0_attn_in_b_loc"], dtype=np.float32)),
        "boc": np.ascontiguousarray(np.asarray(inputs["L0_attn_out_b_cls"], dtype=np.float32)),
        "bol": np.ascontiguousarray(np.asarray(inputs["L0_attn_out_b_loc"], dtype=np.float32)),
    }
    nc = build_nc()
    in_maps = []
    for c in range(NCORES):
        m = dict(wmap)
        m["xin"] = np.ascontiguousarray(x[NSEQ * c:NSEQ * (c + 1)].reshape(T, E))
        in_maps.append(m)
    res = run_bass_kernel_spmd(nc, in_maps, core_ids=list(range(NCORES)))
    out = np.empty((16, S, E), dtype=np.float32)
    for c in range(NCORES):
        out[NSEQ * c:NSEQ * (c + 1)] = np.asarray(res.results[c]["y"]).reshape(NSEQ, S, E)
    return out
